# version 25
# baseline (speedup 1.0000x reference)
import contextlib
import numpy as np
import ml_dtypes
import concourse.bass as bass
import concourse.mybir as mybir
from concourse.bass_utils import run_bass_kernel_spmd
from concourse.replica_groups import maybe_share_collective_output_space

F32 = mybir.dt.float32
BF16 = mybir.dt.bfloat16
AF = mybir.ActivationFunctionType
ALU = mybir.AluOpType
AX = mybir.AxisListType
BF = ml_dtypes.bfloat16

NCORES = 8
NB = 4
S = 2048
D = 1024
NH = 16
HD = 64
NPAIR = 8
NQB = 4
NKT = 16
NFC = 8
CAP = 100
EPS = 1e-5
NEG = -1.0e30


class _Op:
    __slots__ = ("stream", "fn", "deps", "is_dma", "delta", "sem", "incval",
                 "needed", "idx", "tag", "flushq", "flushval")

    def __init__(self, stream, fn, is_dma, tag):
        self.flushq = None
        self.flushval = None
        self.stream = stream
        self.fn = fn
        self.deps = []
        self.is_dma = is_dma
        self.delta = 16 if is_dma else 1
        self.sem = None
        self.incval = None
        self.needed = False
        self.idx = None
        self.tag = tag


class Buf:
    __slots__ = ("name", "last_w", "readers", "rng", "overlaps")

    def __init__(self, name, rng=None):
        self.name = name
        self.last_w = None
        self.readers = []
        self.rng = rng
        self.overlaps = []


class Sched:
    def __init__(self):
        self.order = []
        self.bufs = []

    def link_overlaps(self):
        for i, a in enumerate(self.bufs):
            if a.rng is None:
                continue
            for c in self.bufs[i + 1:]:
                if c.rng is None:
                    continue
                if a.rng[0] < c.rng[1] and c.rng[0] < a.rng[1]:
                    a.overlaps.append(c)
                    c.overlaps.append(a)

    def op(self, stream, fn, reads=(), writes=(), deps=(), is_dma=False, tag="",
           flushq=None):
        o = _Op(stream, fn, is_dma, tag)
        o.flushq = flushq
        o.idx = len(self.order)
        dd = list(deps)
        for b in reads:
            if b.last_w is not None:
                dd.append(b.last_w)
        for b in writes:
            for bb in [b] + b.overlaps:
                if bb.last_w is not None:
                    dd.append(bb.last_w)
                dd.extend(bb.readers)
        seen = set()
        for d in dd:
            if d is None or id(d) in seen or d is o:
                continue
            seen.add(id(d))
            assert d.idx < o.idx
            o.deps.append(d)
        for b in reads:
            b.readers.append(o)
        for b in writes:
            b.last_w = o
            b.readers = []
            for bb in b.overlaps:
                bb.last_w = None
                bb.readers = []
        self.order.append(o)
        return o

    def finalize(self):
        for o in self.order:
            if o.tag == "allgather":
                o.sem = "s_cc"
            elif o.is_dma:
                o.sem = {"sp": "q_sp", "act": "q_act"}.get(o.stream, "q_gp")
            else:
                o.sem = "s_" + o.stream
        for o in self.order:
            for d in o.deps:
                if d.stream == o.stream and not d.is_dma and not o.is_dma:
                    continue
                d.needed = True
        counts = {}
        for o in self.order:
            if o.flushq is not None:
                o.flushval = counts.get(o.flushq, 0)
            if o.needed or o.is_dma:
                counts[o.sem] = counts.get(o.sem, 0) + o.delta
                o.incval = counts[o.sem]
        return counts



def _consts():
    identf = np.eye(128, dtype=np.float32)
    identbf = np.eye(128, dtype=BF)
    ones1f = np.ones((1, 128), dtype=np.float32)
    ones1bf = np.ones((1, 128), dtype=BF)
    ones128f = np.ones((128, 1), dtype=np.float32)
    ones128bf = np.ones((128, 1), dtype=BF)
    k = np.arange(128)[:, None]
    mtq = np.zeros((128, 1280), dtype=np.float32)
    mqk = np.zeros((128, 1280), dtype=np.float32)
    for j in range(4):
        off, w = j * (j + 1) // 2 * 128, (j + 1) * 128
        q5 = np.arange(w)[None, :]
        mtq[:, off:off + w] = np.where(q5 >= j * 128 + k, 0.0, NEG)
        mqk[:, off:off + w] = np.where(q5 <= j * 128 + k, 0.0, NEG)
    mqk2a = np.concatenate([mqk[0:64], mqk[0:64]], axis=0)
    mqk2b = np.concatenate([mqk[64:128], mqk[64:128]], axis=0)
    return {
        "identf": identf, "identbf": identbf,
        "ones1f": ones1f, "ones1bf": ones1bf,
        "ones128f": ones128f, "ones128bf": ones128bf,
        "maskstq": mtq.astype(BF),
        "masksqk2a": mqk2a.astype(BF), "masksqk2b": mqk2b.astype(BF),
    }


def stage(inputs):
    x0 = np.asarray(inputs["L0_attn_inputs"], dtype=np.float32)
    wqkv = np.asarray(inputs["L0_attn_Wqkv_w"], dtype=np.float32)
    bqkv = np.asarray(inputs["L0_attn_Wqkv_b"], dtype=np.float32)
    wo = np.asarray(inputs["L0_attn_Wo_w"], dtype=np.float32)
    bo = np.asarray(inputs["L0_attn_Wo_b"], dtype=np.float32)
    ghost = np.asarray(inputs["L0_attn_ghost"], dtype=np.float32)
    wg = np.asarray(inputs["L0_ffn_w_gate"], dtype=np.float32)
    we = np.asarray(inputs["L0_ffn_w_expert"], dtype=np.float32)
    be = np.asarray(inputs["L0_ffn_b_expert"], dtype=np.float32)

    shared = dict(_consts())
    shared["wqkvt"] = np.ascontiguousarray(wqkv.T)
    shared["wqkvb_pc"] = np.ascontiguousarray(bqkv.reshape(24, 128).T)
    shared["bv_row"] = np.ascontiguousarray(bqkv[2048:].reshape(1, 1024))
    shared["wot"] = np.ascontiguousarray(wo.T)
    shared["wob_pc"] = np.ascontiguousarray(bo.reshape(8, 128).T)
    shared["wgate3"] = np.ascontiguousarray(wg.reshape(8, 128, 8).transpose(1, 0, 2))
    shared["wexp"] = np.ascontiguousarray(we)
    shared["bexp_row"] = np.ascontiguousarray(be.reshape(1, 1024))
    gh2 = np.empty((128, 8), dtype=np.float32)
    gh2[0:64, :] = ghost[0::2][None, :]
    gh2[64:128, :] = ghost[1::2][None, :]
    shared["ghostb"] = gh2

    in_maps = []
    for c in range(NCORES):
        m = dict(shared)
        xb = x0[c * NB:(c + 1) * NB]
        m["x0t"] = np.ascontiguousarray(xb.transpose(0, 2, 1))
        m["premask"] = (np.arange(8) < c).astype(np.float32).reshape(8, 1)
        in_maps.append(m)
    return in_maps



class Builder:
    def __init__(self, nb=NB, npair=NPAIR, use_cc=True, debug=(), skip=()):
        self.skip = set(skip)
        self.nb = nb
        self.npair = npair
        self.use_cc = use_cc
        self.debug = set(debug)
        self.dbg_outs = {}
        self.nc = bass.Bass("TRN2", target_bir_lowering=False)
        self.sch = Sched()
        self.sb_ranges = {}
        self._sb_off = 16512
        self._mk_tensors()

    def _sb(self, name, shape, dtype, off=None):
        nc = self.nc
        per = int(np.prod(shape[1:])) * mybir.dt.size(dtype)
        if off is None:
            off = self._sb_off
            self._sb_off = (off + per + 31) & ~31
        h = nc.alloc_sbuf_tensor_at(name, shape, dtype, offset=off)
        self.sb_ranges[name] = (off, off + per)
        return h

    def _mk_tensors(self):
        nc = self.nc
        self.d = {}
        for name, shp, dt in [
            ("x0t", [NB, D, S], F32), ("wqkvt", [D, 3 * D], F32),
            ("wqkvb_pc", [128, 24], F32), ("bv_row", [1, D], F32),
            ("wot", [D, D], F32), ("wob_pc", [128, 8], F32),
            ("wgate3", [128, 8, 8], F32), ("wexp", [D, D], F32),
            ("bexp_row", [1, D], F32), ("ghostb", [128, 8], F32),
            ("identf", [128, 128], F32), ("identbf", [128, 128], BF16),
            ("ones1f", [1, 128], F32), ("ones1bf", [1, 128], BF16),
            ("ones128f", [128, 1], F32), ("ones128bf", [128, 1], BF16),
            ("maskstq", [128, 1280], BF16), ("masksqk2a", [128, 1280], BF16),
            ("masksqk2b", [128, 1280], BF16),
            ("premask", [8, 1], F32),
        ]:
            self.d[name] = nc.dram_tensor(name, shp, dt, kind="ExternalInput")
        self.d_out = nc.dram_tensor("x2t", [NB, D, S], F32, kind="ExternalOutput")
        self.d_x1sc = nc.dram_tensor("x1sc", [D, NB * S], BF16)
        if self.use_cc:
            rg = [list(range(NCORES))]
            space = maybe_share_collective_output_space("AllGather", rg)
            self.cc_rg = rg
            self.d_ccin = nc.dram_tensor("ccin", [8, 1], F32)
            self.d_ccout = nc.dram_tensor("ccout", [8, 8], F32, addr_space=space)

        self.s_identf = self._sb("identf", [128, 128], F32)
        self.s_identbf = self._sb("identbf", [128, 128], BF16)
        self.s_ones1f = self._sb("ones1f", [1, 128], F32)
        self.s_ones1bf = self._sb("ones1bf", [1, 128], BF16)
        self.s_ones128f = self._sb("ones128f", [128, 1], F32)
        self.s_ones128bf = self._sb("ones128bf", [128, 1], BF16)
        self.s_mtq = self._sb("maskstq", [128, 1280], BF16)
        self.s_mqk2 = [self._sb("masksqk2a", [128, 1280], BF16),
                       self._sb("masksqk2b", [128, 1280], BF16)]
        self.s_ghostb = self._sb("ghostb", [128, 8], F32)
        self.s_bqkv = self._sb("bqkv", [128, 24], F32)
        self.s_bo = self._sb("bo", [128, 8], F32)
        self.s_wg = self._sb("wg", [128, 8, 8], F32)
        self.s_wgbf = self._sb("wgbf", [128, 8, 8], BF16)
        self.s_bvrowbf = self._sb("bvrowbf", [1, D], BF16)
        self.s_berowbf = self._sb("berowbf", [1, D], BF16)
        self.s_premask = self._sb("premask", [8, 1], F32)
        self.s_agc = self._sb("agc", [8, 8], F32)
        self.s_cb = self._sb("cb", [8, 1], F32)
        self.s_ones33 = self._sb("ones33", [33, 128], F32)
        self.s_ones33bf = self._sb("ones33bf", [33, 128], BF16)
        self.s_lrows = self._sb("lrows", [1, 512], F32)
        self.s_ltmp = self._sb("ltmp", [1, 512], F32)
        self.s_laux = self._sb("laux", [1, 512], F32)
        self.s_gateT = self._sb("gateT", [40, NB * S], BF16)
        base = self._sb_off
        o = base
        def at(name, shape, dtype):
            nonlocal o
            h = self._sb(name, shape, dtype, off=o)
            o += (int(np.prod(shape[1:])) * mybir.dt.size(dtype) + 31) & ~31
            return h
        self.s_x0t = at("x0t", [128, NQB, NFC, 512], BF16)
        self.s_rowtmp = self._sb("rowtmp", [1, D], F32, off=base)
        self.s_wqp = [at(f"wqp{i}", [128, NFC, 3, 128], BF16) for i in range(2)]
        self.s_qd = [at(f"qd{h}", [128, S], BF16) for h in range(2)]
        self.s_kbd = [at(f"kbd{h}", [128, S], BF16) for h in range(2)]
        self.s_qbd2 = [at(f"qbd2{h}", [128, S], BF16) for h in range(2)]
        self.s_kt = at("kt", [128, S], BF16)
        self.s_vp = at("vp", [128, NKT, 130], BF16)
        self.s_vt = at("vt", [128, S], BF16)
        self.s_et = [[at(f"et{h}{p}", [128, 512], BF16) for p in range(2)] for h in range(2)]
        self.s_a2n = [at(f"a2n{p}", [128, S], BF16) for p in range(NPAIR)]
        self.s_wot = at("wot", [128, NPAIR, D], BF16)
        self.s_xres = at("xres", [128, NFC, 512], F32)
        self.s_xsq = at("xsq", [128, 2, 512], F32)
        self.s_x1qb = at("x1qb", [128, NFC, 512], BF16)
        self.s_lall = at("lall", [128, 512], F32)
        self.s_oh = at("oh", [128, 512], F32)
        self.s_sig = at("sig", [33, S], BF16)
        self.s_mpair = at("mpair", [128, 32], F32)
        self.s_mprt = at("mprt", [128, 48], F32)
        self.s_emg = at("emg", [128, 64], F32)
        self.s_emgbf = at("emgbf", [128, 32], BF16)
        self.s_r = at("r", [128, 512], F32)
        class1_end = o
        o = base
        self.s_scan = at("scan", [8, NB * S], F32)
        self.s_sel = at("sel", [8, NB * S], BF16)
        self.s_so = at("so", [40, NB * S], BF16)
        self.s_num = at("num", [33, NB * S // 2], F32)
        self.s_zrow = at("zrow", [33, NB * S // 2], F32)
        post_end = o
        self.s_mgbf = at("mgbf", [1, NB * S], BF16)
        cls4_o = o
        self.s_mgt = self._sb("mgt", [33, NB * S // 2], F32,
                              off=base)
        o = cls4_o
        cls4_end = o
        o = base
        self.s_wexp = at("wexp", [128, NFC, D], BF16)
        self.s_x1tb = [at(f"x1tb{i}", [128, NFC, 512], BF16) for i in range(2)]
        self.s_x1mg = at("x1mg", [128, 2, NFC, 512], BF16)
        self.s_xres2 = at("xres2", [128, NFC, 512], F32)
        self.s_x2o = at("x2o", [128, NFC, 512], F32)
        self.s_xsqb = at("xsqb", [128, 2, 512], F32)
        cls5_end = o
        assert cls5_end <= post_end, (cls5_end, post_end)
        self.sbuf_peak = max(class1_end, cls4_end, cls5_end)
        self.sbuf_regions = dict(base=base, class1=class1_end, class3=post_end,
                                 cls4=cls4_end, cls5=cls5_end)
        assert self.sbuf_peak <= 229376, (self.sbuf_peak, self.sbuf_regions)

        self.p_cp = nc.alloc_psum_tensor("cp", [128, 1024], F32)
        self.p_sl = [nc.alloc_psum_tensor(f"sl{i}", [128, 512], F32) for i in range(4)]
        self.p_av = [nc.alloc_psum_tensor(f"av{h}", [128, 512], F32) for h in range(2)]

        rng = self.sb_ranges
        x0lo = rng["x0t"][0]
        x0chunk = NFC * 512 * 2
        name2rng = {
            "wqp0": rng["wqp0"], "wqp1": rng["wqp1"],
            "qd0": rng["qd0"], "qd1": rng["qd1"],
            "kbd0": rng["kbd0"], "kbd1": rng["kbd1"],
            "qbd20": rng["qbd20"], "qbd21": rng["qbd21"],
            "kt": rng["kt"], "vp": rng["vp"], "vt": rng["vt"],
            "et00": rng["et00"], "et01": rng["et01"],
            "et10": rng["et10"], "et11": rng["et11"],
            "wot": rng["wot"], "xres": rng["xres"],
            "xsq": rng["xsq"], "x1qb": rng["x1qb"], "lall": rng["lall"],
            "oh": rng["oh"], "sig": rng["sig"], "mpair": rng["mpair"], "mprt": rng["mprt"],
            "emg": rng["emg"], "emgbf": rng["emgbf"], "r": rng["r"],
            "gateT": rng["gateT"], "scan": rng["scan"],
            "sel": rng["sel"], "so": rng["so"], "num": rng["num"],
            "zrow": rng["zrow"], "mgbf": rng["mgbf"], "mgt": rng["mgt"],
            "wexp": rng["wexp"], "x1tb0": rng["x1tb0"], "x1tb1": rng["x1tb1"],
            "x1mg": rng["x1mg"], "xres2": rng["xres2"], "x2o": rng["x2o"],
            "xsqb": rng["xsqb"], "lrows": rng["lrows"], "ltmp": rng["ltmp"],
            "laux": rng["laux"], "rowtmp": rng["rowtmp"],
            "consts": (16512, rng["gateT"][0]),
        }
        for qb in range(NQB):
            name2rng[f"x0t{qb}"] = (x0lo + qb * x0chunk, x0lo + (qb + 1) * x0chunk)

        self.b = {}
        for n in ["cpb0", "cpb1", "sl0", "sl1", "sl2", "sl3",
                  "av0", "av1", "x1sc", "ccin", "ccout", "agc", "cb"]:
            self.b[n] = Buf(n)
            self.sch.bufs.append(self.b[n])
        for n, r in name2rng.items():
            self.b[n] = Buf(n, r)
            self.sch.bufs.append(self.b[n])
        self.b_a2n = []
        for p in range(NPAIR):
            bb = Buf(f"a2n{p}", rng[f"a2n{p}"])
            self.b_a2n.append(bb)
            self.sch.bufs.append(bb)
        self.sch.link_overlaps()

    def dbg(self, name, src_ap, shape, dtype, reads):
        if name not in self.debug:
            return
        t = self.nc.dram_tensor("dbg_" + name, shape, dtype, kind="ExternalOutput")
        self.dbg_outs[name] = name
        self.sch.op("sp", lambda e, t=t, a=src_ap: e.dma_start(out=t.ap(), in_=a),
                    reads=reads, is_dma=True, tag="dbg_" + name)

    def emit_all(self):
        sch, b = self.sch, self.b
        nc = self.nc

        def ld(sb, dname, cast=False, tag=""):
            src = self.d[dname].ap()
            stream = "gp" if cast else "sp"
            return sch.op(stream, lambda e, o=sb, i=src: e.dma_start(out=o[:], in_=i),
                          writes=[b["consts"]], is_dma=True, tag="ld_" + dname)

        for nm, sb in [("identf", self.s_identf), ("identbf", self.s_identbf),
                       ("ones1f", self.s_ones1f), ("ones1bf", self.s_ones1bf),
                       ("ones128f", self.s_ones128f), ("ones128bf", self.s_ones128bf),
                       ("maskstq", self.s_mtq), ("masksqk2a", self.s_mqk2[0]),
                       ("masksqk2b", self.s_mqk2[1]),
                       ("ghostb", self.s_ghostb), ("wqkvb_pc", self.s_bqkv),
                       ("wob_pc", self.s_bo), ("wgate3", self.s_wg),
                       ("premask", self.s_premask)]:
            ld(sb, nm)
        sch.op("sp", lambda e: e.dma_start(out=self.s_rowtmp[:],
                                           in_=self.d["bv_row"].ap()),
               writes=[b["rowtmp"]], is_dma=True, tag="ld_bv")
        sch.op("dve", lambda e: e.tensor_copy(self.s_bvrowbf[:], self.s_rowtmp[:]),
               reads=[b["rowtmp"]], writes=[b["consts"]], tag="cast_bv")
        sch.op("sp", lambda e: e.dma_start(out=self.s_rowtmp[:],
                                           in_=self.d["bexp_row"].ap()),
               reads=[b["consts"]], writes=[b["rowtmp"]], is_dma=True, tag="ld_be")
        sch.op("dve", lambda e: e.tensor_copy(self.s_berowbf[:], self.s_rowtmp[:]),
               reads=[b["rowtmp"]], writes=[b["consts"]], tag="cast_be")
        sch.op("dve", lambda e: e.memset(self.s_ones33[0:1, :], 1.0),
               writes=[b["consts"]], tag="ones33a")
        sch.op("dve", lambda e: e.memset(self.s_ones33[32:33, :], 1.0),
               writes=[b["consts"]], tag="ones33b")
        sch.op("dve", lambda e: e.memset(self.s_ones33bf[0:1, :], 1.0),
               writes=[b["consts"]], tag="ones33c")
        sch.op("dve", lambda e: e.memset(self.s_ones33bf[32:33, :], 1.0),
               writes=[b["consts"]], tag="ones33d")
        sch.op("dve", lambda e: e.tensor_copy(self.s_wgbf[:], self.s_wg[:]),
               reads=[b["consts"]], writes=[b["consts"]], tag="cast_wg")
        for nm, sb in [("kbd0", self.s_kbd[0]), ("kbd1", self.s_kbd[1]),
                       ("qbd20", self.s_qbd2[0]), ("qbd21", self.s_qbd2[1])]:
            sch.op("dve", lambda e, sb=sb: e.memset(sb[:], 0.0),
                   writes=[b[nm]], tag=f"z_{nm}")

        for bi in range(self.nb):
            self.emit_batch(bi)
        self.emit_routing()
        self.emit_phase_b()

        last_sp = [o for o in sch.order if o.is_dma and o.stream == "sp"][-1]
        last_act = [o for o in sch.order if o.is_dma and o.stream == "act"][-1]
        sch.op("sp", None, deps=[last_sp, last_act], tag="final_wait")
        sch.op("dve", lambda e: e.engine_nop(), deps=[last_act, last_sp],
               tag="final_wait2")

    def _x0_chunk_load(self, bi, qb):
        sch, b = self.sch, self.b
        src = self.d["x0t"].ap()[bi].rearrange(
            "(c p) t -> p c t", p=128)[:, :, qb * 512:(qb + 1) * 512]
        sch.op("gp", lambda e, i=src, qb=qb: e.dma_start(
            out=self.s_x0t[:, qb, :, :], in_=i),
            writes=[b[f"x0t{qb}"]], is_dma=True, tag=f"ld_x0t_b{bi}c{qb}")

    def _wq_load(self, bi, pair):
        sch, b = self.sch, self.b
        wq = self.s_wqp[pair % 2]
        bwq = b[f"wqp{pair % 2}"]
        for blk in range(3):
            off = blk * 1024 + pair * 128
            srcw = bass.AP(self.d["wqkvt"], off,
                           [[3072, 128], [128 * 3072, 8], [1, 128]])
            dst = wq[:, :, blk, :]
            sch.op("gp", lambda e, o=dst, i=srcw: e.dma_start(out=o, in_=i),
                   writes=[bwq], is_dma=True, tag=f"ld_wq_b{bi}p{pair}k{blk}")

    def emit_batch(self, bi):
        sch, b = self.sch, self.b
        if bi == 0:
            for qb in range(NQB):
                self._x0_chunk_load(bi, qb)
            wsrc = self.d["wot"].ap().rearrange("(c p) f -> p c f", p=128)
            sch.op("gp", lambda e, i=wsrc: e.dma_start(out=self.s_wot[:], in_=i),
                   writes=[b["wot"]], is_dma=True, tag=f"ld_wot_b{bi}")

        for pair in range(self.npair):
            self.emit_pair(bi, pair, skip_wload=(bi > 0 and pair < 2))
        if bi + 1 < self.nb:
            self._wq_load(bi + 1, 0)
            self._wq_load(bi + 1, 1)
        self.emit_wo_ln1(bi)
        self.emit_gate_batch(bi)

    def emit_pair(self, bi, pair, skip_wload=False):
        sch, b = self.sch, self.b
        wq = self.s_wqp[pair % 2]
        bwq = b[f"wqp{pair % 2}"]
        if not skip_wload:
            self._wq_load(bi, pair)

        for blk in range(2):
            for qb in range(NQB):
                ps = self.p_sl[qb][:, :]
                def mmgrp(e, ps=ps, wq=wq, blk=blk, qb=qb):
                    r = None
                    for c in range(NFC):
                        r = e.matmul(ps, wq[:, c, blk, :],
                                     self.s_x0t[:, qb, c, :],
                                     start=(c == 0), stop=(c == NFC - 1))
                    return r
                sch.op("pe", mmgrp, reads=[bwq, b[f"x0t{qb}"]], writes=[b[f"sl{qb}"]],
                       tag=f"qkproj_b{bi}p{pair}x{blk}q{qb}")
                if blk == 0:
                    for hl in range(2):
                        r0 = 64 * hl
                        dst = self.s_qd[hl]
                        bias = self.s_bqkv[r0:r0 + 64, pair:pair + 1]
                        psh = self.p_sl[qb].ap()[r0:r0 + 64, :]
                        sch.op("act", lambda e, psh=psh, dst=dst, qb=qb, bias=bias,
                               r0=r0: e.activation(
                                   out=dst[r0:r0 + 64, qb * 512:(qb + 1) * 512],
                                   in_=psh, func=AF.Identity, bias=bias, scale=1.0),
                               reads=[b[f"sl{qb}"], b["consts"]],
                               writes=[b[f"qd{hl}"]],
                               tag=f"qevac_b{bi}p{pair}h{hl}q{qb}")
                else:
                    bias = self.s_bqkv[:, 8 + pair:9 + pair]
                    sch.op("act", lambda e, ps=ps, qb=qb, bias=bias:
                           e.activation(out=self.s_kt[:, qb * 512:(qb + 1) * 512],
                                        in_=ps, func=AF.Identity, bias=bias,
                                        scale=1.0),
                           reads=[b[f"sl{qb}"], b["consts"]], writes=[b["kt"]],
                           tag=f"kevac_b{bi}p{pair}q{qb}")
            def half_view(sb, p0, h):
                return sb.ap()[p0:p0 + 64, :].rearrange(
                    "p (k h c) -> p k h c", h=2, c=64)[:, :, h:h + 1, :]
            if blk == 0:
                def qdma(dst, src, srcbuf, dstbuf, tag):
                    sch.op("sp", lambda e, o=dst, i=src: e.dma_start(out=o, in_=i),
                           reads=[b[srcbuf]], writes=[b[dstbuf]],
                           is_dma=True, tag=tag)
                t = f"_b{bi}p{pair}"
                qdma(self.s_qd[0].ap()[64:128, :], self.s_qd[0].ap()[0:64, :],
                     "qd0", "qd0", "bq_qd0" + t)
                qdma(self.s_qd[1].ap()[0:64, :], self.s_qd[1].ap()[64:128, :],
                     "qd1", "qd1", "bq_qd1" + t)
                qdma(half_view(self.s_qbd2[0], 0, 0), half_view(self.s_qd[0], 0, 0),
                     "qd0", "qbd20", "bq_q20a" + t)
                qdma(half_view(self.s_qbd2[0], 64, 1), half_view(self.s_qd[1], 64, 0),
                     "qd1", "qbd20", "bq_q20b" + t)
                qdma(half_view(self.s_qbd2[1], 0, 0), half_view(self.s_qd[0], 0, 1),
                     "qd0", "qbd21", "bq_q21a" + t)
                qdma(half_view(self.s_qbd2[1], 64, 1), half_view(self.s_qd[1], 64, 1),
                     "qd1", "qbd21", "bq_q21b" + t)
                sch.op("sp", lambda e: e.nop(),
                       reads=[b["qd0"], b["qd1"]],
                       writes=[b["qd0"], b["qd1"], b["qbd20"], b["qbd21"]],
                       flushq="q_sp", tag=f"qflush_b{bi}p{pair}")
            else:
                def kdma(dst, src, dstbuf, tag):
                    sch.op("gp", lambda e, o=dst, i=src: e.dma_start(out=o, in_=i),
                           reads=[b["kt"]], writes=[b[dstbuf]],
                           is_dma=True, tag=tag)
                t = f"_b{bi}p{pair}"
                kdma(half_view(self.s_kbd[0], 0, 0), half_view(self.s_kt, 0, 0),
                     "kbd0", "bk_k0a" + t)
                kdma(half_view(self.s_kbd[0], 64, 1), half_view(self.s_kt, 0, 1),
                     "kbd0", "bk_k0b" + t)
                kdma(half_view(self.s_kbd[1], 0, 0), half_view(self.s_kt, 64, 0),
                     "kbd1", "bk_k1a" + t)
                kdma(half_view(self.s_kbd[1], 64, 1), half_view(self.s_kt, 64, 1),
                     "kbd1", "bk_k1b" + t)
                sch.op("gp", lambda e: e.engine_nop(),
                       reads=[b["kt"]],
                       writes=[b["kbd0"], b["kbd1"]],
                       flushq="q_gp", tag=f"kflush_b{bi}p{pair}")

        vones_ap = bass.AP(self.s_vp, 64, [[NKT * 130, 128], [130, NKT], [65, 2]])
        sch.op("dve", lambda e, a=vones_ap: e.memset(a, 1.0),
               writes=[b["vp"]], tag=f"vones_b{bi}p{pair}")
        for qb in range(NQB):
            ps = self.p_sl[qb][:, :]
            def vmm(e, ps=ps, wq=wq, qb=qb):
                r = None
                for c in range(NFC):
                    r = e.matmul(ps, wq[:, c, 2, :],
                                 self.s_x0t[:, qb, c, :],
                                 start=(c == 0), stop=(c == NFC - 1))
                return r
            sch.op("pe", vmm, reads=[bwq, b[f"x0t{qb}"]], writes=[b[f"sl{qb}"]],
                   tag=f"vproj_b{bi}p{pair}q{qb}")
            bias = self.s_bqkv[:, 16 + pair:17 + pair]
            sch.op("act", lambda e, ps=ps, qb=qb, bias=bias:
                   e.activation(out=self.s_vt[:, qb * 512:(qb + 1) * 512], in_=ps,
                                func=AF.Identity, bias=bias, scale=1.0),
                   reads=[b[f"sl{qb}"], b["consts"]], writes=[b["vt"]],
                   tag=f"vevac_b{bi}p{pair}q{qb}")
        for tt in range(NKT):
            psb = self.p_sl[tt % 4].ap()[:, 0:64].bitcast(BF16)
            sch.op("pe", lambda e, psb=psb, tt=tt: e.transpose(
                psb, self.s_vt[:, tt * 128:(tt + 1) * 128], self.s_identbf[:]),
                reads=[b["vt"], b["consts"]], writes=[b[f"sl{tt % 4}"]],
                tag=f"vtr_b{bi}p{pair}t{tt}")
            dst = bass.AP(self.s_vp, tt * 130, [[NKT * 130, 128], [65, 2], [1, 64]])
            sch.op("dve", lambda e, psb=psb, dst=dst:
                   e.tensor_copy(dst, psb.rearrange("p (h d) -> p h d", h=2)),
                   reads=[b[f"sl{tt % 4}"]], writes=[b["vp"]],
                   tag=f"vevacT_b{bi}p{pair}t{tt}")

        for qb in range(NQB):
            nkt = 4 * qb + 4
            cs_items = self._cs_items(bi, pair, qb)
            ncs, done = len(cs_items), 0
            for kt in range(nkt + 1):
                if kt == 2 and qb > 0:
                    self._qb_denom(bi, pair, qb - 1)
                for hl in range(2):
                    if kt < nkt:
                        sb_i = hl * 2 + kt % 2
                        ps, pb = self.p_sl[sb_i], b[f"sl{sb_i}"]
                        diag = kt - 4 * qb
                        def stgrp(e, ps=ps, hl=hl, kt=kt, qb=qb, diag=diag):
                            r = e.matmul(ps[:, :],
                                         self.s_kbd[hl][:, kt * 128:(kt + 1) * 128],
                                         self.s_qd[hl][:, qb * 512:(qb + 1) * 512],
                                         start=True, stop=(diag < 0))
                            if diag >= 0:
                                off = diag * (diag + 1) // 2 * 128
                                w = (diag + 1) * 128
                                r = e.matmul(ps[:, 0:w], self.s_identbf[:],
                                             self.s_mtq[:, off:off + w], start=False,
                                             stop=True)
                            return r
                        sch.op("pe", stgrp,
                               reads=[b[f"qd{hl}"], b[f"kbd{hl}"], b["consts"]],
                               writes=[pb], tag=f"st_b{bi}p{pair}q{qb}k{kt}h{hl}")
                    if kt > 0:
                        ktp = kt - 1
                        et = self.s_et[hl][ktp % 2]
                        eb = b[f"et{hl}{ktp % 2}"]
                        pa, pab = self.p_av[hl], b[f"av{hl}"]
                        vsl = self.s_vp[:, ktp, hl * 65:hl * 65 + 65]
                        sch.op("pe", lambda e, pa=pa, vsl=vsl, et=et, ktp=ktp, nkt=nkt:
                               e.matmul(pa[0:65, :], vsl, et[:],
                                        start=(ktp == 0), stop=(ktp == nkt - 1)),
                               reads=[b["vp"], eb], writes=[pab],
                               tag=f"av_b{bi}p{pair}q{qb}k{ktp}h{hl}")
                    if kt < nkt:
                        sb_i = hl * 2 + kt % 2
                        ps, pb = self.p_sl[sb_i], b[f"sl{sb_i}"]
                        et = self.s_et[hl][kt % 2]
                        eb = b[f"et{hl}{kt % 2}"]
                        sch.op("act", lambda e, ps=ps, et=et:
                               e.activation(out=et[:], in_=ps[:, :], func=AF.Exp,
                                            scale=0.125),
                               reads=[pb], writes=[eb],
                               tag=f"exp_b{bi}p{pair}q{qb}k{kt}h{hl}")
                tgt = (kt + 1) * ncs // (nkt + 1)
                while done < tgt:
                    cs_items[done]()
                    done += 1

            self._av_evac(bi, pair, qb)
        for q in range(NQB - 1):
            self._one_denom(bi, pair, q)
        self._qb_denom(bi, pair, NQB - 1)
        self._one_denom(bi, pair, NQB - 1)
        self.dbg(f"a2n_b{bi}p{pair}", self.s_a2n[pair][:], [128, S], BF16,
                 [self.b_a2n[pair]])
        self.dbg(f"sig_b{bi}p{pair}", self.s_sig[:], [33, S], BF16, [b["sig"]])
        self.dbg(f"mpair_b{bi}p{pair}", self.s_mpair[:], [128, 32], F32,
                 [b["mpair"]])
        self.dbg(f"qt_b{bi}p{pair}", self.s_qd[0][:], [128, S], BF16, [b["qd0"]])
        self.dbg(f"vp_b{bi}p{pair}", self.s_vp[:], [128, NKT, 130], BF16,
                 [b["vp"]])

    def _cs_items(self, bi, pair, qb):
        sch, b = self.sch, self.b
        items = []
        idx = [0]
        for i in range(4):
            qt = 4 * qb + i
            for qh in range(2):
                jj = qt % 4
                ext = (qt + 1) * 128
                nr = (ext + 511) // 512
                for rd in range(nr):
                    lo = rd * 512
                    w = min(ext - lo, 512)
                    dlast = (rd == nr - 1)
                    def item(qt=qt, qh=qh, rd=rd, lo=lo, w=w, dlast=dlast, jj=jj):
                        cb = idx[0] % 2
                        po = cb * 512
                        idx[0] += 1
                        def cpg(e, lo=lo, w=w, qh=qh, qt=qt, po=po,
                                dlast=dlast, jj=jj):
                            if not dlast:
                                return e.matmul(
                                    self.p_cp[:, po:po + w],
                                    self.s_qbd2[qh][:, qt * 128:(qt + 1) * 128],
                                    self.s_kt[:, lo:lo + w],
                                    start=True, stop=True)
                            off = jj * (jj + 1) // 2 * 128
                            e.matmul(self.p_cp[:, po:po + w],
                                     self.s_qbd2[qh][:, qt * 128:(qt + 1) * 128],
                                     self.s_kt[:, lo:lo + w],
                                     start=True, stop=False)
                            return e.matmul(self.p_cp[:, po:po + w],
                                            self.s_identbf[:],
                                            self.s_mqk2[qh][:, off:off + w],
                                            start=False, stop=True)
                        cbank = b[f"cpb{cb}"]
                        sch.op("pe", cpg,
                               reads=[b[f"qbd2{qh}"], b["kt"], b["consts"]],
                               writes=[cbank],
                               tag=f"cs_b{bi}p{pair}t{qt}h{qh}r{rd}")
                        mcol = self.s_mpair[:, qt * 2 + qh:qt * 2 + qh + 1]
                        if rd == 0:
                            sch.op("dve", lambda e, mcol=mcol, w=w, po=po:
                                   e.reduce_max(mcol, self.p_cp[:, po:po + w],
                                                axis=AX.X),
                                   reads=[cbank], writes=[b["mpair"]],
                                   tag=f"max_b{bi}p{pair}t{qt}h{qh}")
                        else:
                            nr1 = (qt // 4 + 1) - 1
                            pc = (0, 0, 8, 24)[qt // 4] \
                                + ((qt % 4) * 2 + qh) * nr1 + (rd - 1)
                            mp = self.s_mprt[:, pc:pc + 1]
                            sch.op("dve", lambda e, mp=mp, w=w, po=po:
                                   e.reduce_max(mp, self.p_cp[:, po:po + w],
                                                axis=AX.X),
                                   reads=[cbank], writes=[b["mprt"]],
                                   tag=f"maxp_b{bi}p{pair}t{qt}h{qh}r{rd}")
                    items.append(item)
        return items

    def _av_evac(self, bi, pair, qb):
        sch, b = self.sch, self.b
        for hl in range(2):
            pa, pab = self.p_av[hl], b[f"av{hl}"]
            srow = 32 * hl
            so = sch.op("act", lambda e, pa=pa, srow=srow, qb=qb:
                        e.copy(out=self.s_sig[srow:srow + 1, qb * 512:(qb + 1) * 512],
                               in_=pa[64:65, :]),
                        reads=[pab], writes=[b["sig"]],
                        tag=f"sigevac_b{bi}p{pair}q{qb}h{hl}")
            sch.op("dve", lambda e, pa=pa, hl=hl, qb=qb, pair=pair:
                   e.tensor_copy(self.s_a2n[pair][64 * hl:64 * hl + 64,
                                                  qb * 512:(qb + 1) * 512],
                                 pa[0:64, :]),
                   reads=[pab], writes=[self.b_a2n[pair]], deps=[so],
                   tag=f"aevac_b{bi}p{pair}q{qb}h{hl}")

    def _qb_denom(self, bi, pair, qb):
        sch, b = self.sch, self.b
        nr1 = qb
        pb0 = (0, 0, 8, 24)[qb]
        for k in range(1, nr1):
            sch.op("dve", lambda e, k=k, nr1=nr1, pb0=pb0: e.tensor_tensor(
                out=bass.AP(self.s_mprt, pb0 + k, [[48, 128], [nr1, 8]]),
                in0=bass.AP(self.s_mprt, pb0 + k - 1, [[48, 128], [nr1, 8]]),
                in1=bass.AP(self.s_mprt, pb0 + k, [[48, 128], [nr1, 8]]),
                op=ALU.max),
                reads=[b["mprt"]], writes=[b["mprt"]],
                tag=f"mcmb_b{bi}p{pair}q{qb}k{k}")
        if nr1 > 0:
            sch.op("dve", lambda e, nr1=nr1, qb=qb, pb0=pb0: e.tensor_tensor(
                out=self.s_mpair[:, 8 * qb:8 * qb + 8],
                in0=bass.AP(self.s_mprt, pb0 + nr1 - 1, [[48, 128], [nr1, 8]]),
                in1=self.s_mpair[:, 8 * qb:8 * qb + 8],
                op=ALU.max),
                reads=[b["mprt"]], writes=[b["mpair"]],
                tag=f"mcmbf_b{bi}p{pair}q{qb}")
        sch.op("act", lambda e, qb=qb: e.activation(
            out=self.s_emg[:, 8 * qb:8 * qb + 8],
            in_=self.s_mpair[:, 8 * qb:8 * qb + 8], func=AF.Exp, scale=0.125),
            reads=[b["mpair"]], writes=[b["emg"]], tag=f"em_b{bi}p{pair}q{qb}")
        sch.op("dve", lambda e, qb=qb: e.tensor_scalar(
            out=self.s_emgbf[:, 8 * qb:8 * qb + 8],
            in0=self.s_emg[:, 8 * qb:8 * qb + 8],
            scalar1=self.s_ghostb[:, pair:pair + 1], scalar2=None, op0=ALU.mult),
            reads=[b["emg"], b["consts"]], writes=[b["emgbf"]],
            tag=f"emg_b{bi}p{pair}q{qb}")

    def _pair_denoms(self, bi, pair):
        sch, b = self.sch, self.b
        for qb in range(NQB):
            self._one_denom(bi, pair, qb)

    def _one_denom(self, bi, pair, qb):
        sch, b = self.sch, self.b
        for hl in range(2):
            srow = 32 * hl
            r0 = 64 * hl
            ps = self.p_cp[:, hl * 512:(hl + 1) * 512]
            pb = b[f"cpb{hl}"]
            def dmm(e, ps=ps, hl=hl, qb=qb, srow=srow, r0=r0):
                r = e.matmul(ps[r0:r0 + 64, 0:512],
                             self.s_ones33bf[srow:srow + 1, 0:64],
                             self.s_sig[srow:srow + 1, qb * 512:(qb + 1) * 512],
                             start=True, stop=False, tile_position=(srow, r0))
                for i in range(4):
                    qt = 4 * qb + i
                    for qh in range(2):
                        col = self.s_emgbf[:, qt * 2 + qh:qt * 2 + qh + 1]
                        r = e.matmul(
                            ps[r0:r0 + 64,
                               i * 128 + qh * 64:i * 128 + qh * 64 + 64],
                            col.broadcast_to([128, 64]),
                            self.s_identbf[:, 64 * hl:64 * hl + 64],
                            start=False, stop=(i == 3 and qh == 1),
                            tile_position=(0, r0))
                return r
            sch.op("pe", dmm, reads=[b["emgbf"], b["sig"], b["consts"]], writes=[pb],
                   tag=f"denom_b{bi}p{pair}h{hl}q{qb}")
        for hl in range(2):
            r0 = 64 * hl
            ps = self.p_cp[:, hl * 512:(hl + 1) * 512]
            pb = b[f"cpb{hl}"]
            sch.op("dve", lambda e, ps=ps, r0=r0: e.reciprocal(
                self.s_r[r0:r0 + 64, :], ps[r0:r0 + 64, :]),
                reads=[pb], writes=[b["r"]], tag=f"recip_b{bi}p{pair}h{hl}q{qb}")
        for hl in range(2):
            r0 = 64 * hl
            sch.op("dve", lambda e, hl=hl, qb=qb, pair=pair, r0=r0: e.tensor_tensor(
                out=self.s_a2n[pair][r0:r0 + 64, qb * 512:(qb + 1) * 512],
                in0=self.s_a2n[pair][r0:r0 + 64, qb * 512:(qb + 1) * 512],
                in1=self.s_r[r0:r0 + 64, :], op=ALU.mult),
                reads=[b["r"], self.b_a2n[pair]], writes=[self.b_a2n[pair]],
                tag=f"norm_b{bi}p{pair}h{hl}q{qb}")

    def emit_wo_ln1(self, bi):
        sch, b = self.sch, self.b
        banks = self.p_av + self.p_sl[0:2]
        bnames = ("av0", "av1", "sl0", "sl1")

        def womm(qb, fc):
            ps, pb = banks[fc % 4], b[bnames[fc % 4]]
            np_ = self.npair
            def mm(e, ps=ps, fc=fc, qb=qb, np_=np_):
                r = None
                for p in range(np_):
                    r = e.matmul(ps[:, 0:512],
                                 self.s_wot[:, p, fc * 128:(fc + 1) * 128],
                                 self.s_a2n[p][:, qb * 512:(qb + 1) * 512],
                                 start=(p == 0), stop=(p == NPAIR - 1))
                return r
            sch.op("pe", mm, reads=[b["wot"]] + list(self.b_a2n[:self.npair]),
                   writes=[pb], tag=f"wo_b{bi}q{qb}f{fc}")

        def xres(qb, fc):
            ps, pb = banks[fc % 4], b[bnames[fc % 4]]
            sch.op("dve", lambda e, ps=ps, fc=fc, qb=qb: e.scalar_tensor_tensor(
                out=self.s_xres[:, fc, :], in0=ps[:, 0:512],
                scalar=self.s_bo[:, fc:fc + 1],
                in1=self.s_x0t[:, qb, fc, :],
                op0=ALU.add, op1=ALU.add),
                reads=[pb, b[f"x0t{qb}"], b["consts"]], writes=[b["xres"]],
                tag=f"xres_b{bi}q{qb}f{fc}")
            sch.op("act", lambda e, fc=fc: e.activation(
                out=self.s_xsq[:, fc % 2, :], in_=self.s_xres[:, fc, :],
                func=AF.Square),
                reads=[b["xres"]], writes=[b["xsq"]], tag=f"sq_b{bi}q{qb}f{fc}")

        def stats(qb, fc):
            o1 = (qb % 2) * 512
            b1 = b[f"cpb{qb % 2}"]
            sch.op("pe", lambda e, fc=fc, o1=o1: e.matmul(
                self.p_cp[0:1, o1:o1 + 512], self.s_ones128f[:],
                self.s_xres[:, fc, :],
                start=(fc == 0), stop=(fc == NFC - 1)),
                reads=[b["xres"], b["consts"]], writes=[b1],
                tag=f"sum_b{bi}q{qb}f{fc}")
            sch.op("pe", lambda e, fc=fc, o1=o1: e.matmul(
                self.p_cp[64:65, o1:o1 + 512], self.s_ones128f[:],
                self.s_xsq[:, fc % 2, :],
                start=(fc == 0), stop=(fc == NFC - 1)),
                reads=[b["xsq"], b["consts"]], writes=[b1],
                tag=f"sumsq_b{bi}q{qb}f{fc}")

        def rowchain(qb):
            o1 = (qb % 2) * 512
            b1 = b[f"cpb{qb % 2}"]
            sch.op("act", lambda e, o1=o1: e.activation(
                out=self.s_lrows[:], in_=self.p_cp[0:1, o1:o1 + 512],
                func=AF.Identity, scale=1.0 / D),
                reads=[b1], writes=[b["lrows"]], tag=f"mean_b{bi}q{qb}")
            sch.op("act", lambda e, o1=o1: e.activation(
                out=self.s_ltmp[:], in_=self.p_cp[64:65, o1:o1 + 512],
                func=AF.Identity, scale=1.0 / D),
                reads=[b1], writes=[b["ltmp"]], tag=f"msq_b{bi}q{qb}")
            sch.op("act", lambda e: e.activation(
                out=self.s_laux[:], in_=self.s_lrows[:], func=AF.Square),
                reads=[b["lrows"]], writes=[b["laux"]], tag=f"m2_b{bi}q{qb}")

        def rowchain2(qb):
            sch.op("dve", lambda e: e.scalar_tensor_tensor(
                out=self.s_lrows[:], in0=self.s_ltmp[:], scalar=EPS,
                in1=self.s_laux[:], op0=ALU.add, op1=ALU.subtract),
                reads=[b["ltmp"], b["laux"]], writes=[b["lrows"]],
                tag=f"var_b{bi}q{qb}")
            sch.op("act", lambda e: e.activation(
                out=self.s_ltmp[:], in_=self.s_lrows[:], func=AF.Sqrt),
                reads=[b["lrows"]], writes=[b["ltmp"]], tag=f"sqrt_b{bi}q{qb}")
            sch.op("dve", lambda e: e.reciprocal(self.s_laux[:], self.s_ltmp[:]),
                   reads=[b["ltmp"]], writes=[b["laux"]], tag=f"rstd_b{bi}q{qb}")

        def bmean(qb):
            sch.op("pe", lambda e: e.matmul(
                self.p_sl[2][:, :], self.s_ones1f[:],
                self.s_lrows[:], start=True, stop=True),
                reads=[b["lrows"], b["consts"]], writes=[b["sl2"]],
                tag=f"bmean_b{bi}q{qb}")

        def brstd(qb):
            sch.op("pe", lambda e: e.matmul(
                self.p_sl[3][:, :], self.s_ones1f[:],
                self.s_laux[:], start=True, stop=True),
                reads=[b["laux"], b["consts"]], writes=[b["sl3"]],
                tag=f"brstd_b{bi}q{qb}")

        subscr = [self.s_r, self.s_oh]
        subbuf = [b["r"], b["oh"]]

        def sub(qb, fc):
            sch.op("dve", lambda e, fc=fc: e.scalar_tensor_tensor(
                out=subscr[fc % 2][:], in0=self.p_sl[2][:, :],
                scalar=-1.0, in1=self.s_xres[:, fc, :],
                op0=ALU.mult, op1=ALU.add),
                reads=[b["xres"], b["sl2"]], writes=[subbuf[fc % 2]],
                tag=f"sub_b{bi}q{qb}f{fc}")

        def x1(qb, fc):
            sch.op("dve", lambda e, fc=fc: e.tensor_tensor(
                out=self.s_x1qb[:, fc, :], in0=self.p_sl[3][:, :],
                in1=subscr[fc % 2][:], op=ALU.mult),
                reads=[subbuf[fc % 2], b["sl3"]], writes=[b["x1qb"]],
                tag=f"x1_b{bi}q{qb}f{fc}")
            dst = bass.AP(self.d_x1sc, fc * 128 * (NB * S) + bi * S + qb * 512,
                          [[NB * S, 128], [1, 512]])
            if "stx1" not in self.skip:
                sch.op("sp", lambda e, fc=fc, dst=dst:
                       e.dma_start(out=dst, in_=self.s_x1qb[:, fc, :]),
                       reads=[b["x1qb"]], writes=[b["x1sc"]], is_dma=True,
                       tag=f"st_x1_b{bi}q{qb}f{fc}")

        def gmm(qb):
            o1 = (qb % 2) * 512
            if "gmm" in self.skip:
                return
            def g(e, qb=qb, o1=o1):
                r = None
                for tt in range(4):
                    for fc in range(NFC):
                        r = e.matmul(self.p_cp[:, o1 + tt * 8:o1 + tt * 8 + 8],
                                     self.s_x1qb[:, fc, tt * 128:(tt + 1) * 128],
                                     self.s_wgbf[:, fc, :],
                                     start=(fc == 0), stop=(fc == NFC - 1))
                return r
            sch.op("pe", g, reads=[b["x1qb"], b["consts"]],
                   writes=[b[f"cpb{qb % 2}"]], tag=f"gate_b{bi}q{qb}")
            sch.op("dve", lambda e, qb=qb, o1=o1: e.tensor_copy(
                self.s_lall[:, qb * 32:qb * 32 + 32], self.p_cp[:, o1:o1 + 32]),
                reads=[b[f"cpb{qb % 2}"]], writes=[b["lall"]],
                tag=f"levac_b{bi}q{qb}")

        def tail_head(qp):
            rowchain(qp)

        def tail_mid(qp, fc):
            sub(qp, fc)
            if fc > 0:
                x1(qp, fc - 1)

        def tail_end(qp):
            x1(qp, NFC - 1)
            gmm(qp)

        for qb in range(NQB):
            if qb > 0:
                tail_head(qb - 1)
            womm(qb, 0)
            if qb > 0:
                bmean(qb - 1)
                rowchain2(qb - 1)
            womm(qb, 1)
            if qb > 0:
                brstd(qb - 1)
            for fc in range(NFC):
                if qb > 0:
                    tail_mid(qb - 1, fc)
                xres(qb, fc)
                if fc + 2 < NFC:
                    womm(qb, fc + 2)
                if fc > 0:
                    stats(qb, fc - 1)
            if qb > 0:
                tail_end(qb - 1)
            stats(qb, NFC - 1)
            if bi + 1 < self.nb:
                self._x0_chunk_load(bi + 1, qb)
        tail_head(NQB - 1)
        bmean(NQB - 1)
        rowchain2(NQB - 1)
        brstd(NQB - 1)
        for fc in range(NFC):
            tail_mid(NQB - 1, fc)
        tail_end(NQB - 1)
        sch.op("sp", lambda e: e.nop(), reads=[b["x1qb"]], writes=[b["x1qb"]],
               flushq="q_sp", tag=f"x1flush_b{bi}")
        self.dbg(f"x1qb_b{bi}", self.s_x1qb[:], [128, NFC, 512], BF16, [b["x1qb"]])
        self.dbg(f"xres_b{bi}", self.s_xres[:], [128, NFC, 512], F32, [b["xres"]])
        self.dbg(f"lall_b{bi}", self.s_lall[:], [128, 512], F32, [b["lall"]])

    def emit_gate_batch(self, bi):
        sch, b = self.sch, self.b
        l3 = self.s_lall.ap().rearrange("p (t e) -> p t e", e=8)
        sch.op("dve", lambda e, l3=l3: e.reduce_max(self.s_xsq[:, 0, 0:64], l3, axis=AX.X),
               reads=[b["lall"]], writes=[b["xsq"]], tag=f"gmax_b{bi}")
        mx3 = self.s_xsq.ap()[:, 0, 0:64].rearrange("p (t o) -> p t o", o=1)
        sch.op("dve", lambda e, l3=l3, mx3=mx3: e.tensor_tensor(
            out=self.s_oh.ap().rearrange("p (t e) -> p t e", e=8), in0=l3,
            in1=mx3.broadcast_to([128, 64, 8]), op=ALU.is_equal),
            reads=[b["lall"], b["xsq"]], writes=[b["oh"]], tag=f"oneh_b{bi}")
        for g in range(4):
            def trg(e, g=g, src=self.s_oh):
                r = None
                for i in range(4):
                    tt = 4 * g + i
                    r = e.transpose(self.p_cp[0:8, i * 128:(i + 1) * 128],
                                    src[:, tt * 8:tt * 8 + 8], self.s_identf[:])
                return r
            sch.op("pe", trg, reads=[b["oh"], b["consts"]], writes=[b["cpb0"]],
                   tag=f"ohT_b{bi}g{g}")
            sch.op("dve", lambda e, g=g: e.tensor_copy(
                self.s_gateT[0:8, bi * S + g * 512:bi * S + (g + 1) * 512],
                self.p_cp[0:8, 0:512]),
                reads=[b["cpb0"]], writes=[b["gateT"]], tag=f"ohTe_b{bi}g{g}")
            def trl(e, g=g):
                r = None
                for i in range(4):
                    tt = 4 * g + i
                    r = e.transpose(self.p_cp[0:8, 512 + i * 128:640 + i * 128],
                                    self.s_lall[:, tt * 8:tt * 8 + 8], self.s_identf[:])
                return r
            sch.op("pe", trl, reads=[b["lall"], b["consts"]], writes=[b["cpb1"]],
                   tag=f"lT_b{bi}g{g}")
            sch.op("act", lambda e, g=g: e.activation(
                out=self.s_gateT[32:40, bi * S + g * 512:bi * S + (g + 1) * 512],
                in_=self.p_cp[0:8, 512:1024], func=AF.Exp),
                reads=[b["cpb1"]], writes=[b["gateT"]], tag=f"exT_b{bi}g{g}")

    def emit_routing(self):
        sch, b = self.sch, self.b
        T = self.nb * S
        sch.op("dve", lambda e: e.tensor_tensor_scan(
            out=self.s_scan[:, 0:T], data0=self.s_gateT[0:8, 0:T],
            data1=self.s_gateT[0:8, 0:T], initial=0.0, op0=ALU.add, op1=ALU.bypass),
            reads=[b["gateT"]], writes=[b["scan"]], tag="scan")
        sch.op("dve", lambda e: e.drain(), tag="scan_drain")
        if self.use_cc:
            sch.op("gp", lambda e: e.dma_start(out=self.d_ccin.ap(),
                                               in_=self.s_scan[0:8, T - 1:T]),
                   reads=[b["scan"]], writes=[b["ccin"]], is_dma=True, tag="ccin")
            cco = sch.op("gp", lambda e: e.collective_compute(
                "AllGather", ALU.bypass, replica_groups=self.cc_rg,
                ins=[self.d_ccin.ap().rearrange("a b -> b a").opt()],
                outs=[self.d_ccout.ap().opt()]),
                reads=[b["ccin"]], writes=[b["ccout"]], is_dma=True, tag="allgather")
            cco.delta = 1
            sch.op("gp", lambda e: e.dma_start(out=self.s_agc[:], in_=self.d_ccout.ap()),
                   reads=[b["ccout"]], writes=[b["agc"]], is_dma=True, tag="ld_agc")
            sch.op("pe", lambda e: e.matmul(self.p_sl[0][0:8, 0:1], self.s_agc[:],
                                            self.s_premask[:], start=True, stop=True),
                   reads=[b["agc"], b["consts"]], writes=[b["sl0"]], tag="cbmm")
            sch.op("dve", lambda e: e.tensor_copy(self.s_cb[:], self.p_sl[0][0:8, 0:1]),
                   reads=[b["sl0"]], writes=[b["cb"]], tag="cbevac")
        else:
            sch.op("dve", lambda e: e.memset(self.s_cb[:], 0.0),
                   writes=[b["cb"]], tag="cbzero")
        sch.op("dve", lambda e: e.tensor_scalar(
            out=self.s_sel[:, 0:T], in0=self.s_scan[:, 0:T], scalar1=self.s_cb[:],
            scalar2=float(CAP + 1), op0=ALU.add, op1=ALU.is_lt),
            reads=[b["scan"], b["cb"]], writes=[b["sel"]], tag="sel")
        sch.op("dve", lambda e: e.tensor_tensor(
            out=self.s_so[32:40, 0:T], in0=self.s_gateT[0:8, 0:T],
            in1=self.s_sel[:, 0:T],
            op=ALU.mult), reads=[b["gateT"], b["sel"]], writes=[b["so"]], tag="so")
        sch.op("dve", lambda e: e.tensor_tensor(
            out=self.s_gateT[0:8, 0:T], in0=self.s_so[32:40, 0:T],
            in1=self.s_gateT[32:40, 0:T],
            op=ALU.mult), reads=[b["so"], b["gateT"]], writes=[b["gateT"]], tag="soe")
        nhalf = T // 1024
        for tb in range(T // 512):
            row, col = (0, tb * 512) if tb < nhalf else (32, (tb - nhalf) * 512)
            ps = self.p_sl[tb % 2]
            pb = b[f"sl{tb % 2}"]
            sch.op("pe", lambda e, ps=ps, tb=tb: e.matmul(
                ps[0:1, 0:512], self.s_ones128bf[0:8, :],
                self.s_gateT[0:8, tb * 512:(tb + 1) * 512], start=True, stop=True),
                reads=[b["gateT"], b["consts"]], writes=[pb], tag=f"nummm{tb}")
            sch.op("dve", lambda e, ps=ps, row=row, col=col: e.tensor_copy(
                self.s_num[row:row + 1, col:col + 512], ps[0:1, 0:512]),
                reads=[pb], writes=[b["num"]], tag=f"nume{tb}")
            sch.op("pe", lambda e, ps=ps, tb=tb: e.matmul(
                ps[0:1, 0:512], self.s_ones128bf[32:40, :],
                self.s_gateT[32:40, tb * 512:(tb + 1) * 512], start=True, stop=True),
                reads=[b["gateT"], b["consts"]], writes=[pb], tag=f"zmm{tb}")
            sch.op("dve", lambda e, ps=ps, row=row, col=col: e.tensor_copy(
                self.s_zrow[row:row + 1, col:col + 512], ps[0:1, 0:512]),
                reads=[pb], writes=[b["zrow"]], tag=f"ze{tb}")
        half = T // 2
        for hf, row in ((0, 0), (1, 32)):
            sch.op("dve", lambda e, row=row, half=half: e.reciprocal(
                self.s_mgt[row:row + 1, 0:half], self.s_zrow[row:row + 1, 0:half]),
                reads=[b["zrow"]], writes=[b["mgt"], b["scan"]], tag=f"zrec{row}")
            sch.op("dve", lambda e, row=row, hf=hf, half=half: e.tensor_tensor(
                out=self.s_mgbf[0:1, hf * half:(hf + 1) * half],
                in0=self.s_num[row:row + 1, 0:half],
                in1=self.s_mgt[row:row + 1, 0:half], op=ALU.mult),
                reads=[b["num"], b["mgt"]], writes=[b["mgbf"]], tag=f"mg{row}")
        self.dbg("mg", self.s_mgbf[:], [1, T], BF16, [b["mgbf"]])
        self.dbg("scan", self.s_scan[:], [8, T], F32, [b["scan"]])

    def fence(self):
        sch = self.sch
        last = {}
        for o in sch.order:
            last[o.stream] = o
        ops = []
        for st in ("pe", "dve", "act", "gp", "sp"):
            deps = [v for k, v in last.items() if k != st]
            fn = (lambda e: e.engine_nop()) if st in ("dve", "gp") else None
            ops.append(sch.op(st, fn, deps=deps, tag=f"fence_{st}",
                              flushq="q_sp" if st == "sp" else None))

    def emit_phase_b(self):
        sch, b = self.sch, self.b
        self.fence()
        wexp_ops = []
        for fo in range(NFC):
            wsrc = self.d["wexp"].ap().rearrange(
                "(c p) f -> p c f", p=128)[:, :, fo * 128:(fo + 1) * 128]
            wexp_ops.append(sch.op("gp", lambda e, i=wsrc, fo=fo: e.dma_start(
                out=self.s_wexp[:, :, fo * 128:(fo + 1) * 128], in_=i),
                writes=[b["wexp"]] if fo == 0 else [], is_dma=True,
                tag=f"ld_wexp{fo}"))
        T = self.nb * S
        banks = [self.p_av[0], self.p_av[1], self.p_sl[0]]
        bnames = ("av0", "av1", "sl0")

        def loads(tb):
            x1, bx1 = self.s_x1tb[tb % 2], b[f"x1tb{tb % 2}"]
            for fc in range(NFC):
                src = bass.AP(self.d_x1sc, fc * 128 * T + tb * 512,
                              [[T, 128], [1, 512]])
                sch.op("sp", lambda e, src=src, x1=x1, fc=fc:
                       e.dma_start(out=x1[:, fc, :], in_=src),
                       reads=[b["x1sc"]], writes=[bx1], is_dma=True,
                       tag=f"ld_x1_{tb}f{fc}")

        def mgb(tb):
            col = tb * 512
            sch.op("pe", lambda e, col=col: e.matmul(
                self.p_sl[3][:, :], self.s_ones1bf[:],
                self.s_mgbf[0:1, col:col + 512], start=True, stop=True),
                reads=[b["mgbf"], b["consts"]], writes=[b["sl3"]], tag=f"mgb{tb}")
            x1, bx1 = self.s_x1tb[tb % 2], b[f"x1tb{tb % 2}"]
            x13 = x1.ap().rearrange("p c t -> p c t")
            mg3 = self.p_sl[3].ap()[:, :].rearrange("p (o t) -> p o t", o=1)
            sch.op("dve", lambda e, x13=x13, mg3=mg3, tb=tb: e.tensor_tensor(
                out=self.s_x1mg.ap()[:, tb % 2], in0=mg3.broadcast_to([128, NFC, 512]),
                in1=x13, op=ALU.mult),
                reads=[bx1, b["sl3"]], writes=[b["x1mg"]], tag=f"x1mg{tb}")

        def emm(tb, fo):
            ps, pb = banks[fo % 3], b[bnames[fo % 3]]
            col = tb * 512
            def mm(e, ps=ps, fo=fo, col=col, tb=tb):
                for c in range(NFC):
                    e.matmul(ps[:, 0:512],
                             self.s_wexp[:, c, fo * 128:(fo + 1) * 128],
                             self.s_x1mg[:, tb % 2, c, :], start=(c == 0),
                             stop=False)
                return e.matmul(ps[:, 0:512],
                                self.s_berowbf[:, fo * 128:(fo + 1) * 128],
                                self.s_mgbf[0:1, col:col + 512],
                                start=False, stop=True)
            sch.op("pe", mm, reads=[b["x1mg"], b["mgbf"], b["consts"]],
                   deps=[wexp_ops[fo]],
                   writes=[pb], tag=f"exp_mm{tb}f{fo}")

        def xres2(tb, fo):
            ps, pb = banks[fo % 3], b[bnames[fo % 3]]
            x1, bx1 = self.s_x1tb[tb % 2], b[f"x1tb{tb % 2}"]
            sch.op("dve", lambda e, ps=ps, fo=fo, x1=x1: e.tensor_tensor(
                out=self.s_xres2[:, fo, :], in0=ps[:, 0:512], in1=x1[:, fo, :],
                op=ALU.add),
                reads=[pb, bx1], writes=[b["xres2"]], tag=f"xres2_{tb}f{fo}")
            sch.op("act", lambda e, fo=fo: e.activation(
                out=self.s_xsqb[:, fo % 2, :], in_=self.s_xres2[:, fo, :],
                func=AF.Square),
                reads=[b["xres2"]], writes=[b["xsqb"]], tag=f"sq2_{tb}f{fo}")

        def stats2(tb, fo):
            o1 = (tb % 2) * 512
            b1 = b[f"cpb{tb % 2}"]
            sch.op("pe", lambda e, fo=fo, o1=o1: e.matmul(
                self.p_cp[0:1, o1:o1 + 512], self.s_ones128f[:],
                self.s_xres2[:, fo, :],
                start=(fo == 0), stop=(fo == NFC - 1)),
                reads=[b["xres2"], b["consts"]], writes=[b1], tag=f"sum2_{tb}f{fo}")
            sch.op("pe", lambda e, fo=fo, o1=o1: e.matmul(
                self.p_cp[64:65, o1:o1 + 512], self.s_ones128f[:],
                self.s_xsqb[:, fo % 2, :],
                start=(fo == 0), stop=(fo == NFC - 1)),
                reads=[b["xsqb"], b["consts"]], writes=[b1], tag=f"sumsq2_{tb}f{fo}")

        def rowchainA(tb):
            o1 = (tb % 2) * 512
            b1 = b[f"cpb{tb % 2}"]
            sch.op("act", lambda e, o1=o1: e.activation(
                out=self.s_lrows[:], in_=self.p_cp[0:1, o1:o1 + 512],
                func=AF.Identity, scale=1.0 / D),
                reads=[b1], writes=[b["lrows"]], tag=f"mean2_{tb}")
            sch.op("act", lambda e, o1=o1: e.activation(
                out=self.s_ltmp[:], in_=self.p_cp[64:65, o1:o1 + 512],
                func=AF.Identity, scale=1.0 / D),
                reads=[b1], writes=[b["ltmp"]], tag=f"msq2_{tb}")
            sch.op("act", lambda e: e.activation(
                out=self.s_laux[:], in_=self.s_lrows[:], func=AF.Square),
                reads=[b["lrows"]], writes=[b["laux"]], tag=f"m22_{tb}")

        def rowchain2(tb):
            sch.op("dve", lambda e: e.scalar_tensor_tensor(
                out=self.s_lrows[:], in0=self.s_ltmp[:], scalar=EPS,
                in1=self.s_laux[:], op0=ALU.add, op1=ALU.subtract),
                reads=[b["ltmp"], b["laux"]], writes=[b["lrows"]], tag=f"var2_{tb}")
            sch.op("act", lambda e: e.activation(
                out=self.s_ltmp[:], in_=self.s_lrows[:], func=AF.Sqrt),
                reads=[b["lrows"]], writes=[b["ltmp"]], tag=f"sqrt2_{tb}")
            sch.op("dve", lambda e: e.reciprocal(self.s_laux[:], self.s_ltmp[:]),
                   reads=[b["ltmp"]], writes=[b["laux"]], tag=f"rstd2_{tb}")

        def bmean2(tb):
            sch.op("pe", lambda e: e.matmul(
                self.p_sl[2][:, :], self.s_ones1f[:],
                self.s_lrows[:], start=True, stop=True),
                reads=[b["lrows"], b["consts"]], writes=[b["sl2"]],
                tag=f"bmean2_{tb}")

        def brstd2(tb):
            sch.op("pe", lambda e: e.matmul(
                self.p_sl[1][:, :], self.s_ones1f[:],
                self.s_laux[:], start=True, stop=True),
                reads=[b["laux"], b["consts"]], writes=[b["sl1"]],
                tag=f"brstd2_{tb}")

        def sub2(tb, fo):
            sch.op("dve", lambda e, fo=fo: e.scalar_tensor_tensor(
                out=self.s_r[:] if fo % 2 == 0 else self.s_oh[:],
                in0=self.p_sl[2][:, :],
                scalar=-1.0, in1=self.s_xres2[:, fo, :],
                op0=ALU.mult, op1=ALU.add),
                reads=[b["xres2"], b["sl2"]],
                writes=[b["r"] if fo % 2 == 0 else b["oh"]],
                tag=f"sub2_{tb}f{fo}")

        def x2(tb, fo):
            bi, qb = tb // 4, tb % 4
            sch.op("dve", lambda e, fo=fo: e.tensor_tensor(
                out=self.s_x2o[:, fo, :], in0=self.p_sl[1][:, :],
                in1=self.s_r[:] if fo % 2 == 0 else self.s_oh[:], op=ALU.mult),
                reads=[b["r"] if fo % 2 == 0 else b["oh"], b["sl1"]],
                writes=[b["x2o"]], tag=f"x2_{tb}f{fo}")
            dst = bass.AP(self.d_out, bi * D * S + fo * 128 * S + qb * 512,
                          [[S, 128], [1, 512]])
            sch.op("act", lambda e, dst=dst, fo=fo:
                   e.dma_start(out=dst, in_=self.s_x2o[:, fo, :]),
                   reads=[b["x2o"]], is_dma=True, tag=f"st_x2_{tb}f{fo}")

        NTB = T // 512
        for tb in range(NTB):
            loads(tb)
            mgb(tb)
            if tb > 0:
                rowchainA(tb - 1)
            emm(tb, 0)
            if tb > 0:
                bmean2(tb - 1)
                rowchain2(tb - 1)
            emm(tb, 1)
            if tb > 0:
                brstd2(tb - 1)
                sch.op("act", lambda e: e.nop(), reads=[b["x2o"]],
                       writes=[b["x2o"]], flushq="q_act", tag=f"x2flush_{tb}")
            for fo in range(NFC):
                if tb > 0:
                    sub2(tb - 1, fo)
                    if fo > 0:
                        x2(tb - 1, fo - 1)
                xres2(tb, fo)
                if fo + 2 < NFC:
                    emm(tb, fo + 2)
                if fo > 0:
                    stats2(tb, fo - 1)
            if tb > 0:
                x2(tb - 1, NFC - 1)
            stats2(tb, NFC - 1)
        tb = NTB - 1
        rowchainA(tb)
        bmean2(tb)
        rowchain2(tb)
        brstd2(tb)
        sch.op("act", lambda e: e.nop(), reads=[b["x2o"]],
               writes=[b["x2o"]], flushq="q_act", tag="x2flush_last")
        for fo in range(NFC):
            sub2(tb, fo)
            if fo > 0:
                x2(tb, fo - 1)
        x2(tb, NFC - 1)

    def realize(self):
        sch = self.sch
        counts = sch.finalize()
        nc = self.nc
        streams = {}
        for o in sch.order:
            streams.setdefault(o.stream, []).append(o)

        with contextlib.ExitStack() as es:
            sems = {}
            for name in ["s_pe", "s_dve", "s_act", "s_gp", "s_sp", "q_sp", "q_gp",
                         "q_act", "s_cc"]:
                sems[name] = es.enter_context(nc.semaphore(name))
            block = es.enter_context(nc.Block())

            def replay(eng, ops, pad_raw=False, chain_dma=None):
                watermark = {}
                pos = {}
                slot = 0
                for o in ops:
                    if o.flushq is not None and o.flushval:
                        if watermark.get(o.flushq, 0) < o.flushval:
                            eng.wait_ge(sems[o.flushq], o.flushval)
                            watermark[o.flushq] = o.flushval
                    if chain_dma and o.is_dma and o.sem == chain_dma \
                            and o.incval is not None and o.incval > 16:
                        v = o.incval - 16
                        if watermark.get(o.sem, 0) < v:
                            eng.wait_ge(sems[o.sem], v)
                            watermark[o.sem] = v
                    waited_pe = False
                    for d in o.deps:
                        if d.stream == o.stream and not d.is_dma and not o.is_dma:
                            continue
                        if d.incval is None:
                            continue
                        if watermark.get(d.sem, 0) >= d.incval:
                            continue
                        eng.wait_ge(sems[d.sem], d.incval)
                        watermark[d.sem] = d.incval
                        if d.sem == "s_pe":
                            waited_pe = True
                    if pad_raw and waited_pe and o.fn is not None:
                        for _ in range(6):
                            eng.engine_nop()
                        slot += 6
                    if pad_raw and o.fn is not None:
                        need = 0
                        for d in o.deps:
                            if d.stream == o.stream and not d.is_dma and id(d) in pos:
                                need = max(need, pos[id(d)] + 3 - slot)
                        for _ in range(need):
                            eng.engine_nop()
                            slot += 1
                    if o.fn is None:
                        continue
                    inst = o.fn(eng)
                    pos[id(o)] = slot
                    slot += 1
                    if o.incval is not None:
                        assert inst is not None, o.tag
                        inst.then_inc(sems[o.sem], o.delta)

            @block.tensor
            def _(eng):
                replay(eng, streams.get("pe", []))

            @block.vector
            def _(eng):
                replay(eng, streams.get("dve", []), pad_raw=True)

            @block.scalar
            def _(eng):
                replay(eng, streams.get("act", []))

            @block.gpsimd
            def _(eng):
                replay(eng, streams.get("gp", []), chain_dma="q_gp")

            @block.sync
            def _(eng):
                replay(eng, streams.get("sp", []))
        return nc


def build(nb=NB, npair=NPAIR, use_cc=True, debug=(), skip=()):
    bld = Builder(nb=nb, npair=npair, use_cc=use_cc, debug=debug, skip=skip)
    bld.emit_all()
    nc = bld.realize()
    return nc, bld


def kernel(**inputs):
    in_maps = stage(inputs)
    nc, _ = build()
    res = run_bass_kernel_spmd(nc, in_maps, core_ids=list(range(NCORES)))
    out = np.empty((32, 2048, 1024), dtype=np.float32)
    for c in range(NCORES):
        out[c * NB:(c + 1) * NB] = res.results[c]["x2t"].transpose(0, 2, 1)
    return out


# version 26
# speedup vs baseline: 1.0224x; 1.0224x over previous
import contextlib
import numpy as np
import ml_dtypes
import concourse.bass as bass
import concourse.mybir as mybir
from concourse.bass_utils import run_bass_kernel_spmd
from concourse.replica_groups import maybe_share_collective_output_space

F32 = mybir.dt.float32
BF16 = mybir.dt.bfloat16
AF = mybir.ActivationFunctionType
ALU = mybir.AluOpType
AX = mybir.AxisListType
BF = ml_dtypes.bfloat16

NCORES = 8
NB = 4
S = 2048
D = 1024
NH = 16
HD = 64
NPAIR = 8
NQB = 4
NKT = 16
NFC = 8
CAP = 100
EPS = 1e-5
NEG = -1.0e30


class _Op:
    __slots__ = ("stream", "fn", "deps", "is_dma", "delta", "sem", "incval",
                 "needed", "idx", "tag", "flushq", "flushval")

    def __init__(self, stream, fn, is_dma, tag):
        self.flushq = None
        self.flushval = None
        self.stream = stream
        self.fn = fn
        self.deps = []
        self.is_dma = is_dma
        self.delta = 16 if is_dma else 1
        self.sem = None
        self.incval = None
        self.needed = False
        self.idx = None
        self.tag = tag


class Buf:
    __slots__ = ("name", "last_w", "readers", "rng", "overlaps")

    def __init__(self, name, rng=None):
        self.name = name
        self.last_w = None
        self.readers = []
        self.rng = rng
        self.overlaps = []


class Sched:
    def __init__(self):
        self.order = []
        self.bufs = []

    def link_overlaps(self):
        for i, a in enumerate(self.bufs):
            if a.rng is None:
                continue
            for c in self.bufs[i + 1:]:
                if c.rng is None:
                    continue
                if a.rng[0] < c.rng[1] and c.rng[0] < a.rng[1]:
                    a.overlaps.append(c)
                    c.overlaps.append(a)

    def op(self, stream, fn, reads=(), writes=(), deps=(), is_dma=False, tag="",
           flushq=None):
        o = _Op(stream, fn, is_dma, tag)
        o.flushq = flushq
        o.idx = len(self.order)
        dd = list(deps)
        for b in reads:
            if b.last_w is not None:
                dd.append(b.last_w)
        for b in writes:
            for bb in [b] + b.overlaps:
                if bb.last_w is not None:
                    dd.append(bb.last_w)
                dd.extend(bb.readers)
        seen = set()
        for d in dd:
            if d is None or id(d) in seen or d is o:
                continue
            seen.add(id(d))
            assert d.idx < o.idx
            o.deps.append(d)
        for b in reads:
            b.readers.append(o)
        for b in writes:
            b.last_w = o
            b.readers = []
            for bb in b.overlaps:
                bb.last_w = None
                bb.readers = []
        self.order.append(o)
        return o

    def finalize(self):
        for o in self.order:
            if o.tag == "allgather":
                o.sem = "s_cc"
            elif o.is_dma:
                o.sem = {"sp": "q_sp", "act": "q_act"}.get(o.stream, "q_gp")
            else:
                o.sem = "s_" + o.stream
        for o in self.order:
            for d in o.deps:
                if d.stream == o.stream and not d.is_dma and not o.is_dma:
                    continue
                d.needed = True
        counts = {}
        for o in self.order:
            if o.flushq is not None:
                o.flushval = counts.get(o.flushq, 0)
            if o.needed or o.is_dma:
                counts[o.sem] = counts.get(o.sem, 0) + o.delta
                o.incval = counts[o.sem]
        return counts



def _consts():
    identf = np.eye(128, dtype=np.float32)
    identbf = np.eye(128, dtype=BF)
    ones1f = np.ones((1, 128), dtype=np.float32)
    ones1bf = np.ones((1, 128), dtype=BF)
    ones128f = np.ones((128, 1), dtype=np.float32)
    ones128bf = np.ones((128, 1), dtype=BF)
    k = np.arange(128)[:, None]
    mtq = np.zeros((128, 1280), dtype=np.float32)
    mqk = np.zeros((128, 1280), dtype=np.float32)
    for j in range(4):
        off, w = j * (j + 1) // 2 * 128, (j + 1) * 128
        q5 = np.arange(w)[None, :]
        mtq[:, off:off + w] = np.where(q5 >= j * 128 + k, 0.0, NEG)
        mqk[:, off:off + w] = np.where(q5 <= j * 128 + k, 0.0, NEG)
    mqk2a = np.concatenate([mqk[0:64], mqk[0:64]], axis=0)
    mqk2b = np.concatenate([mqk[64:128], mqk[64:128]], axis=0)
    return {
        "identf": identf, "identbf": identbf,
        "ones1f": ones1f, "ones1bf": ones1bf,
        "ones128f": ones128f, "ones128bf": ones128bf,
        "maskstq": mtq.astype(BF),
        "masksqk2a": mqk2a.astype(BF), "masksqk2b": mqk2b.astype(BF),
    }


def stage(inputs):
    x0 = np.asarray(inputs["L0_attn_inputs"], dtype=np.float32)
    wqkv = np.asarray(inputs["L0_attn_Wqkv_w"], dtype=np.float32)
    bqkv = np.asarray(inputs["L0_attn_Wqkv_b"], dtype=np.float32)
    wo = np.asarray(inputs["L0_attn_Wo_w"], dtype=np.float32)
    bo = np.asarray(inputs["L0_attn_Wo_b"], dtype=np.float32)
    ghost = np.asarray(inputs["L0_attn_ghost"], dtype=np.float32)
    wg = np.asarray(inputs["L0_ffn_w_gate"], dtype=np.float32)
    we = np.asarray(inputs["L0_ffn_w_expert"], dtype=np.float32)
    be = np.asarray(inputs["L0_ffn_b_expert"], dtype=np.float32)

    shared = dict(_consts())
    shared["wqkvt"] = np.ascontiguousarray(wqkv.T)
    shared["wqkvb_pc"] = np.ascontiguousarray(bqkv.reshape(24, 128).T)
    shared["bv_row"] = np.ascontiguousarray(bqkv[2048:].reshape(1, 1024))
    shared["wot"] = np.ascontiguousarray(wo.T)
    shared["wob_pc"] = np.ascontiguousarray(bo.reshape(8, 128).T)
    shared["wgate3"] = np.ascontiguousarray(wg.reshape(8, 128, 8).transpose(1, 0, 2))
    shared["wexp"] = np.ascontiguousarray(we)
    shared["bexp_row"] = np.ascontiguousarray(be.reshape(1, 1024))
    gh2 = np.empty((128, 8), dtype=np.float32)
    gh2[0:64, :] = ghost[0::2][None, :]
    gh2[64:128, :] = ghost[1::2][None, :]
    shared["ghostb"] = gh2

    in_maps = []
    for c in range(NCORES):
        m = dict(shared)
        xb = x0[c * NB:(c + 1) * NB]
        m["x0t"] = np.ascontiguousarray(xb.transpose(0, 2, 1))
        m["premask"] = (np.arange(8) < c).astype(np.float32).reshape(8, 1)
        in_maps.append(m)
    return in_maps



class Builder:
    def __init__(self, nb=NB, npair=NPAIR, use_cc=True, debug=(), skip=()):
        self.skip = set(skip)
        self.nb = nb
        self.npair = npair
        self.use_cc = use_cc
        self.debug = set(debug)
        self.dbg_outs = {}
        self.nc = bass.Bass("TRN2", target_bir_lowering=False)
        self.sch = Sched()
        self.sb_ranges = {}
        self._sb_off = 16512
        self._mk_tensors()

    def _sb(self, name, shape, dtype, off=None):
        nc = self.nc
        per = int(np.prod(shape[1:])) * mybir.dt.size(dtype)
        if off is None:
            off = self._sb_off
            self._sb_off = (off + per + 31) & ~31
        h = nc.alloc_sbuf_tensor_at(name, shape, dtype, offset=off)
        self.sb_ranges[name] = (off, off + per)
        return h

    def _mk_tensors(self):
        nc = self.nc
        self.d = {}
        for name, shp, dt in [
            ("x0t", [NB, D, S], F32), ("wqkvt", [D, 3 * D], F32),
            ("wqkvb_pc", [128, 24], F32), ("bv_row", [1, D], F32),
            ("wot", [D, D], F32), ("wob_pc", [128, 8], F32),
            ("wgate3", [128, 8, 8], F32), ("wexp", [D, D], F32),
            ("bexp_row", [1, D], F32), ("ghostb", [128, 8], F32),
            ("identf", [128, 128], F32), ("identbf", [128, 128], BF16),
            ("ones1f", [1, 128], F32), ("ones1bf", [1, 128], BF16),
            ("ones128f", [128, 1], F32), ("ones128bf", [128, 1], BF16),
            ("maskstq", [128, 1280], BF16), ("masksqk2a", [128, 1280], BF16),
            ("masksqk2b", [128, 1280], BF16),
            ("premask", [8, 1], F32),
        ]:
            self.d[name] = nc.dram_tensor(name, shp, dt, kind="ExternalInput")
        self.d_out = nc.dram_tensor("x2t", [NB, D, S], F32, kind="ExternalOutput")
        self.d_x1sc = nc.dram_tensor("x1sc", [D, NB * S], BF16)
        if self.use_cc:
            rg = [list(range(NCORES))]
            space = maybe_share_collective_output_space("AllGather", rg)
            self.cc_rg = rg
            self.d_ccin = nc.dram_tensor("ccin", [8, 1], F32)
            self.d_ccout = nc.dram_tensor("ccout", [8, 8], F32, addr_space=space)

        self.s_identf = self._sb("identf", [128, 128], F32)
        self.s_identbf = self._sb("identbf", [128, 128], BF16)
        self.s_ones1f = self._sb("ones1f", [1, 128], F32)
        self.s_ones1bf = self._sb("ones1bf", [1, 128], BF16)
        self.s_ones128f = self._sb("ones128f", [128, 1], F32)
        self.s_ones128bf = self._sb("ones128bf", [128, 1], BF16)
        self.s_mtq = self._sb("maskstq", [128, 1280], BF16)
        self.s_mqk2 = [self._sb("masksqk2a", [128, 1280], BF16),
                       self._sb("masksqk2b", [128, 1280], BF16)]
        self.s_ghostb = self._sb("ghostb", [128, 8], F32)
        self.s_bqkv = self._sb("bqkv", [128, 24], F32)
        self.s_bo = self._sb("bo", [128, 8], F32)
        self.s_wg = self._sb("wg", [128, 8, 8], F32)
        self.s_wgbf = self._sb("wgbf", [128, 8, 8], BF16)
        self.s_bvrowbf = self._sb("bvrowbf", [1, D], BF16)
        self.s_berowbf = self._sb("berowbf", [1, D], BF16)
        self.s_premask = self._sb("premask", [8, 1], F32)
        self.s_agc = self._sb("agc", [8, 8], F32)
        self.s_cb = self._sb("cb", [8, 1], F32)
        self.s_ones33 = self._sb("ones33", [33, 128], F32)
        self.s_ones33bf = self._sb("ones33bf", [33, 128], BF16)
        self.s_lrows = self._sb("lrows", [1, 512], F32)
        self.s_ltmp = self._sb("ltmp", [1, 512], F32)
        self.s_laux = self._sb("laux", [1, 512], F32)
        self.s_gateT = self._sb("gateT", [40, NB * S], BF16)
        base = self._sb_off
        o = base
        def at(name, shape, dtype):
            nonlocal o
            h = self._sb(name, shape, dtype, off=o)
            o += (int(np.prod(shape[1:])) * mybir.dt.size(dtype) + 31) & ~31
            return h
        self.s_x0t = at("x0t", [128, NQB, NFC, 512], BF16)
        self.s_rowtmp = self._sb("rowtmp", [1, D], F32, off=base)
        self.s_wqp = [at(f"wqp{i}", [128, NFC, 3, 128], BF16) for i in range(2)]
        self.s_qd = [at(f"qd{h}", [128, S], BF16) for h in range(2)]
        self.s_kbd = [at(f"kbd{h}", [128, S], BF16) for h in range(2)]
        self.s_qbd2 = [at(f"qbd2{h}", [128, S], BF16) for h in range(2)]
        self.s_kt = at("kt", [128, S], BF16)
        self.s_vp = at("vp", [128, NKT, 130], BF16)
        self.s_vt = at("vt", [128, S], BF16)
        self.s_et = [[at(f"et{h}{p}", [128, 512], BF16) for p in range(2)] for h in range(2)]
        self.s_a2n = [at(f"a2n{p}", [128, S], BF16) for p in range(NPAIR)]
        self.s_wot = at("wot", [128, NPAIR, D], BF16)
        self.s_xres = at("xres", [128, NFC, 512], F32)
        self.s_xsq = at("xsq", [128, 2, 512], F32)
        self.s_x1qb = at("x1qb", [128, NFC, 512], BF16)
        self.s_lall = at("lall", [128, 512], F32)
        self.s_oh = at("oh", [128, 512], F32)
        self.s_sig = at("sig", [33, S], BF16)
        self.s_mpair = at("mpair", [128, 32], F32)
        self.s_mprt = at("mprt", [128, 48], F32)
        self.s_emg = at("emg", [128, 64], F32)
        self.s_emgbf = at("emgbf", [128, 32], BF16)
        self.s_r = at("r", [128, 512], F32)
        class1_end = o
        o = base
        self.s_scan = at("scan", [8, NB * S], F32)
        self.s_sel = at("sel", [8, NB * S], BF16)
        self.s_so = at("so", [40, NB * S], BF16)
        self.s_num = at("num", [33, NB * S // 2], F32)
        self.s_zrow = at("zrow", [33, NB * S // 2], F32)
        post_end = o
        self.s_mgbf = at("mgbf", [1, NB * S], BF16)
        cls4_o = o
        self.s_mgt = self._sb("mgt", [33, NB * S // 2], F32,
                              off=base)
        o = cls4_o
        cls4_end = o
        o = base
        self.s_wexp = at("wexp", [128, NFC, D], BF16)
        self.s_x1tb = [at(f"x1tb{i}", [128, NFC, 512], BF16) for i in range(2)]
        self.s_x1mg = at("x1mg", [128, 2, NFC, 512], BF16)
        self.s_xres2 = at("xres2", [128, NFC, 512], F32)
        self.s_x2o = at("x2o", [128, NFC, 512], F32)
        self.s_xsqb = at("xsqb", [128, 2, 512], F32)
        cls5_end = o
        assert cls5_end <= post_end, (cls5_end, post_end)
        self.sbuf_peak = max(class1_end, cls4_end, cls5_end)
        self.sbuf_regions = dict(base=base, class1=class1_end, class3=post_end,
                                 cls4=cls4_end, cls5=cls5_end)
        assert self.sbuf_peak <= 229376, (self.sbuf_peak, self.sbuf_regions)

        self.p_cp = nc.alloc_psum_tensor("cp", [128, 1024], F32)
        self.p_sl = [nc.alloc_psum_tensor(f"sl{i}", [128, 512], F32) for i in range(4)]
        self.p_av = [nc.alloc_psum_tensor(f"av{h}", [128, 512], F32) for h in range(2)]

        rng = self.sb_ranges
        x0lo = rng["x0t"][0]
        x0chunk = NFC * 512 * 2
        name2rng = {
            "wqp0": rng["wqp0"], "wqp1": rng["wqp1"],
            "qd0": rng["qd0"], "qd1": rng["qd1"],
            "kbd0": rng["kbd0"], "kbd1": rng["kbd1"],
            "qbd20": rng["qbd20"], "qbd21": rng["qbd21"],
            "kt": rng["kt"], "vp": rng["vp"], "vt": rng["vt"],
            "et00": rng["et00"], "et01": rng["et01"],
            "et10": rng["et10"], "et11": rng["et11"],
            "wot": rng["wot"], "xres": rng["xres"],
            "xsq": rng["xsq"], "x1qb": rng["x1qb"], "lall": rng["lall"],
            "oh": rng["oh"], "sig": rng["sig"], "mpair": rng["mpair"], "mprt": rng["mprt"],
            "emg": rng["emg"], "emgbf": rng["emgbf"], "r": rng["r"],
            "gateT": rng["gateT"], "scan": rng["scan"],
            "sel": rng["sel"], "so": rng["so"], "num": rng["num"],
            "zrow": rng["zrow"], "mgbf": rng["mgbf"], "mgt": rng["mgt"],
            "wexp": rng["wexp"], "x1tb0": rng["x1tb0"], "x1tb1": rng["x1tb1"],
            "x1mg": rng["x1mg"], "xres2": rng["xres2"], "x2o": rng["x2o"],
            "xsqb": rng["xsqb"], "lrows": rng["lrows"], "ltmp": rng["ltmp"],
            "laux": rng["laux"], "rowtmp": rng["rowtmp"],
            "consts": (16512, rng["gateT"][0]),
        }
        for qb in range(NQB):
            name2rng[f"x0t{qb}"] = (x0lo + qb * x0chunk, x0lo + (qb + 1) * x0chunk)
        self.b = {}
        for n in ["cpb0", "cpb1", "sl0", "sl1", "sl2", "sl3",
                  "av0", "av1", "x1sc", "ccin", "ccout", "agc", "cb"]:
            self.b[n] = Buf(n)
            self.sch.bufs.append(self.b[n])
        for n, r in name2rng.items():
            self.b[n] = Buf(n, r)
            self.sch.bufs.append(self.b[n])
        self.b_a2n = []
        for p in range(NPAIR):
            bb = Buf(f"a2n{p}", rng[f"a2n{p}"])
            self.b_a2n.append(bb)
            self.sch.bufs.append(bb)
        self.sch.link_overlaps()

    def dbg(self, name, src_ap, shape, dtype, reads):
        if name not in self.debug:
            return
        t = self.nc.dram_tensor("dbg_" + name, shape, dtype, kind="ExternalOutput")
        self.dbg_outs[name] = name
        self.sch.op("sp", lambda e, t=t, a=src_ap: e.dma_start(out=t.ap(), in_=a),
                    reads=reads, is_dma=True, tag="dbg_" + name)

    def emit_all(self):
        sch, b = self.sch, self.b
        nc = self.nc

        def ld(sb, dname, cast=False, tag=""):
            src = self.d[dname].ap()
            stream = "gp" if cast else "sp"
            return sch.op(stream, lambda e, o=sb, i=src: e.dma_start(out=o[:], in_=i),
                          writes=[b["consts"]], is_dma=True, tag="ld_" + dname)

        for nm, sb in [("identf", self.s_identf), ("identbf", self.s_identbf),
                       ("ones1f", self.s_ones1f), ("ones1bf", self.s_ones1bf),
                       ("ones128f", self.s_ones128f), ("ones128bf", self.s_ones128bf),
                       ("maskstq", self.s_mtq), ("masksqk2a", self.s_mqk2[0]),
                       ("masksqk2b", self.s_mqk2[1]),
                       ("ghostb", self.s_ghostb), ("wqkvb_pc", self.s_bqkv),
                       ("wob_pc", self.s_bo), ("wgate3", self.s_wg),
                       ("premask", self.s_premask)]:
            ld(sb, nm)
        sch.op("sp", lambda e: e.dma_start(out=self.s_rowtmp[:],
                                           in_=self.d["bv_row"].ap()),
               writes=[b["rowtmp"]], is_dma=True, tag="ld_bv")
        sch.op("dve", lambda e: e.tensor_copy(self.s_bvrowbf[:], self.s_rowtmp[:]),
               reads=[b["rowtmp"]], writes=[b["consts"]], tag="cast_bv")
        sch.op("sp", lambda e: e.dma_start(out=self.s_rowtmp[:],
                                           in_=self.d["bexp_row"].ap()),
               reads=[b["consts"]], writes=[b["rowtmp"]], is_dma=True, tag="ld_be")
        sch.op("dve", lambda e: e.tensor_copy(self.s_berowbf[:], self.s_rowtmp[:]),
               reads=[b["rowtmp"]], writes=[b["consts"]], tag="cast_be")
        sch.op("dve", lambda e: e.memset(self.s_ones33[0:1, :], 1.0),
               writes=[b["consts"]], tag="ones33a")
        sch.op("dve", lambda e: e.memset(self.s_ones33[32:33, :], 1.0),
               writes=[b["consts"]], tag="ones33b")
        sch.op("dve", lambda e: e.memset(self.s_ones33bf[0:1, :], 1.0),
               writes=[b["consts"]], tag="ones33c")
        sch.op("dve", lambda e: e.memset(self.s_ones33bf[32:33, :], 1.0),
               writes=[b["consts"]], tag="ones33d")
        sch.op("dve", lambda e: e.tensor_copy(self.s_wgbf[:], self.s_wg[:]),
               reads=[b["consts"]], writes=[b["consts"]], tag="cast_wg")
        for nm, sb in [("kbd0", self.s_kbd[0]), ("kbd1", self.s_kbd[1]),
                       ("qbd20", self.s_qbd2[0]), ("qbd21", self.s_qbd2[1])]:
            sch.op("dve", lambda e, sb=sb: e.memset(sb[:], 0.0),
                   writes=[b[nm]], tag=f"z_{nm}")

        for bi in range(self.nb):
            self.emit_batch(bi)
        self.emit_routing()
        self.emit_phase_b()

        last_sp = [o for o in sch.order if o.is_dma and o.stream == "sp"][-1]
        last_act = [o for o in sch.order if o.is_dma and o.stream == "act"][-1]
        sch.op("sp", None, deps=[last_sp, last_act], tag="final_wait")
        sch.op("dve", lambda e: e.engine_nop(), deps=[last_act, last_sp],
               tag="final_wait2")

    def _x0_chunk_load(self, bi, qb):
        sch, b = self.sch, self.b
        src = self.d["x0t"].ap()[bi].rearrange(
            "(c p) t -> p c t", p=128)[:, :, qb * 512:(qb + 1) * 512]
        sch.op("gp", lambda e, i=src, qb=qb: e.dma_start(
            out=self.s_x0t[:, qb, :, :], in_=i),
            writes=[b[f"x0t{qb}"]], is_dma=True, tag=f"ld_x0t_b{bi}c{qb}")

    def _wq_load(self, bi, pair):
        sch, b = self.sch, self.b
        wq = self.s_wqp[pair % 2]
        bwq = b[f"wqp{pair % 2}"]
        for blk in range(3):
            off = blk * 1024 + pair * 128
            srcw = bass.AP(self.d["wqkvt"], off,
                           [[3072, 128], [128 * 3072, 8], [1, 128]])
            dst = wq[:, :, blk, :]
            sch.op("gp", lambda e, o=dst, i=srcw: e.dma_start(out=o, in_=i),
                   writes=[bwq], is_dma=True, tag=f"ld_wq_b{bi}p{pair}k{blk}")

    def emit_batch(self, bi):
        sch, b = self.sch, self.b
        if bi == 0:
            for qb in range(NQB):
                self._x0_chunk_load(bi, qb)
            wsrc = self.d["wot"].ap().rearrange("(c p) f -> p c f", p=128)
            sch.op("gp", lambda e, i=wsrc: e.dma_start(out=self.s_wot[:], in_=i),
                   writes=[b["wot"]], is_dma=True, tag=f"ld_wot_b{bi}")

        for pair in range(self.npair):
            self.emit_pair(bi, pair, skip_wload=(bi > 0 and pair < 2))
        if bi + 1 < self.nb:
            self._wq_load(bi + 1, 0)
            self._wq_load(bi + 1, 1)
        self.emit_wo_ln1(bi)
        self.emit_gate_batch(bi)

    def emit_pair(self, bi, pair, skip_wload=False):
        sch, b = self.sch, self.b
        wq = self.s_wqp[pair % 2]
        bwq = b[f"wqp{pair % 2}"]
        if not skip_wload:
            self._wq_load(bi, pair)

        for blk in range(2):
            for qb in range(NQB):
                ps = self.p_sl[qb][:, :]
                def mmgrp(e, ps=ps, wq=wq, blk=blk, qb=qb):
                    r = None
                    for c in range(NFC):
                        r = e.matmul(ps, wq[:, c, blk, :],
                                     self.s_x0t[:, qb, c, :],
                                     start=(c == 0), stop=(c == NFC - 1))
                    return r
                sch.op("pe", mmgrp, reads=[bwq, b[f"x0t{qb}"]], writes=[b[f"sl{qb}"]],
                       tag=f"qkproj_b{bi}p{pair}x{blk}q{qb}")
                if blk == 0:
                    for hl in range(2):
                        r0 = 64 * hl
                        dst = self.s_qd[hl]
                        bias = self.s_bqkv[r0:r0 + 64, pair:pair + 1]
                        psh = self.p_sl[qb].ap()[r0:r0 + 64, :]
                        sch.op("act", lambda e, psh=psh, dst=dst, qb=qb, bias=bias,
                               r0=r0: e.activation(
                                   out=dst[r0:r0 + 64, qb * 512:(qb + 1) * 512],
                                   in_=psh, func=AF.Identity, bias=bias, scale=1.0),
                               reads=[b[f"sl{qb}"], b["consts"]],
                               writes=[b[f"qd{hl}"]],
                               tag=f"qevac_b{bi}p{pair}h{hl}q{qb}")
                else:
                    bias = self.s_bqkv[:, 8 + pair:9 + pair]
                    sch.op("act", lambda e, ps=ps, qb=qb, bias=bias:
                           e.activation(out=self.s_kt[:, qb * 512:(qb + 1) * 512],
                                        in_=ps, func=AF.Identity, bias=bias,
                                        scale=1.0),
                           reads=[b[f"sl{qb}"], b["consts"]], writes=[b["kt"]],
                           tag=f"kevac_b{bi}p{pair}q{qb}")
            def half_view(sb, p0, h):
                return sb.ap()[p0:p0 + 64, :].rearrange(
                    "p (k h c) -> p k h c", h=2, c=64)[:, :, h:h + 1, :]
            if blk == 0:
                def qdma(dst, src, srcbuf, dstbuf, tag):
                    sch.op("sp", lambda e, o=dst, i=src: e.dma_start(out=o, in_=i),
                           reads=[b[srcbuf]], writes=[b[dstbuf]],
                           is_dma=True, tag=tag)
                t = f"_b{bi}p{pair}"
                qdma(self.s_qd[0].ap()[64:128, :], self.s_qd[0].ap()[0:64, :],
                     "qd0", "qd0", "bq_qd0" + t)
                qdma(self.s_qd[1].ap()[0:64, :], self.s_qd[1].ap()[64:128, :],
                     "qd1", "qd1", "bq_qd1" + t)
                qdma(half_view(self.s_qbd2[0], 0, 0), half_view(self.s_qd[0], 0, 0),
                     "qd0", "qbd20", "bq_q20a" + t)
                qdma(half_view(self.s_qbd2[0], 64, 1), half_view(self.s_qd[1], 64, 0),
                     "qd1", "qbd20", "bq_q20b" + t)
                qdma(half_view(self.s_qbd2[1], 0, 0), half_view(self.s_qd[0], 0, 1),
                     "qd0", "qbd21", "bq_q21a" + t)
                qdma(half_view(self.s_qbd2[1], 64, 1), half_view(self.s_qd[1], 64, 1),
                     "qd1", "qbd21", "bq_q21b" + t)
                sch.op("sp", lambda e: e.nop(),
                       reads=[b["qd0"], b["qd1"]],
                       writes=[b["qd0"], b["qd1"], b["qbd20"], b["qbd21"]],
                       flushq="q_sp", tag=f"qflush_b{bi}p{pair}")
            else:
                def kdma(dst, src, dstbuf, tag):
                    sch.op("gp", lambda e, o=dst, i=src: e.dma_start(out=o, in_=i),
                           reads=[b["kt"]], writes=[b[dstbuf]],
                           is_dma=True, tag=tag)
                t = f"_b{bi}p{pair}"
                kdma(half_view(self.s_kbd[0], 0, 0), half_view(self.s_kt, 0, 0),
                     "kbd0", "bk_k0a" + t)
                kdma(half_view(self.s_kbd[0], 64, 1), half_view(self.s_kt, 0, 1),
                     "kbd0", "bk_k0b" + t)
                kdma(half_view(self.s_kbd[1], 0, 0), half_view(self.s_kt, 64, 0),
                     "kbd1", "bk_k1a" + t)
                kdma(half_view(self.s_kbd[1], 64, 1), half_view(self.s_kt, 64, 1),
                     "kbd1", "bk_k1b" + t)
                sch.op("gp", lambda e: e.engine_nop(),
                       reads=[b["kt"]],
                       writes=[b["kbd0"], b["kbd1"]],
                       flushq="q_gp", tag=f"kflush_b{bi}p{pair}")

        vones_ap = bass.AP(self.s_vp, 64, [[NKT * 130, 128], [130, NKT], [65, 2]])
        sch.op("dve", lambda e, a=vones_ap: e.memset(a, 1.0),
               writes=[b["vp"]], tag=f"vones_b{bi}p{pair}")
        for qb in range(NQB):
            ps = self.p_sl[qb][:, :]
            def vmm(e, ps=ps, wq=wq, qb=qb):
                r = None
                for c in range(NFC):
                    r = e.matmul(ps, wq[:, c, 2, :],
                                 self.s_x0t[:, qb, c, :],
                                 start=(c == 0), stop=(c == NFC - 1))
                return r
            sch.op("pe", vmm, reads=[bwq, b[f"x0t{qb}"]], writes=[b[f"sl{qb}"]],
                   tag=f"vproj_b{bi}p{pair}q{qb}")
            bias = self.s_bqkv[:, 16 + pair:17 + pair]
            sch.op("act", lambda e, ps=ps, qb=qb, bias=bias:
                   e.activation(out=self.s_vt[:, qb * 512:(qb + 1) * 512], in_=ps,
                                func=AF.Identity, bias=bias, scale=1.0),
                   reads=[b[f"sl{qb}"], b["consts"]], writes=[b["vt"]],
                   tag=f"vevac_b{bi}p{pair}q{qb}")
        for tt in range(NKT):
            psb = self.p_sl[tt % 4].ap()[:, 0:64].bitcast(BF16)
            sch.op("pe", lambda e, psb=psb, tt=tt: e.transpose(
                psb, self.s_vt[:, tt * 128:(tt + 1) * 128], self.s_identbf[:]),
                reads=[b["vt"], b["consts"]], writes=[b[f"sl{tt % 4}"]],
                tag=f"vtr_b{bi}p{pair}t{tt}")
            dst = bass.AP(self.s_vp, tt * 130, [[NKT * 130, 128], [65, 2], [1, 64]])
            sch.op("dve", lambda e, psb=psb, dst=dst:
                   e.tensor_copy(dst, psb.rearrange("p (h d) -> p h d", h=2)),
                   reads=[b[f"sl{tt % 4}"]], writes=[b["vp"]],
                   tag=f"vevacT_b{bi}p{pair}t{tt}")

        for qb in range(NQB):
            nkt = 4 * qb + 4
            cs_items = self._cs_items(bi, pair, qb)
            ncs, done = len(cs_items), 0
            for kt in range(nkt + 1):
                if kt == 2 and qb > 0:
                    self._qb_denom(bi, pair, qb - 1)
                for hl in range(2):
                    if kt < nkt:
                        sb_i = hl * 2 + kt % 2
                        ps, pb = self.p_sl[sb_i], b[f"sl{sb_i}"]
                        diag = kt - 4 * qb
                        def stgrp(e, ps=ps, hl=hl, kt=kt, qb=qb, diag=diag):
                            r = e.matmul(ps[:, :],
                                         self.s_kbd[hl][:, kt * 128:(kt + 1) * 128],
                                         self.s_qd[hl][:, qb * 512:(qb + 1) * 512],
                                         start=True, stop=(diag < 0))
                            if diag >= 0:
                                off = diag * (diag + 1) // 2 * 128
                                w = (diag + 1) * 128
                                r = e.matmul(ps[:, 0:w], self.s_identbf[:],
                                             self.s_mtq[:, off:off + w], start=False,
                                             stop=True)
                            return r
                        sch.op("pe", stgrp,
                               reads=[b[f"qd{hl}"], b[f"kbd{hl}"], b["consts"]],
                               writes=[pb], tag=f"st_b{bi}p{pair}q{qb}k{kt}h{hl}")
                    if kt > 0:
                        ktp = kt - 1
                        et = self.s_et[hl][ktp % 2]
                        eb = b[f"et{hl}{ktp % 2}"]
                        pa, pab = self.p_av[hl], b[f"av{hl}"]
                        vsl = self.s_vp[:, ktp, hl * 65:hl * 65 + 65]
                        sch.op("pe", lambda e, pa=pa, vsl=vsl, et=et, ktp=ktp, nkt=nkt:
                               e.matmul(pa[0:65, :], vsl, et[:],
                                        start=(ktp == 0), stop=(ktp == nkt - 1)),
                               reads=[b["vp"], eb], writes=[pab],
                               tag=f"av_b{bi}p{pair}q{qb}k{ktp}h{hl}")
                    if kt < nkt:
                        sb_i = hl * 2 + kt % 2
                        ps, pb = self.p_sl[sb_i], b[f"sl{sb_i}"]
                        et = self.s_et[hl][kt % 2]
                        eb = b[f"et{hl}{kt % 2}"]
                        sch.op("act", lambda e, ps=ps, et=et:
                               e.activation(out=et[:], in_=ps[:, :], func=AF.Exp,
                                            scale=0.125),
                               reads=[pb], writes=[eb],
                               tag=f"exp_b{bi}p{pair}q{qb}k{kt}h{hl}")
                tgt = (kt + 1) * ncs // (nkt + 1)
                while done < tgt:
                    cs_items[done]()
                    done += 1

            self._av_evac(bi, pair, qb)
        for q in range(NQB - 1):
            self._one_denom(bi, pair, q)
        self._qb_denom(bi, pair, NQB - 1)
        self._one_denom(bi, pair, NQB - 1)
        self.dbg(f"a2n_b{bi}p{pair}", self.s_a2n[pair][:], [128, S], BF16,
                 [self.b_a2n[pair]])
        self.dbg(f"sig_b{bi}p{pair}", self.s_sig[:], [33, S], BF16, [b["sig"]])
        self.dbg(f"mpair_b{bi}p{pair}", self.s_mpair[:], [128, 32], F32,
                 [b["mpair"]])
        self.dbg(f"qt_b{bi}p{pair}", self.s_qd[0][:], [128, S], BF16, [b["qd0"]])
        self.dbg(f"vp_b{bi}p{pair}", self.s_vp[:], [128, NKT, 130], BF16,
                 [b["vp"]])

    def _cs_items(self, bi, pair, qb):
        sch, b = self.sch, self.b
        items = []
        idx = [0]
        for i in range(4):
            qt = 4 * qb + i
            for qh in range(2):
                jj = qt % 4
                ext = (qt + 1) * 128
                nr = (ext + 511) // 512
                for rd in range(nr):
                    lo = rd * 512
                    w = min(ext - lo, 512)
                    dlast = (rd == nr - 1)
                    def item(qt=qt, qh=qh, rd=rd, lo=lo, w=w, dlast=dlast, jj=jj):
                        cb = idx[0] % 2
                        po = cb * 512
                        idx[0] += 1
                        def cpg(e, lo=lo, w=w, qh=qh, qt=qt, po=po,
                                dlast=dlast, jj=jj):
                            if not dlast:
                                return e.matmul(
                                    self.p_cp[:, po:po + w],
                                    self.s_qbd2[qh][:, qt * 128:(qt + 1) * 128],
                                    self.s_kt[:, lo:lo + w],
                                    start=True, stop=True)
                            off = jj * (jj + 1) // 2 * 128
                            e.matmul(self.p_cp[:, po:po + w],
                                     self.s_qbd2[qh][:, qt * 128:(qt + 1) * 128],
                                     self.s_kt[:, lo:lo + w],
                                     start=True, stop=False)
                            return e.matmul(self.p_cp[:, po:po + w],
                                            self.s_identbf[:],
                                            self.s_mqk2[qh][:, off:off + w],
                                            start=False, stop=True)
                        cbank = b[f"cpb{cb}"]
                        sch.op("pe", cpg,
                               reads=[b[f"qbd2{qh}"], b["kt"], b["consts"]],
                               writes=[cbank],
                               tag=f"cs_b{bi}p{pair}t{qt}h{qh}r{rd}")
                        mcol = self.s_mpair[:, qt * 2 + qh:qt * 2 + qh + 1]
                        if rd == 0:
                            sch.op("dve", lambda e, mcol=mcol, w=w, po=po:
                                   e.reduce_max(mcol, self.p_cp[:, po:po + w],
                                                axis=AX.X),
                                   reads=[cbank], writes=[b["mpair"]],
                                   tag=f"max_b{bi}p{pair}t{qt}h{qh}")
                        else:
                            nr1 = (qt // 4 + 1) - 1
                            pc = (0, 0, 8, 24)[qt // 4] \
                                + ((qt % 4) * 2 + qh) * nr1 + (rd - 1)
                            mp = self.s_mprt[:, pc:pc + 1]
                            sch.op("dve", lambda e, mp=mp, w=w, po=po:
                                   e.reduce_max(mp, self.p_cp[:, po:po + w],
                                                axis=AX.X),
                                   reads=[cbank], writes=[b["mprt"]],
                                   tag=f"maxp_b{bi}p{pair}t{qt}h{qh}r{rd}")
                    items.append(item)
        return items

    def _av_evac(self, bi, pair, qb):
        sch, b = self.sch, self.b
        for hl in range(2):
            pa, pab = self.p_av[hl], b[f"av{hl}"]
            srow = 32 * hl
            so = sch.op("act", lambda e, pa=pa, srow=srow, qb=qb:
                        e.copy(out=self.s_sig[srow:srow + 1, qb * 512:(qb + 1) * 512],
                               in_=pa[64:65, :]),
                        reads=[pab], writes=[b["sig"]],
                        tag=f"sigevac_b{bi}p{pair}q{qb}h{hl}")
            sch.op("dve", lambda e, pa=pa, hl=hl, qb=qb, pair=pair:
                   e.tensor_copy(self.s_a2n[pair][64 * hl:64 * hl + 64,
                                                  qb * 512:(qb + 1) * 512],
                                 pa[0:64, :]),
                   reads=[pab], writes=[self.b_a2n[pair]], deps=[so],
                   tag=f"aevac_b{bi}p{pair}q{qb}h{hl}")

    def _qb_denom(self, bi, pair, qb):
        sch, b = self.sch, self.b
        nr1 = qb
        pb0 = (0, 0, 8, 24)[qb]
        for k in range(1, nr1):
            sch.op("dve", lambda e, k=k, nr1=nr1, pb0=pb0: e.tensor_tensor(
                out=bass.AP(self.s_mprt, pb0 + k, [[48, 128], [nr1, 8]]),
                in0=bass.AP(self.s_mprt, pb0 + k - 1, [[48, 128], [nr1, 8]]),
                in1=bass.AP(self.s_mprt, pb0 + k, [[48, 128], [nr1, 8]]),
                op=ALU.max),
                reads=[b["mprt"]], writes=[b["mprt"]],
                tag=f"mcmb_b{bi}p{pair}q{qb}k{k}")
        if nr1 > 0:
            sch.op("dve", lambda e, nr1=nr1, qb=qb, pb0=pb0: e.tensor_tensor(
                out=self.s_mpair[:, 8 * qb:8 * qb + 8],
                in0=bass.AP(self.s_mprt, pb0 + nr1 - 1, [[48, 128], [nr1, 8]]),
                in1=self.s_mpair[:, 8 * qb:8 * qb + 8],
                op=ALU.max),
                reads=[b["mprt"]], writes=[b["mpair"]],
                tag=f"mcmbf_b{bi}p{pair}q{qb}")
        sch.op("act", lambda e, qb=qb: e.activation(
            out=self.s_emg[:, 8 * qb:8 * qb + 8],
            in_=self.s_mpair[:, 8 * qb:8 * qb + 8], func=AF.Exp, scale=0.125),
            reads=[b["mpair"]], writes=[b["emg"]], tag=f"em_b{bi}p{pair}q{qb}")
        sch.op("dve", lambda e, qb=qb: e.tensor_scalar(
            out=self.s_emgbf[:, 8 * qb:8 * qb + 8],
            in0=self.s_emg[:, 8 * qb:8 * qb + 8],
            scalar1=self.s_ghostb[:, pair:pair + 1], scalar2=None, op0=ALU.mult),
            reads=[b["emg"], b["consts"]], writes=[b["emgbf"]],
            tag=f"emg_b{bi}p{pair}q{qb}")

    def _pair_denoms(self, bi, pair):
        sch, b = self.sch, self.b
        for qb in range(NQB):
            self._one_denom(bi, pair, qb)

    def _one_denom(self, bi, pair, qb):
        sch, b = self.sch, self.b
        for hl in range(2):
            srow = 32 * hl
            r0 = 64 * hl
            ps = self.p_cp[:, hl * 512:(hl + 1) * 512]
            pb = b[f"cpb{hl}"]
            def dmm(e, ps=ps, hl=hl, qb=qb, srow=srow, r0=r0):
                r = e.matmul(ps[r0:r0 + 64, 0:512],
                             self.s_ones33bf[srow:srow + 1, 0:64],
                             self.s_sig[srow:srow + 1, qb * 512:(qb + 1) * 512],
                             start=True, stop=False, tile_position=(srow, r0))
                for i in range(4):
                    qt = 4 * qb + i
                    for qh in range(2):
                        col = self.s_emgbf[:, qt * 2 + qh:qt * 2 + qh + 1]
                        r = e.matmul(
                            ps[r0:r0 + 64,
                               i * 128 + qh * 64:i * 128 + qh * 64 + 64],
                            col.broadcast_to([128, 64]),
                            self.s_identbf[:, 64 * hl:64 * hl + 64],
                            start=False, stop=(i == 3 and qh == 1),
                            tile_position=(0, r0))
                return r
            sch.op("pe", dmm, reads=[b["emgbf"], b["sig"], b["consts"]], writes=[pb],
                   tag=f"denom_b{bi}p{pair}h{hl}q{qb}")
        for hl in range(2):
            r0 = 64 * hl
            ps = self.p_cp[:, hl * 512:(hl + 1) * 512]
            pb = b[f"cpb{hl}"]
            sch.op("dve", lambda e, ps=ps, r0=r0: e.reciprocal(
                self.s_r[r0:r0 + 64, :], ps[r0:r0 + 64, :]),
                reads=[pb], writes=[b["r"]], tag=f"recip_b{bi}p{pair}h{hl}q{qb}")
        for hl in range(2):
            r0 = 64 * hl
            sch.op("dve", lambda e, hl=hl, qb=qb, pair=pair, r0=r0: e.tensor_tensor(
                out=self.s_a2n[pair][r0:r0 + 64, qb * 512:(qb + 1) * 512],
                in0=self.s_a2n[pair][r0:r0 + 64, qb * 512:(qb + 1) * 512],
                in1=self.s_r[r0:r0 + 64, :], op=ALU.mult),
                reads=[b["r"], self.b_a2n[pair]], writes=[self.b_a2n[pair]],
                tag=f"norm_b{bi}p{pair}h{hl}q{qb}")

    def emit_wo_ln1(self, bi):
        sch, b = self.sch, self.b
        banks = self.p_av + self.p_sl[0:2]
        bnames = ("av0", "av1", "sl0", "sl1")

        def womm(qb, fc):
            ps, pb = banks[fc % 4], b[bnames[fc % 4]]
            np_ = self.npair
            def mm(e, ps=ps, fc=fc, qb=qb, np_=np_):
                r = None
                for p in range(np_):
                    r = e.matmul(ps[:, 0:512],
                                 self.s_wot[:, p, fc * 128:(fc + 1) * 128],
                                 self.s_a2n[p][:, qb * 512:(qb + 1) * 512],
                                 start=(p == 0), stop=(p == NPAIR - 1))
                return r
            sch.op("pe", mm, reads=[b["wot"]] + list(self.b_a2n[:self.npair]),
                   writes=[pb], tag=f"wo_b{bi}q{qb}f{fc}")

        def xres(qb, fc):
            ps, pb = banks[fc % 4], b[bnames[fc % 4]]
            sch.op("dve", lambda e, ps=ps, fc=fc, qb=qb: e.scalar_tensor_tensor(
                out=self.s_xres[:, fc, :], in0=ps[:, 0:512],
                scalar=self.s_bo[:, fc:fc + 1],
                in1=self.s_x0t[:, qb, fc, :],
                op0=ALU.add, op1=ALU.add),
                reads=[pb, b[f"x0t{qb}"], b["consts"]], writes=[b["xres"]],
                tag=f"xres_b{bi}q{qb}f{fc}")
            sch.op("act", lambda e, fc=fc: e.activation(
                out=self.s_xsq[:, fc % 2, :], in_=self.s_xres[:, fc, :],
                func=AF.Square),
                reads=[b["xres"]], writes=[b["xsq"]], tag=f"sq_b{bi}q{qb}f{fc}")

        def stats(qb, fc):
            o1 = (qb % 2) * 512
            b1 = b[f"cpb{qb % 2}"]
            sch.op("pe", lambda e, fc=fc, o1=o1: e.matmul(
                self.p_cp[0:1, o1:o1 + 512], self.s_ones128f[:],
                self.s_xres[:, fc, :],
                start=(fc == 0), stop=(fc == NFC - 1)),
                reads=[b["xres"], b["consts"]], writes=[b1],
                tag=f"sum_b{bi}q{qb}f{fc}")
            sch.op("pe", lambda e, fc=fc, o1=o1: e.matmul(
                self.p_cp[64:65, o1:o1 + 512], self.s_ones128f[:],
                self.s_xsq[:, fc % 2, :],
                start=(fc == 0), stop=(fc == NFC - 1)),
                reads=[b["xsq"], b["consts"]], writes=[b1],
                tag=f"sumsq_b{bi}q{qb}f{fc}")

        def rowchain(qb):
            o1 = (qb % 2) * 512
            b1 = b[f"cpb{qb % 2}"]
            sch.op("act", lambda e, o1=o1: e.activation(
                out=self.s_lrows[:], in_=self.p_cp[0:1, o1:o1 + 512],
                func=AF.Identity, scale=1.0 / D),
                reads=[b1], writes=[b["lrows"]], tag=f"mean_b{bi}q{qb}")
            sch.op("act", lambda e, o1=o1: e.activation(
                out=self.s_ltmp[:], in_=self.p_cp[64:65, o1:o1 + 512],
                func=AF.Identity, scale=1.0 / D),
                reads=[b1], writes=[b["ltmp"]], tag=f"msq_b{bi}q{qb}")
            sch.op("act", lambda e: e.activation(
                out=self.s_laux[:], in_=self.s_lrows[:], func=AF.Square),
                reads=[b["lrows"]], writes=[b["laux"]], tag=f"m2_b{bi}q{qb}")

        def rowchain2(qb):
            sch.op("dve", lambda e: e.scalar_tensor_tensor(
                out=self.s_lrows[:], in0=self.s_ltmp[:], scalar=EPS,
                in1=self.s_laux[:], op0=ALU.add, op1=ALU.subtract),
                reads=[b["ltmp"], b["laux"]], writes=[b["lrows"]],
                tag=f"var_b{bi}q{qb}")
            sch.op("act", lambda e: e.activation(
                out=self.s_ltmp[:], in_=self.s_lrows[:], func=AF.Sqrt),
                reads=[b["lrows"]], writes=[b["ltmp"]], tag=f"sqrt_b{bi}q{qb}")
            sch.op("dve", lambda e: e.reciprocal(self.s_laux[:], self.s_ltmp[:]),
                   reads=[b["ltmp"]], writes=[b["laux"]], tag=f"rstd_b{bi}q{qb}")

        def bmean(qb):
            sch.op("pe", lambda e: e.matmul(
                self.p_sl[2][:, :], self.s_ones1f[:],
                self.s_lrows[:], start=True, stop=True),
                reads=[b["lrows"], b["consts"]], writes=[b["sl2"]],
                tag=f"bmean_b{bi}q{qb}")

        def brstd(qb):
            sch.op("pe", lambda e: e.matmul(
                self.p_sl[3][:, :], self.s_ones1f[:],
                self.s_laux[:], start=True, stop=True),
                reads=[b["laux"], b["consts"]], writes=[b["sl3"]],
                tag=f"brstd_b{bi}q{qb}")

        subscr = [self.s_r, self.s_oh]
        subbuf = [b["r"], b["oh"]]

        def sub(qb, fc):
            sch.op("dve", lambda e, fc=fc: e.scalar_tensor_tensor(
                out=subscr[fc % 2][:], in0=self.p_sl[2][:, :],
                scalar=-1.0, in1=self.s_xres[:, fc, :],
                op0=ALU.mult, op1=ALU.add),
                reads=[b["xres"], b["sl2"]], writes=[subbuf[fc % 2]],
                tag=f"sub_b{bi}q{qb}f{fc}")

        def x1(qb, fc):
            sch.op("dve", lambda e, fc=fc: e.tensor_tensor(
                out=self.s_x1qb[:, fc, :], in0=self.p_sl[3][:, :],
                in1=subscr[fc % 2][:], op=ALU.mult),
                reads=[subbuf[fc % 2], b["sl3"]], writes=[b["x1qb"]],
                tag=f"x1_b{bi}q{qb}f{fc}")
            dst = bass.AP(self.d_x1sc, fc * 128 * (NB * S) + bi * S + qb * 512,
                          [[NB * S, 128], [1, 512]])
            if "stx1" not in self.skip:
                sch.op("sp", lambda e, fc=fc, dst=dst:
                       e.dma_start(out=dst, in_=self.s_x1qb[:, fc, :]),
                       reads=[b["x1qb"]], writes=[b["x1sc"]], is_dma=True,
                       tag=f"st_x1_b{bi}q{qb}f{fc}")

        def gmm(qb):
            o1 = (qb % 2) * 512
            if "gmm" in self.skip:
                return
            def g(e, qb=qb, o1=o1):
                r = None
                for tt in range(4):
                    for fc in range(NFC):
                        r = e.matmul(self.p_cp[:, o1 + tt * 8:o1 + tt * 8 + 8],
                                     self.s_x1qb[:, fc, tt * 128:(tt + 1) * 128],
                                     self.s_wgbf[:, fc, :],
                                     start=(fc == 0), stop=(fc == NFC - 1))
                return r
            sch.op("pe", g, reads=[b["x1qb"], b["consts"]],
                   writes=[b[f"cpb{qb % 2}"]], tag=f"gate_b{bi}q{qb}")
            sch.op("dve", lambda e, qb=qb, o1=o1: e.tensor_copy(
                self.s_lall[:, qb * 32:qb * 32 + 32], self.p_cp[:, o1:o1 + 32]),
                reads=[b[f"cpb{qb % 2}"]], writes=[b["lall"]],
                tag=f"levac_b{bi}q{qb}")

        def tail_head(qp):
            rowchain(qp)

        def tail_mid(qp, fc):
            sub(qp, fc)
            if fc > 0:
                x1(qp, fc - 1)

        def tail_end(qp):
            x1(qp, NFC - 1)
            gmm(qp)

        for qb in range(NQB):
            if qb > 0:
                tail_head(qb - 1)
            womm(qb, 0)
            if qb > 0:
                bmean(qb - 1)
                rowchain2(qb - 1)
            womm(qb, 1)
            if qb > 0:
                brstd(qb - 1)
            for fc in range(NFC):
                if qb > 0:
                    tail_mid(qb - 1, fc)
                xres(qb, fc)
                if fc + 2 < NFC:
                    womm(qb, fc + 2)
                if fc > 0:
                    stats(qb, fc - 1)
            if qb > 0:
                tail_end(qb - 1)
            stats(qb, NFC - 1)
            if bi + 1 < self.nb:
                self._x0_chunk_load(bi + 1, qb)
        tail_head(NQB - 1)
        bmean(NQB - 1)
        rowchain2(NQB - 1)
        brstd(NQB - 1)
        for fc in range(NFC):
            tail_mid(NQB - 1, fc)
        tail_end(NQB - 1)
        sch.op("sp", lambda e: e.nop(), reads=[b["x1qb"]], writes=[b["x1qb"]],
               flushq="q_sp", tag=f"x1flush_b{bi}")
        self.dbg(f"x1qb_b{bi}", self.s_x1qb[:], [128, NFC, 512], BF16, [b["x1qb"]])
        self.dbg(f"xres_b{bi}", self.s_xres[:], [128, NFC, 512], F32, [b["xres"]])
        self.dbg(f"lall_b{bi}", self.s_lall[:], [128, 512], F32, [b["lall"]])

    def emit_gate_batch(self, bi):
        sch, b = self.sch, self.b
        l3 = self.s_lall.ap().rearrange("p (t e) -> p t e", e=8)
        sch.op("dve", lambda e, l3=l3: e.reduce_max(self.s_xsq[:, 0, 0:64], l3, axis=AX.X),
               reads=[b["lall"]], writes=[b["xsq"]], tag=f"gmax_b{bi}")
        mx3 = self.s_xsq.ap()[:, 0, 0:64].rearrange("p (t o) -> p t o", o=1)
        sch.op("dve", lambda e, l3=l3, mx3=mx3: e.tensor_tensor(
            out=self.s_oh.ap().rearrange("p (t e) -> p t e", e=8), in0=l3,
            in1=mx3.broadcast_to([128, 64, 8]), op=ALU.is_equal),
            reads=[b["lall"], b["xsq"]], writes=[b["oh"]], tag=f"oneh_b{bi}")
        for g in range(4):
            def trg(e, g=g, src=self.s_oh):
                r = None
                for i in range(4):
                    tt = 4 * g + i
                    r = e.transpose(self.p_cp[0:8, i * 128:(i + 1) * 128],
                                    src[:, tt * 8:tt * 8 + 8], self.s_identf[:])
                return r
            sch.op("pe", trg, reads=[b["oh"], b["consts"]], writes=[b["cpb0"]],
                   tag=f"ohT_b{bi}g{g}")
            sch.op("dve", lambda e, g=g: e.tensor_copy(
                self.s_gateT[0:8, bi * S + g * 512:bi * S + (g + 1) * 512],
                self.p_cp[0:8, 0:512]),
                reads=[b["cpb0"]], writes=[b["gateT"]], tag=f"ohTe_b{bi}g{g}")
            def trl(e, g=g):
                r = None
                for i in range(4):
                    tt = 4 * g + i
                    r = e.transpose(self.p_cp[0:8, 512 + i * 128:640 + i * 128],
                                    self.s_lall[:, tt * 8:tt * 8 + 8], self.s_identf[:])
                return r
            sch.op("pe", trl, reads=[b["lall"], b["consts"]], writes=[b["cpb1"]],
                   tag=f"lT_b{bi}g{g}")
            sch.op("act", lambda e, g=g: e.activation(
                out=self.s_gateT[32:40, bi * S + g * 512:bi * S + (g + 1) * 512],
                in_=self.p_cp[0:8, 512:1024], func=AF.Exp),
                reads=[b["cpb1"]], writes=[b["gateT"]], tag=f"exT_b{bi}g{g}")

    def emit_routing(self):
        sch, b = self.sch, self.b
        T = self.nb * S
        sch.op("dve", lambda e: e.tensor_tensor_scan(
            out=self.s_scan[:, 0:T], data0=self.s_gateT[0:8, 0:T],
            data1=self.s_gateT[0:8, 0:T], initial=0.0, op0=ALU.add, op1=ALU.bypass),
            reads=[b["gateT"]], writes=[b["scan"]], tag="scan")
        sch.op("dve", lambda e: e.drain(), tag="scan_drain")
        if self.use_cc:
            sch.op("gp", lambda e: e.dma_start(out=self.d_ccin.ap(),
                                               in_=self.s_scan[0:8, T - 1:T]),
                   reads=[b["scan"]], writes=[b["ccin"]], is_dma=True, tag="ccin")
            cco = sch.op("gp", lambda e: e.collective_compute(
                "AllGather", ALU.bypass, replica_groups=self.cc_rg,
                ins=[self.d_ccin.ap().rearrange("a b -> b a").opt()],
                outs=[self.d_ccout.ap().opt()]),
                reads=[b["ccin"]], writes=[b["ccout"]], is_dma=True, tag="allgather")
            cco.delta = 1
            sch.op("gp", lambda e: e.dma_start(out=self.s_agc[:], in_=self.d_ccout.ap()),
                   reads=[b["ccout"]], writes=[b["agc"]], is_dma=True, tag="ld_agc")
            sch.op("pe", lambda e: e.matmul(self.p_sl[0][0:8, 0:1], self.s_agc[:],
                                            self.s_premask[:], start=True, stop=True),
                   reads=[b["agc"], b["consts"]], writes=[b["sl0"]], tag="cbmm")
            sch.op("dve", lambda e: e.tensor_copy(self.s_cb[:], self.p_sl[0][0:8, 0:1]),
                   reads=[b["sl0"]], writes=[b["cb"]], tag="cbevac")
        else:
            sch.op("dve", lambda e: e.memset(self.s_cb[:], 0.0),
                   writes=[b["cb"]], tag="cbzero")
        sch.op("dve", lambda e: e.tensor_scalar(
            out=self.s_sel[:, 0:T], in0=self.s_scan[:, 0:T], scalar1=self.s_cb[:],
            scalar2=float(CAP + 1), op0=ALU.add, op1=ALU.is_lt),
            reads=[b["scan"], b["cb"]], writes=[b["sel"]], tag="sel")
        sch.op("dve", lambda e: e.tensor_tensor(
            out=self.s_so[32:40, 0:T], in0=self.s_gateT[0:8, 0:T],
            in1=self.s_sel[:, 0:T],
            op=ALU.mult), reads=[b["gateT"], b["sel"]], writes=[b["so"]], tag="so")
        sch.op("dve", lambda e: e.tensor_tensor(
            out=self.s_gateT[0:8, 0:T], in0=self.s_so[32:40, 0:T],
            in1=self.s_gateT[32:40, 0:T],
            op=ALU.mult), reads=[b["so"], b["gateT"]], writes=[b["gateT"]], tag="soe")
        nhalf = T // 1024
        for tb in range(T // 512):
            row, col = (0, tb * 512) if tb < nhalf else (32, (tb - nhalf) * 512)
            ps = self.p_sl[tb % 2]
            pb = b[f"sl{tb % 2}"]
            sch.op("pe", lambda e, ps=ps, tb=tb: e.matmul(
                ps[0:1, 0:512], self.s_ones128bf[0:8, :],
                self.s_gateT[0:8, tb * 512:(tb + 1) * 512], start=True, stop=True),
                reads=[b["gateT"], b["consts"]], writes=[pb], tag=f"nummm{tb}")
            sch.op("dve", lambda e, ps=ps, row=row, col=col: e.tensor_copy(
                self.s_num[row:row + 1, col:col + 512], ps[0:1, 0:512]),
                reads=[pb], writes=[b["num"]], tag=f"nume{tb}")
            sch.op("pe", lambda e, ps=ps, tb=tb: e.matmul(
                ps[0:1, 0:512], self.s_ones128bf[32:40, :],
                self.s_gateT[32:40, tb * 512:(tb + 1) * 512], start=True, stop=True),
                reads=[b["gateT"], b["consts"]], writes=[pb], tag=f"zmm{tb}")
            sch.op("dve", lambda e, ps=ps, row=row, col=col: e.tensor_copy(
                self.s_zrow[row:row + 1, col:col + 512], ps[0:1, 0:512]),
                reads=[pb], writes=[b["zrow"]], tag=f"ze{tb}")
        half = T // 2
        for hf, row in ((0, 0), (1, 32)):
            sch.op("dve", lambda e, row=row, half=half: e.reciprocal(
                self.s_mgt[row:row + 1, 0:half], self.s_zrow[row:row + 1, 0:half]),
                reads=[b["zrow"]], writes=[b["mgt"], b["scan"]], tag=f"zrec{row}")
            sch.op("dve", lambda e, row=row, hf=hf, half=half: e.tensor_tensor(
                out=self.s_mgbf[0:1, hf * half:(hf + 1) * half],
                in0=self.s_num[row:row + 1, 0:half],
                in1=self.s_mgt[row:row + 1, 0:half], op=ALU.mult),
                reads=[b["num"], b["mgt"]], writes=[b["mgbf"]], tag=f"mg{row}")
        self.dbg("mg", self.s_mgbf[:], [1, T], BF16, [b["mgbf"]])
        self.dbg("scan", self.s_scan[:], [8, T], F32, [b["scan"]])

    def fence(self):
        sch = self.sch
        last = {}
        for o in sch.order:
            last[o.stream] = o
        ops = []
        for st in ("pe", "dve", "act", "gp", "sp"):
            deps = [v for k, v in last.items() if k != st]
            fn = (lambda e: e.engine_nop()) if st in ("dve", "gp") else None
            ops.append(sch.op(st, fn, deps=deps, tag=f"fence_{st}",
                              flushq="q_sp" if st == "sp" else None))

    def emit_phase_b(self):
        sch, b = self.sch, self.b
        self.fence()
        wsrc = self.d["wexp"].ap().rearrange("(c p) f -> p c f", p=128)
        sch.op("gp", lambda e, i=wsrc: e.dma_start(out=self.s_wexp[:], in_=i),
               writes=[b["wexp"]], is_dma=True, tag="ld_wexp")
        T = self.nb * S
        banks = [self.p_av[0], self.p_av[1], self.p_sl[0]]
        bnames = ("av0", "av1", "sl0")

        def loads(tb):
            x1, bx1 = self.s_x1tb[tb % 2], b[f"x1tb{tb % 2}"]
            for fc in range(NFC):
                src = bass.AP(self.d_x1sc, fc * 128 * T + tb * 512,
                              [[T, 128], [1, 512]])
                sch.op("sp", lambda e, src=src, x1=x1, fc=fc:
                       e.dma_start(out=x1[:, fc, :], in_=src),
                       reads=[b["x1sc"]], writes=[bx1], is_dma=True,
                       tag=f"ld_x1_{tb}f{fc}")

        def mgb(tb):
            col = tb * 512
            sch.op("pe", lambda e, col=col: e.matmul(
                self.p_sl[3][:, :], self.s_ones1bf[:],
                self.s_mgbf[0:1, col:col + 512], start=True, stop=True),
                reads=[b["mgbf"], b["consts"]], writes=[b["sl3"]], tag=f"mgb{tb}")
            x1, bx1 = self.s_x1tb[tb % 2], b[f"x1tb{tb % 2}"]
            x13 = x1.ap().rearrange("p c t -> p c t")
            mg3 = self.p_sl[3].ap()[:, :].rearrange("p (o t) -> p o t", o=1)
            sch.op("dve", lambda e, x13=x13, mg3=mg3, tb=tb: e.tensor_tensor(
                out=self.s_x1mg.ap()[:, tb % 2], in0=mg3.broadcast_to([128, NFC, 512]),
                in1=x13, op=ALU.mult),
                reads=[bx1, b["sl3"]], writes=[b["x1mg"]], tag=f"x1mg{tb}")

        def emm(tb, fo):
            ps, pb = banks[fo % 3], b[bnames[fo % 3]]
            col = tb * 512
            def mm(e, ps=ps, fo=fo, col=col, tb=tb):
                for c in range(NFC):
                    e.matmul(ps[:, 0:512],
                             self.s_wexp[:, c, fo * 128:(fo + 1) * 128],
                             self.s_x1mg[:, tb % 2, c, :], start=(c == 0),
                             stop=False)
                return e.matmul(ps[:, 0:512],
                                self.s_berowbf[:, fo * 128:(fo + 1) * 128],
                                self.s_mgbf[0:1, col:col + 512],
                                start=False, stop=True)
            sch.op("pe", mm, reads=[b["wexp"], b["x1mg"], b["mgbf"], b["consts"]],
                   writes=[pb], tag=f"exp_mm{tb}f{fo}")

        def xres2(tb, fo):
            ps, pb = banks[fo % 3], b[bnames[fo % 3]]
            x1, bx1 = self.s_x1tb[tb % 2], b[f"x1tb{tb % 2}"]
            sch.op("dve", lambda e, ps=ps, fo=fo, x1=x1: e.tensor_tensor(
                out=self.s_xres2[:, fo, :], in0=ps[:, 0:512], in1=x1[:, fo, :],
                op=ALU.add),
                reads=[pb, bx1], writes=[b["xres2"]], tag=f"xres2_{tb}f{fo}")
            sch.op("act", lambda e, fo=fo: e.activation(
                out=self.s_xsqb[:, fo % 2, :], in_=self.s_xres2[:, fo, :],
                func=AF.Square),
                reads=[b["xres2"]], writes=[b["xsqb"]], tag=f"sq2_{tb}f{fo}")

        def stats2(tb, fo):
            o1 = (tb % 2) * 512
            b1 = b[f"cpb{tb % 2}"]
            sch.op("pe", lambda e, fo=fo, o1=o1: e.matmul(
                self.p_cp[0:1, o1:o1 + 512], self.s_ones128f[:],
                self.s_xres2[:, fo, :],
                start=(fo == 0), stop=(fo == NFC - 1)),
                reads=[b["xres2"], b["consts"]], writes=[b1], tag=f"sum2_{tb}f{fo}")
            sch.op("pe", lambda e, fo=fo, o1=o1: e.matmul(
                self.p_cp[64:65, o1:o1 + 512], self.s_ones128f[:],
                self.s_xsqb[:, fo % 2, :],
                start=(fo == 0), stop=(fo == NFC - 1)),
                reads=[b["xsqb"], b["consts"]], writes=[b1], tag=f"sumsq2_{tb}f{fo}")

        def rowchainA(tb):
            o1 = (tb % 2) * 512
            b1 = b[f"cpb{tb % 2}"]
            sch.op("act", lambda e, o1=o1: e.activation(
                out=self.s_lrows[:], in_=self.p_cp[0:1, o1:o1 + 512],
                func=AF.Identity, scale=1.0 / D),
                reads=[b1], writes=[b["lrows"]], tag=f"mean2_{tb}")
            sch.op("act", lambda e, o1=o1: e.activation(
                out=self.s_ltmp[:], in_=self.p_cp[64:65, o1:o1 + 512],
                func=AF.Identity, scale=1.0 / D),
                reads=[b1], writes=[b["ltmp"]], tag=f"msq2_{tb}")
            sch.op("act", lambda e: e.activation(
                out=self.s_laux[:], in_=self.s_lrows[:], func=AF.Square),
                reads=[b["lrows"]], writes=[b["laux"]], tag=f"m22_{tb}")

        def rowchain2(tb):
            sch.op("dve", lambda e: e.scalar_tensor_tensor(
                out=self.s_lrows[:], in0=self.s_ltmp[:], scalar=EPS,
                in1=self.s_laux[:], op0=ALU.add, op1=ALU.subtract),
                reads=[b["ltmp"], b["laux"]], writes=[b["lrows"]], tag=f"var2_{tb}")
            sch.op("act", lambda e: e.activation(
                out=self.s_ltmp[:], in_=self.s_lrows[:], func=AF.Sqrt),
                reads=[b["lrows"]], writes=[b["ltmp"]], tag=f"sqrt2_{tb}")
            sch.op("dve", lambda e: e.reciprocal(self.s_laux[:], self.s_ltmp[:]),
                   reads=[b["ltmp"]], writes=[b["laux"]], tag=f"rstd2_{tb}")

        def bmean2(tb):
            sch.op("pe", lambda e: e.matmul(
                self.p_sl[2][:, :], self.s_ones1f[:],
                self.s_lrows[:], start=True, stop=True),
                reads=[b["lrows"], b["consts"]], writes=[b["sl2"]],
                tag=f"bmean2_{tb}")

        def brstd2(tb):
            sch.op("pe", lambda e: e.matmul(
                self.p_sl[1][:, :], self.s_ones1f[:],
                self.s_laux[:], start=True, stop=True),
                reads=[b["laux"], b["consts"]], writes=[b["sl1"]],
                tag=f"brstd2_{tb}")

        def sub2(tb, fo):
            sch.op("dve", lambda e, fo=fo: e.scalar_tensor_tensor(
                out=self.s_r[:] if fo % 2 == 0 else self.s_oh[:],
                in0=self.p_sl[2][:, :],
                scalar=-1.0, in1=self.s_xres2[:, fo, :],
                op0=ALU.mult, op1=ALU.add),
                reads=[b["xres2"], b["sl2"]],
                writes=[b["r"] if fo % 2 == 0 else b["oh"]],
                tag=f"sub2_{tb}f{fo}")

        def x2(tb, fo):
            bi, qb = tb // 4, tb % 4
            sch.op("dve", lambda e, fo=fo: e.tensor_tensor(
                out=self.s_x2o[:, fo, :], in0=self.p_sl[1][:, :],
                in1=self.s_r[:] if fo % 2 == 0 else self.s_oh[:], op=ALU.mult),
                reads=[b["r"] if fo % 2 == 0 else b["oh"], b["sl1"]],
                writes=[b["x2o"]], tag=f"x2_{tb}f{fo}")
            dst = bass.AP(self.d_out, bi * D * S + fo * 128 * S + qb * 512,
                          [[S, 128], [1, 512]])
            sch.op("act", lambda e, dst=dst, fo=fo:
                   e.dma_start(out=dst, in_=self.s_x2o[:, fo, :]),
                   reads=[b["x2o"]], is_dma=True, tag=f"st_x2_{tb}f{fo}")

        NTB = T // 512
        for tb in range(NTB):
            loads(tb)
            mgb(tb)
            if tb > 0:
                rowchainA(tb - 1)
            emm(tb, 0)
            if tb > 0:
                bmean2(tb - 1)
                rowchain2(tb - 1)
            emm(tb, 1)
            if tb > 0:
                brstd2(tb - 1)
                sch.op("act", lambda e: e.nop(), reads=[b["x2o"]],
                       writes=[b["x2o"]], flushq="q_act", tag=f"x2flush_{tb}")
            for fo in range(NFC):
                if tb > 0:
                    sub2(tb - 1, fo)
                    if fo > 0:
                        x2(tb - 1, fo - 1)
                xres2(tb, fo)
                if fo + 2 < NFC:
                    emm(tb, fo + 2)
                if fo > 0:
                    stats2(tb, fo - 1)
            if tb > 0:
                x2(tb - 1, NFC - 1)
            stats2(tb, NFC - 1)
        tb = NTB - 1
        rowchainA(tb)
        bmean2(tb)
        rowchain2(tb)
        brstd2(tb)
        sch.op("act", lambda e: e.nop(), reads=[b["x2o"]],
               writes=[b["x2o"]], flushq="q_act", tag="x2flush_last")
        for fo in range(NFC):
            sub2(tb, fo)
            if fo > 0:
                x2(tb, fo - 1)
        x2(tb, NFC - 1)

    def realize(self):
        sch = self.sch
        counts = sch.finalize()
        nc = self.nc
        streams = {}
        for o in sch.order:
            streams.setdefault(o.stream, []).append(o)

        with contextlib.ExitStack() as es:
            sems = {}
            for name in ["s_pe", "s_dve", "s_act", "s_gp", "s_sp", "q_sp", "q_gp",
                         "q_act", "s_cc"]:
                sems[name] = es.enter_context(nc.semaphore(name))
            block = es.enter_context(nc.Block())

            def replay(eng, ops, pad_raw=False, chain_dma=None):
                watermark = {}
                pos = {}
                slot = 0
                for o in ops:
                    if o.flushq is not None and o.flushval:
                        if watermark.get(o.flushq, 0) < o.flushval:
                            eng.wait_ge(sems[o.flushq], o.flushval)
                            watermark[o.flushq] = o.flushval
                    if chain_dma and o.is_dma and o.sem == chain_dma \
                            and o.incval is not None and o.incval > 16:
                        v = o.incval - 16
                        if watermark.get(o.sem, 0) < v:
                            eng.wait_ge(sems[o.sem], v)
                            watermark[o.sem] = v
                    waited_pe = False
                    for d in o.deps:
                        if d.stream == o.stream and not d.is_dma and not o.is_dma:
                            continue
                        if d.incval is None:
                            continue
                        if watermark.get(d.sem, 0) >= d.incval:
                            continue
                        eng.wait_ge(sems[d.sem], d.incval)
                        watermark[d.sem] = d.incval
                        if d.sem == "s_pe":
                            waited_pe = True
                    if pad_raw and waited_pe and o.fn is not None:
                        for _ in range(6):
                            eng.engine_nop()
                        slot += 6
                    if pad_raw and o.fn is not None:
                        need = 0
                        for d in o.deps:
                            if d.stream == o.stream and not d.is_dma and id(d) in pos:
                                need = max(need, pos[id(d)] + 3 - slot)
                        for _ in range(need):
                            eng.engine_nop()
                            slot += 1
                    if o.fn is None:
                        continue
                    inst = o.fn(eng)
                    pos[id(o)] = slot
                    slot += 1
                    if o.incval is not None:
                        assert inst is not None, o.tag
                        inst.then_inc(sems[o.sem], o.delta)

            @block.tensor
            def _(eng):
                replay(eng, streams.get("pe", []))

            @block.vector
            def _(eng):
                replay(eng, streams.get("dve", []), pad_raw=True)

            @block.scalar
            def _(eng):
                replay(eng, streams.get("act", []))

            @block.gpsimd
            def _(eng):
                replay(eng, streams.get("gp", []), chain_dma="q_gp")

            @block.sync
            def _(eng):
                replay(eng, streams.get("sp", []))
        return nc


def build(nb=NB, npair=NPAIR, use_cc=True, debug=(), skip=()):
    bld = Builder(nb=nb, npair=npair, use_cc=use_cc, debug=debug, skip=skip)
    bld.emit_all()
    nc = bld.realize()
    return nc, bld


def kernel(**inputs):
    in_maps = stage(inputs)
    nc, _ = build()
    res = run_bass_kernel_spmd(nc, in_maps, core_ids=list(range(NCORES)))
    out = np.empty((32, 2048, 1024), dtype=np.float32)
    for c in range(NCORES):
        out[c * NB:(c + 1) * NB] = res.results[c]["x2t"].transpose(0, 2, 1)
    return out


# version 27
# speedup vs baseline: 1.0307x; 1.0081x over previous
import contextlib
import numpy as np
import ml_dtypes
import concourse.bass as bass
import concourse.mybir as mybir
from concourse.bass_utils import run_bass_kernel_spmd
from concourse.replica_groups import maybe_share_collective_output_space

F32 = mybir.dt.float32
BF16 = mybir.dt.bfloat16
AF = mybir.ActivationFunctionType
ALU = mybir.AluOpType
AX = mybir.AxisListType
BF = ml_dtypes.bfloat16

NCORES = 8
NB = 4
S = 2048
D = 1024
NH = 16
HD = 64
NPAIR = 8
NQB = 4
NKT = 16
NFC = 8
CAP = 100
EPS = 1e-5
NEG = -1.0e30


class _Op:
    __slots__ = ("stream", "fn", "deps", "is_dma", "delta", "sem", "incval",
                 "needed", "idx", "tag", "flushq", "flushval")

    def __init__(self, stream, fn, is_dma, tag):
        self.flushq = None
        self.flushval = None
        self.stream = stream
        self.fn = fn
        self.deps = []
        self.is_dma = is_dma
        self.delta = 16 if is_dma else 1
        self.sem = None
        self.incval = None
        self.needed = False
        self.idx = None
        self.tag = tag


class Buf:
    __slots__ = ("name", "last_w", "readers", "rng", "overlaps")

    def __init__(self, name, rng=None):
        self.name = name
        self.last_w = None
        self.readers = []
        self.rng = rng
        self.overlaps = []


class Sched:
    def __init__(self):
        self.order = []
        self.bufs = []

    def link_overlaps(self):
        for i, a in enumerate(self.bufs):
            if a.rng is None:
                continue
            for c in self.bufs[i + 1:]:
                if c.rng is None:
                    continue
                if a.rng[0] < c.rng[1] and c.rng[0] < a.rng[1]:
                    a.overlaps.append(c)
                    c.overlaps.append(a)

    def op(self, stream, fn, reads=(), writes=(), deps=(), is_dma=False, tag="",
           flushq=None):
        o = _Op(stream, fn, is_dma, tag)
        o.flushq = flushq
        o.idx = len(self.order)
        dd = list(deps)
        for b in reads:
            if b.last_w is not None:
                dd.append(b.last_w)
        for b in writes:
            for bb in [b] + b.overlaps:
                if bb.last_w is not None:
                    dd.append(bb.last_w)
                dd.extend(bb.readers)
        seen = set()
        for d in dd:
            if d is None or id(d) in seen or d is o:
                continue
            seen.add(id(d))
            assert d.idx < o.idx
            o.deps.append(d)
        for b in reads:
            b.readers.append(o)
        for b in writes:
            b.last_w = o
            b.readers = []
            for bb in b.overlaps:
                bb.last_w = None
                bb.readers = []
        self.order.append(o)
        return o

    def finalize(self):
        for o in self.order:
            if o.tag == "allgather":
                o.sem = "s_cc"
            elif o.is_dma:
                o.sem = {"sp": "q_sp", "act": "q_act"}.get(o.stream, "q_gp")
            else:
                o.sem = "s_" + o.stream
        for o in self.order:
            for d in o.deps:
                if d.stream == o.stream and not d.is_dma and not o.is_dma:
                    continue
                d.needed = True
        counts = {}
        for o in self.order:
            if o.flushq is not None:
                o.flushval = counts.get(o.flushq, 0)
            if o.needed or o.is_dma:
                counts[o.sem] = counts.get(o.sem, 0) + o.delta
                o.incval = counts[o.sem]
        return counts



def _consts():
    identf = np.eye(128, dtype=np.float32)
    identbf = np.eye(128, dtype=BF)
    ones1f = np.ones((1, 128), dtype=np.float32)
    ones1bf = np.ones((1, 128), dtype=BF)
    ones128f = np.ones((128, 1), dtype=np.float32)
    ones128bf = np.ones((128, 1), dtype=BF)
    k = np.arange(128)[:, None]
    mtq = np.zeros((128, 1280), dtype=np.float32)
    mqk = np.zeros((128, 1280), dtype=np.float32)
    for j in range(4):
        off, w = j * (j + 1) // 2 * 128, (j + 1) * 128
        q5 = np.arange(w)[None, :]
        mtq[:, off:off + w] = np.where(q5 >= j * 128 + k, 0.0, NEG)
        mqk[:, off:off + w] = np.where(q5 <= j * 128 + k, 0.0, NEG)
    mqk2a = np.concatenate([mqk[0:64], mqk[0:64]], axis=0)
    mqk2b = np.concatenate([mqk[64:128], mqk[64:128]], axis=0)
    return {
        "identf": identf, "identbf": identbf,
        "ones1f": ones1f, "ones1bf": ones1bf,
        "ones128f": ones128f, "ones128bf": ones128bf,
        "maskstq": mtq.astype(BF),
        "masksqk2a": mqk2a.astype(BF), "masksqk2b": mqk2b.astype(BF),
    }


def stage(inputs):
    x0 = np.asarray(inputs["L0_attn_inputs"], dtype=np.float32)
    wqkv = np.asarray(inputs["L0_attn_Wqkv_w"], dtype=np.float32)
    bqkv = np.asarray(inputs["L0_attn_Wqkv_b"], dtype=np.float32)
    wo = np.asarray(inputs["L0_attn_Wo_w"], dtype=np.float32)
    bo = np.asarray(inputs["L0_attn_Wo_b"], dtype=np.float32)
    ghost = np.asarray(inputs["L0_attn_ghost"], dtype=np.float32)
    wg = np.asarray(inputs["L0_ffn_w_gate"], dtype=np.float32)
    we = np.asarray(inputs["L0_ffn_w_expert"], dtype=np.float32)
    be = np.asarray(inputs["L0_ffn_b_expert"], dtype=np.float32)

    shared = dict(_consts())
    shared["wqkvt"] = np.ascontiguousarray(wqkv.T)
    shared["wqkvb_pc"] = np.ascontiguousarray(bqkv.reshape(24, 128).T)
    shared["bv_row"] = np.ascontiguousarray(bqkv[2048:].reshape(1, 1024))
    shared["wot"] = np.ascontiguousarray(wo.T)
    shared["wob_pc"] = np.ascontiguousarray(bo.reshape(8, 128).T)
    shared["wgate3"] = np.ascontiguousarray(wg.reshape(8, 128, 8).transpose(1, 0, 2))
    shared["wexp"] = np.ascontiguousarray(we)
    shared["bexp_row"] = np.ascontiguousarray(be.reshape(1, 1024))
    gh2 = np.empty((128, 8), dtype=np.float32)
    gh2[0:64, :] = ghost[0::2][None, :]
    gh2[64:128, :] = ghost[1::2][None, :]
    shared["ghostb"] = gh2

    in_maps = []
    for c in range(NCORES):
        m = dict(shared)
        xb = x0[c * NB:(c + 1) * NB]
        m["x0t"] = np.ascontiguousarray(xb.transpose(0, 2, 1))
        m["premask"] = (np.arange(8) < c).astype(np.float32).reshape(8, 1)
        in_maps.append(m)
    return in_maps



class Builder:
    def __init__(self, nb=NB, npair=NPAIR, use_cc=True, debug=(), skip=()):
        self.skip = set(skip)
        self.nb = nb
        self.npair = npair
        self.use_cc = use_cc
        self.debug = set(debug)
        self.dbg_outs = {}
        self.nc = bass.Bass("TRN2", target_bir_lowering=False)
        self.sch = Sched()
        self.sb_ranges = {}
        self._sb_off = 16512
        self._mk_tensors()

    def _sb(self, name, shape, dtype, off=None):
        nc = self.nc
        per = int(np.prod(shape[1:])) * mybir.dt.size(dtype)
        if off is None:
            off = self._sb_off
            self._sb_off = (off + per + 31) & ~31
        h = nc.alloc_sbuf_tensor_at(name, shape, dtype, offset=off)
        self.sb_ranges[name] = (off, off + per)
        return h

    def _mk_tensors(self):
        nc = self.nc
        self.d = {}
        for name, shp, dt in [
            ("x0t", [NB, D, S], F32), ("wqkvt", [D, 3 * D], F32),
            ("wqkvb_pc", [128, 24], F32), ("bv_row", [1, D], F32),
            ("wot", [D, D], F32), ("wob_pc", [128, 8], F32),
            ("wgate3", [128, 8, 8], F32), ("wexp", [D, D], F32),
            ("bexp_row", [1, D], F32), ("ghostb", [128, 8], F32),
            ("identf", [128, 128], F32), ("identbf", [128, 128], BF16),
            ("ones1f", [1, 128], F32), ("ones1bf", [1, 128], BF16),
            ("ones128f", [128, 1], F32), ("ones128bf", [128, 1], BF16),
            ("maskstq", [128, 1280], BF16), ("masksqk2a", [128, 1280], BF16),
            ("masksqk2b", [128, 1280], BF16),
            ("premask", [8, 1], F32),
        ]:
            self.d[name] = nc.dram_tensor(name, shp, dt, kind="ExternalInput")
        self.d_out = nc.dram_tensor("x2t", [NB, D, S], F32, kind="ExternalOutput")
        self.d_x1sc = nc.dram_tensor("x1sc", [D, NB * S], BF16)
        if self.use_cc:
            rg = [list(range(NCORES))]
            space = maybe_share_collective_output_space("AllGather", rg)
            self.cc_rg = rg
            self.d_ccin = nc.dram_tensor("ccin", [8, 1], F32)
            self.d_ccout = nc.dram_tensor("ccout", [8, 8], F32, addr_space=space)

        self.s_identf = self._sb("identf", [128, 128], F32)
        self.s_identbf = self._sb("identbf", [128, 128], BF16)
        self.s_ones1f = self._sb("ones1f", [1, 128], F32)
        self.s_ones1bf = self._sb("ones1bf", [1, 128], BF16)
        self.s_ones128f = self._sb("ones128f", [128, 1], F32)
        self.s_ones128bf = self._sb("ones128bf", [128, 1], BF16)
        self.s_mtq = self._sb("maskstq", [128, 1280], BF16)
        self.s_mqk2 = [self._sb("masksqk2a", [128, 1280], BF16),
                       self._sb("masksqk2b", [128, 1280], BF16)]
        self.s_ghostb = self._sb("ghostb", [128, 8], F32)
        self.s_bqkv = self._sb("bqkv", [128, 24], F32)
        self.s_bo = self._sb("bo", [128, 8], F32)
        self.s_wg = self._sb("wg", [128, 8, 8], F32)
        self.s_wgbf = self._sb("wgbf", [128, 8, 8], BF16)
        self.s_bvrowbf = self._sb("bvrowbf", [1, D], BF16)
        self.s_berowbf = self._sb("berowbf", [1, D], BF16)
        self.s_premask = self._sb("premask", [8, 1], F32)
        self.s_agc = self._sb("agc", [8, 8], F32)
        self.s_cb = self._sb("cb", [8, 1], F32)
        self.s_ones33 = self._sb("ones33", [33, 128], F32)
        self.s_ones33bf = self._sb("ones33bf", [33, 128], BF16)
        self.s_lrows = self._sb("lrows", [1, 512], F32)
        self.s_ltmp = self._sb("ltmp", [1, 512], F32)
        self.s_laux = self._sb("laux", [1, 512], F32)
        self.s_gateT = self._sb("gateT", [40, NB * S], BF16)
        base = self._sb_off
        o = base
        def at(name, shape, dtype):
            nonlocal o
            h = self._sb(name, shape, dtype, off=o)
            o += (int(np.prod(shape[1:])) * mybir.dt.size(dtype) + 31) & ~31
            return h
        self.s_x0t = at("x0t", [128, NQB, NFC, 512], BF16)
        self.s_rowtmp = self._sb("rowtmp", [1, D], F32, off=base)
        self.s_wqp = [at(f"wqp{i}", [128, NFC, 3, 128], BF16) for i in range(2)]
        self.s_qd = [at(f"qd{h}", [128, S], BF16) for h in range(2)]
        self.s_kbd = [at(f"kbd{h}", [128, S], BF16) for h in range(2)]
        self.s_qbd2 = [at(f"qbd2{h}", [128, S], BF16) for h in range(2)]
        self.s_kt = at("kt", [128, S], BF16)
        self.s_vp = at("vp", [128, NKT, 130], BF16)
        self.s_vt = at("vt", [128, S], BF16)
        self.s_et = [[at(f"et{h}{p}", [128, 512], BF16) for p in range(2)] for h in range(2)]
        self.s_a2n = [at(f"a2n{p}", [128, S], BF16) for p in range(NPAIR)]
        self.s_wot = at("wot", [128, NPAIR, D], BF16)
        self.s_xres = at("xres", [128, NFC, 512], F32)
        self.s_xsq = at("xsq", [128, 2, 512], F32)
        self.s_x1qb = at("x1qb", [128, NFC, 512], BF16)
        self.s_lall = at("lall", [128, 512], F32)
        self.s_oh = at("oh", [128, 512], F32)
        self.s_sig = at("sig", [33, S], BF16)
        self.s_mpair = at("mpair", [128, 32], F32)
        self.s_mprt = at("mprt", [128, 48], F32)
        self.s_emg = at("emg", [128, 64], F32)
        self.s_emgbf = at("emgbf", [128, 32], BF16)
        self.s_r = at("r", [128, 512], F32)
        class1_end = o
        o = base
        self.s_scan = at("scan", [8, NB * S], F32)
        self.s_sel = at("sel", [8, NB * S], BF16)
        self.s_so = at("so", [40, NB * S], BF16)
        self.s_num = at("num", [33, NB * S // 2], F32)
        self.s_zrow = at("zrow", [33, NB * S // 2], F32)
        post_end = o
        self.s_mgbf = at("mgbf", [1, NB * S], BF16)
        cls4_o = o
        self.s_mgt = self._sb("mgt", [33, NB * S // 2], F32,
                              off=base)
        o = cls4_o
        cls4_end = o
        o = base
        self.s_wexp = at("wexp", [128, NFC, D], BF16)
        self.s_x1tb = [at(f"x1tb{i}", [128, NFC, 512], BF16) for i in range(2)]
        self.s_x1mg = at("x1mg", [128, 2, NFC, 512], BF16)
        self.s_xres2 = at("xres2", [128, NFC, 512], F32)
        self.s_x2o = at("x2o", [128, NFC, 512], F32)
        self.s_xsqb = at("xsqb", [128, 2, 512], F32)
        cls5_end = o
        assert cls5_end <= post_end, (cls5_end, post_end)
        self.sbuf_peak = max(class1_end, cls4_end, cls5_end)
        self.sbuf_regions = dict(base=base, class1=class1_end, class3=post_end,
                                 cls4=cls4_end, cls5=cls5_end)
        assert self.sbuf_peak <= 229376, (self.sbuf_peak, self.sbuf_regions)

        self.p_cp = nc.alloc_psum_tensor("cp", [128, 1024], F32)
        self.p_sl = [nc.alloc_psum_tensor(f"sl{i}", [128, 512], F32) for i in range(4)]
        self.p_av = [nc.alloc_psum_tensor(f"av{h}", [128, 512], F32) for h in range(2)]

        rng = self.sb_ranges
        x0lo = rng["x0t"][0]
        x0chunk = NFC * 512 * 2
        name2rng = {
            "wqp0": rng["wqp0"], "wqp1": rng["wqp1"],
            "qd0": rng["qd0"], "qd1": rng["qd1"],
            "kbd0": rng["kbd0"], "kbd1": rng["kbd1"],
            "qbd20": rng["qbd20"], "qbd21": rng["qbd21"],
            "kt": rng["kt"], "vp": rng["vp"], "vt": rng["vt"],
            "et00": rng["et00"], "et01": rng["et01"],
            "et10": rng["et10"], "et11": rng["et11"],
            "wot": rng["wot"], "xres": rng["xres"],
            "xsq": rng["xsq"], "x1qb": rng["x1qb"], "lall": rng["lall"],
            "oh": rng["oh"], "sig": rng["sig"], "mpair": rng["mpair"], "mprt": rng["mprt"],
            "emg": rng["emg"], "emgbf": rng["emgbf"], "r": rng["r"],
            "gateT": rng["gateT"], "scan": rng["scan"],
            "sel": rng["sel"], "so": rng["so"], "num": rng["num"],
            "zrow": rng["zrow"], "mgbf": rng["mgbf"], "mgt": rng["mgt"],
            "wexp": rng["wexp"], "x1tb0": rng["x1tb0"], "x1tb1": rng["x1tb1"],
            "x1mg": rng["x1mg"], "xres2": rng["xres2"], "x2o": rng["x2o"],
            "xsqb": rng["xsqb"], "lrows": rng["lrows"], "ltmp": rng["ltmp"],
            "laux": rng["laux"], "rowtmp": rng["rowtmp"],
            "consts": (16512, rng["gateT"][0]),
        }
        for qb in range(NQB):
            name2rng[f"x0t{qb}"] = (x0lo + qb * x0chunk, x0lo + (qb + 1) * x0chunk)
        self.b = {}
        for n in ["cpb0", "cpb1", "sl0", "sl1", "sl2", "sl3",
                  "av0", "av1", "x1sc", "ccin", "ccout", "agc", "cb"]:
            self.b[n] = Buf(n)
            self.sch.bufs.append(self.b[n])
        for n, r in name2rng.items():
            self.b[n] = Buf(n, r)
            self.sch.bufs.append(self.b[n])
        self.b_a2n = []
        for p in range(NPAIR):
            bb = Buf(f"a2n{p}", rng[f"a2n{p}"])
            self.b_a2n.append(bb)
            self.sch.bufs.append(bb)
        self.sch.link_overlaps()

    def dbg(self, name, src_ap, shape, dtype, reads):
        if name not in self.debug:
            return
        t = self.nc.dram_tensor("dbg_" + name, shape, dtype, kind="ExternalOutput")
        self.dbg_outs[name] = name
        self.sch.op("sp", lambda e, t=t, a=src_ap: e.dma_start(out=t.ap(), in_=a),
                    reads=reads, is_dma=True, tag="dbg_" + name)

    def emit_all(self):
        sch, b = self.sch, self.b
        nc = self.nc

        def ld(sb, dname, cast=False, tag=""):
            src = self.d[dname].ap()
            stream = "gp" if cast else "sp"
            return sch.op(stream, lambda e, o=sb, i=src: e.dma_start(out=o[:], in_=i),
                          writes=[b["consts"]], is_dma=True, tag="ld_" + dname)

        for nm, sb in [("identf", self.s_identf), ("identbf", self.s_identbf),
                       ("ones1f", self.s_ones1f), ("ones1bf", self.s_ones1bf),
                       ("ones128f", self.s_ones128f), ("ones128bf", self.s_ones128bf),
                       ("maskstq", self.s_mtq), ("masksqk2a", self.s_mqk2[0]),
                       ("masksqk2b", self.s_mqk2[1]),
                       ("ghostb", self.s_ghostb), ("wqkvb_pc", self.s_bqkv),
                       ("wob_pc", self.s_bo), ("wgate3", self.s_wg),
                       ("premask", self.s_premask)]:
            ld(sb, nm)
        sch.op("sp", lambda e: e.dma_start(out=self.s_rowtmp[:],
                                           in_=self.d["bv_row"].ap()),
               writes=[b["rowtmp"]], is_dma=True, tag="ld_bv")
        sch.op("dve", lambda e: e.tensor_copy(self.s_bvrowbf[:], self.s_rowtmp[:]),
               reads=[b["rowtmp"]], writes=[b["consts"]], tag="cast_bv")
        sch.op("sp", lambda e: e.dma_start(out=self.s_rowtmp[:],
                                           in_=self.d["bexp_row"].ap()),
               reads=[b["consts"]], writes=[b["rowtmp"]], is_dma=True, tag="ld_be")
        sch.op("dve", lambda e: e.tensor_copy(self.s_berowbf[:], self.s_rowtmp[:]),
               reads=[b["rowtmp"]], writes=[b["consts"]], tag="cast_be")
        sch.op("dve", lambda e: e.memset(self.s_ones33[0:1, :], 1.0),
               writes=[b["consts"]], tag="ones33a")
        sch.op("dve", lambda e: e.memset(self.s_ones33[32:33, :], 1.0),
               writes=[b["consts"]], tag="ones33b")
        sch.op("dve", lambda e: e.memset(self.s_ones33bf[0:1, :], 1.0),
               writes=[b["consts"]], tag="ones33c")
        sch.op("dve", lambda e: e.memset(self.s_ones33bf[32:33, :], 1.0),
               writes=[b["consts"]], tag="ones33d")
        sch.op("dve", lambda e: e.tensor_copy(self.s_wgbf[:], self.s_wg[:]),
               reads=[b["consts"]], writes=[b["consts"]], tag="cast_wg")
        for nm, sb in [("kbd0", self.s_kbd[0]), ("kbd1", self.s_kbd[1]),
                       ("qbd20", self.s_qbd2[0]), ("qbd21", self.s_qbd2[1])]:
            sch.op("dve", lambda e, sb=sb: e.memset(sb[:], 0.0),
                   writes=[b[nm]], tag=f"z_{nm}")

        for bi in range(self.nb):
            self.emit_batch(bi)
        self.emit_routing()
        self.emit_phase_b()

        last_sp = [o for o in sch.order if o.is_dma and o.stream == "sp"][-1]
        last_act = [o for o in sch.order if o.is_dma and o.stream == "act"][-1]
        sch.op("sp", None, deps=[last_sp, last_act], tag="final_wait")
        sch.op("dve", lambda e: e.engine_nop(), deps=[last_act, last_sp],
               tag="final_wait2")

    def _x0_chunk_load(self, bi, qb):
        sch, b = self.sch, self.b
        src = self.d["x0t"].ap()[bi].rearrange(
            "(c p) t -> p c t", p=128)[:, :, qb * 512:(qb + 1) * 512]
        sch.op("gp", lambda e, i=src, qb=qb: e.dma_start(
            out=self.s_x0t[:, qb, :, :], in_=i),
            writes=[b[f"x0t{qb}"]], is_dma=True, tag=f"ld_x0t_b{bi}c{qb}")

    def _wq_load(self, bi, pair):
        sch, b = self.sch, self.b
        wq = self.s_wqp[pair % 2]
        bwq = b[f"wqp{pair % 2}"]
        for blk in range(3):
            off = blk * 1024 + pair * 128
            srcw = bass.AP(self.d["wqkvt"], off,
                           [[3072, 128], [128 * 3072, 8], [1, 128]])
            dst = wq[:, :, blk, :]
            sch.op("gp", lambda e, o=dst, i=srcw: e.dma_start(out=o, in_=i),
                   writes=[bwq], is_dma=True, tag=f"ld_wq_b{bi}p{pair}k{blk}")

    def emit_batch(self, bi):
        sch, b = self.sch, self.b
        if bi == 0:
            for qb in range(NQB):
                self._x0_chunk_load(bi, qb)
            wsrc = self.d["wot"].ap().rearrange("(c p) f -> p c f", p=128)
            sch.op("gp", lambda e, i=wsrc: e.dma_start(out=self.s_wot[:], in_=i),
                   writes=[b["wot"]], is_dma=True, tag=f"ld_wot_b{bi}")

        for pair in range(self.npair):
            self.emit_pair(bi, pair, skip_wload=(bi > 0 and pair < 2))
        if bi + 1 < self.nb:
            self._wq_load(bi + 1, 0)
            self._wq_load(bi + 1, 1)
        self.emit_wo_ln1(bi)
        self.emit_gate_batch(bi)

    def emit_pair(self, bi, pair, skip_wload=False):
        sch, b = self.sch, self.b
        wq = self.s_wqp[pair % 2]
        bwq = b[f"wqp{pair % 2}"]
        if not skip_wload:
            self._wq_load(bi, pair)

        for blk in range(2):
            for qb in range(NQB):
                ps = self.p_sl[qb][:, :]
                def mmgrp(e, ps=ps, wq=wq, blk=blk, qb=qb):
                    r = None
                    for c in range(NFC):
                        r = e.matmul(ps, wq[:, c, blk, :],
                                     self.s_x0t[:, qb, c, :],
                                     start=(c == 0), stop=(c == NFC - 1))
                    return r
                sch.op("pe", mmgrp, reads=[bwq, b[f"x0t{qb}"]], writes=[b[f"sl{qb}"]],
                       tag=f"qkproj_b{bi}p{pair}x{blk}q{qb}")
                if blk == 0:
                    for hl in range(2):
                        r0 = 64 * hl
                        dst = self.s_qd[hl]
                        bias = self.s_bqkv[r0:r0 + 64, pair:pair + 1]
                        psh = self.p_sl[qb].ap()[r0:r0 + 64, :]
                        sch.op("act", lambda e, psh=psh, dst=dst, qb=qb, bias=bias,
                               r0=r0: e.activation(
                                   out=dst[r0:r0 + 64, qb * 512:(qb + 1) * 512],
                                   in_=psh, func=AF.Identity, bias=bias, scale=1.0),
                               reads=[b[f"sl{qb}"], b["consts"]],
                               writes=[b[f"qd{hl}"]],
                               tag=f"qevac_b{bi}p{pair}h{hl}q{qb}")
                else:
                    bias = self.s_bqkv[:, 8 + pair:9 + pair]
                    sch.op("act", lambda e, ps=ps, qb=qb, bias=bias:
                           e.activation(out=self.s_kt[:, qb * 512:(qb + 1) * 512],
                                        in_=ps, func=AF.Identity, bias=bias,
                                        scale=1.0),
                           reads=[b[f"sl{qb}"], b["consts"]], writes=[b["kt"]],
                           tag=f"kevac_b{bi}p{pair}q{qb}")
            def half_view(sb, p0, h):
                return sb.ap()[p0:p0 + 64, :].rearrange(
                    "p (k h c) -> p k h c", h=2, c=64)[:, :, h:h + 1, :]
            if blk == 0:
                def qdma(dst, src, srcbuf, dstbuf, tag):
                    sch.op("sp", lambda e, o=dst, i=src: e.dma_start(out=o, in_=i),
                           reads=[b[srcbuf]], writes=[b[dstbuf]],
                           is_dma=True, tag=tag)
                t = f"_b{bi}p{pair}"
                qdma(self.s_qd[0].ap()[64:128, :], self.s_qd[0].ap()[0:64, :],
                     "qd0", "qd0", "bq_qd0" + t)
                qdma(self.s_qd[1].ap()[0:64, :], self.s_qd[1].ap()[64:128, :],
                     "qd1", "qd1", "bq_qd1" + t)
                qdma(half_view(self.s_qbd2[0], 0, 0), half_view(self.s_qd[0], 0, 0),
                     "qd0", "qbd20", "bq_q20a" + t)
                qdma(half_view(self.s_qbd2[0], 64, 1), half_view(self.s_qd[1], 64, 0),
                     "qd1", "qbd20", "bq_q20b" + t)
                qdma(half_view(self.s_qbd2[1], 0, 0), half_view(self.s_qd[0], 0, 1),
                     "qd0", "qbd21", "bq_q21a" + t)
                qdma(half_view(self.s_qbd2[1], 64, 1), half_view(self.s_qd[1], 64, 1),
                     "qd1", "qbd21", "bq_q21b" + t)
                sch.op("sp", lambda e: e.nop(),
                       reads=[b["qd0"], b["qd1"]],
                       writes=[b["qd0"], b["qd1"], b["qbd20"], b["qbd21"]],
                       flushq="q_sp", tag=f"qflush_b{bi}p{pair}")
            else:
                def kdma(dst, src, dstbuf, tag):
                    sch.op("gp", lambda e, o=dst, i=src: e.dma_start(out=o, in_=i),
                           reads=[b["kt"]], writes=[b[dstbuf]],
                           is_dma=True, tag=tag)
                t = f"_b{bi}p{pair}"
                kdma(half_view(self.s_kbd[0], 0, 0), half_view(self.s_kt, 0, 0),
                     "kbd0", "bk_k0a" + t)
                kdma(half_view(self.s_kbd[0], 64, 1), half_view(self.s_kt, 0, 1),
                     "kbd0", "bk_k0b" + t)
                kdma(half_view(self.s_kbd[1], 0, 0), half_view(self.s_kt, 64, 0),
                     "kbd1", "bk_k1a" + t)
                kdma(half_view(self.s_kbd[1], 64, 1), half_view(self.s_kt, 64, 1),
                     "kbd1", "bk_k1b" + t)
                sch.op("gp", lambda e: e.engine_nop(),
                       reads=[b["kt"]],
                       writes=[b["kbd0"], b["kbd1"]],
                       flushq="q_gp", tag=f"kflush_b{bi}p{pair}")

        vones_ap = bass.AP(self.s_vp, 64, [[NKT * 130, 128], [130, NKT], [65, 2]])
        sch.op("dve", lambda e, a=vones_ap: e.memset(a, 1.0),
               writes=[b["vp"]], tag=f"vones_b{bi}p{pair}")
        for qb in range(NQB):
            ps = self.p_sl[qb][:, :]
            def vmm(e, ps=ps, wq=wq, qb=qb):
                r = None
                for c in range(NFC):
                    r = e.matmul(ps, wq[:, c, 2, :],
                                 self.s_x0t[:, qb, c, :],
                                 start=(c == 0), stop=(c == NFC - 1))
                return r
            sch.op("pe", vmm, reads=[bwq, b[f"x0t{qb}"]], writes=[b[f"sl{qb}"]],
                   tag=f"vproj_b{bi}p{pair}q{qb}")
            bias = self.s_bqkv[:, 16 + pair:17 + pair]
            sch.op("act", lambda e, ps=ps, qb=qb, bias=bias:
                   e.activation(out=self.s_vt[:, qb * 512:(qb + 1) * 512], in_=ps,
                                func=AF.Identity, bias=bias, scale=1.0),
                   reads=[b[f"sl{qb}"], b["consts"]], writes=[b["vt"]],
                   tag=f"vevac_b{bi}p{pair}q{qb}")
        for tt in range(NKT):
            psb = self.p_sl[tt % 4].ap()[:, 0:64].bitcast(BF16)
            sch.op("pe", lambda e, psb=psb, tt=tt: e.transpose(
                psb, self.s_vt[:, tt * 128:(tt + 1) * 128], self.s_identbf[:]),
                reads=[b["vt"], b["consts"]], writes=[b[f"sl{tt % 4}"]],
                tag=f"vtr_b{bi}p{pair}t{tt}")
            dst = bass.AP(self.s_vp, tt * 130, [[NKT * 130, 128], [65, 2], [1, 64]])
            sch.op("dve", lambda e, psb=psb, dst=dst:
                   e.tensor_copy(dst, psb.rearrange("p (h d) -> p h d", h=2)),
                   reads=[b[f"sl{tt % 4}"]], writes=[b["vp"]],
                   tag=f"vevacT_b{bi}p{pair}t{tt}")

        for qb in range(NQB):
            nkt = 4 * qb + 4
            cs_items = self._cs_items(bi, pair, qb)
            ncs, done = len(cs_items), 0
            for kt in range(nkt + 1):
                if kt == 2 and qb > 0:
                    self._qb_denom(bi, pair, qb - 1)
                for hl in range(2):
                    if kt < nkt:
                        sb_i = hl * 2 + kt % 2
                        ps, pb = self.p_sl[sb_i], b[f"sl{sb_i}"]
                        diag = kt - 4 * qb
                        def stgrp(e, ps=ps, hl=hl, kt=kt, qb=qb, diag=diag):
                            r = e.matmul(ps[:, :],
                                         self.s_kbd[hl][:, kt * 128:(kt + 1) * 128],
                                         self.s_qd[hl][:, qb * 512:(qb + 1) * 512],
                                         start=True, stop=(diag < 0))
                            if diag >= 0:
                                off = diag * (diag + 1) // 2 * 128
                                w = (diag + 1) * 128
                                r = e.matmul(ps[:, 0:w], self.s_identbf[:],
                                             self.s_mtq[:, off:off + w], start=False,
                                             stop=True)
                            return r
                        sch.op("pe", stgrp,
                               reads=[b[f"qd{hl}"], b[f"kbd{hl}"], b["consts"]],
                               writes=[pb], tag=f"st_b{bi}p{pair}q{qb}k{kt}h{hl}")
                    if kt > 0:
                        ktp = kt - 1
                        et = self.s_et[hl][ktp % 2]
                        eb = b[f"et{hl}{ktp % 2}"]
                        pa, pab = self.p_av[hl], b[f"av{hl}"]
                        vsl = self.s_vp[:, ktp, hl * 65:hl * 65 + 65]
                        sch.op("pe", lambda e, pa=pa, vsl=vsl, et=et, ktp=ktp, nkt=nkt:
                               e.matmul(pa[0:65, :], vsl, et[:],
                                        start=(ktp == 0), stop=(ktp == nkt - 1)),
                               reads=[b["vp"], eb], writes=[pab],
                               tag=f"av_b{bi}p{pair}q{qb}k{ktp}h{hl}")
                    if kt < nkt:
                        sb_i = hl * 2 + kt % 2
                        ps, pb = self.p_sl[sb_i], b[f"sl{sb_i}"]
                        et = self.s_et[hl][kt % 2]
                        eb = b[f"et{hl}{kt % 2}"]
                        sch.op("act", lambda e, ps=ps, et=et:
                               e.activation(out=et[:], in_=ps[:, :], func=AF.Exp,
                                            scale=0.125),
                               reads=[pb], writes=[eb],
                               tag=f"exp_b{bi}p{pair}q{qb}k{kt}h{hl}")
                tgt = (kt + 1) * ncs // (nkt + 1)
                while done < tgt:
                    cs_items[done]()
                    done += 1

            self._av_evac(bi, pair, qb)
        self._qb_denom(bi, pair, NQB - 1)
        self._pair_denoms(bi, pair)
        self.dbg(f"a2n_b{bi}p{pair}", self.s_a2n[pair][:], [128, S], BF16,
                 [self.b_a2n[pair]])
        self.dbg(f"sig_b{bi}p{pair}", self.s_sig[:], [33, S], BF16, [b["sig"]])
        self.dbg(f"mpair_b{bi}p{pair}", self.s_mpair[:], [128, 32], F32,
                 [b["mpair"]])
        self.dbg(f"qt_b{bi}p{pair}", self.s_qd[0][:], [128, S], BF16, [b["qd0"]])
        self.dbg(f"vp_b{bi}p{pair}", self.s_vp[:], [128, NKT, 130], BF16,
                 [b["vp"]])

    def _cs_items(self, bi, pair, qb):
        sch, b = self.sch, self.b
        items = []
        idx = [0]
        for i in range(4):
            qt = 4 * qb + i
            for qh in range(2):
                jj = qt % 4
                ext = (qt + 1) * 128
                nr = (ext + 511) // 512
                for rd in range(nr):
                    lo = rd * 512
                    w = min(ext - lo, 512)
                    dlast = (rd == nr - 1)
                    def item(qt=qt, qh=qh, rd=rd, lo=lo, w=w, dlast=dlast, jj=jj):
                        cb = idx[0] % 2
                        po = cb * 512
                        idx[0] += 1
                        def cpg(e, lo=lo, w=w, qh=qh, qt=qt, po=po,
                                dlast=dlast, jj=jj):
                            if not dlast:
                                return e.matmul(
                                    self.p_cp[:, po:po + w],
                                    self.s_qbd2[qh][:, qt * 128:(qt + 1) * 128],
                                    self.s_kt[:, lo:lo + w],
                                    start=True, stop=True)
                            off = jj * (jj + 1) // 2 * 128
                            e.matmul(self.p_cp[:, po:po + w],
                                     self.s_qbd2[qh][:, qt * 128:(qt + 1) * 128],
                                     self.s_kt[:, lo:lo + w],
                                     start=True, stop=False)
                            return e.matmul(self.p_cp[:, po:po + w],
                                            self.s_identbf[:],
                                            self.s_mqk2[qh][:, off:off + w],
                                            start=False, stop=True)
                        cbank = b[f"cpb{cb}"]
                        sch.op("pe", cpg,
                               reads=[b[f"qbd2{qh}"], b["kt"], b["consts"]],
                               writes=[cbank],
                               tag=f"cs_b{bi}p{pair}t{qt}h{qh}r{rd}")
                        mcol = self.s_mpair[:, qt * 2 + qh:qt * 2 + qh + 1]
                        if rd == 0:
                            sch.op("dve", lambda e, mcol=mcol, w=w, po=po:
                                   e.reduce_max(mcol, self.p_cp[:, po:po + w],
                                                axis=AX.X),
                                   reads=[cbank], writes=[b["mpair"]],
                                   tag=f"max_b{bi}p{pair}t{qt}h{qh}")
                        else:
                            nr1 = (qt // 4 + 1) - 1
                            pc = (0, 0, 8, 24)[qt // 4] \
                                + ((qt % 4) * 2 + qh) * nr1 + (rd - 1)
                            mp = self.s_mprt[:, pc:pc + 1]
                            sch.op("dve", lambda e, mp=mp, w=w, po=po:
                                   e.reduce_max(mp, self.p_cp[:, po:po + w],
                                                axis=AX.X),
                                   reads=[cbank], writes=[b["mprt"]],
                                   tag=f"maxp_b{bi}p{pair}t{qt}h{qh}r{rd}")
                    items.append(item)
        return items

    def _av_evac(self, bi, pair, qb):
        sch, b = self.sch, self.b
        for hl in range(2):
            pa, pab = self.p_av[hl], b[f"av{hl}"]
            srow = 32 * hl
            so = sch.op("act", lambda e, pa=pa, srow=srow, qb=qb:
                        e.copy(out=self.s_sig[srow:srow + 1, qb * 512:(qb + 1) * 512],
                               in_=pa[64:65, :]),
                        reads=[pab], writes=[b["sig"]],
                        tag=f"sigevac_b{bi}p{pair}q{qb}h{hl}")
            sch.op("dve", lambda e, pa=pa, hl=hl, qb=qb, pair=pair:
                   e.tensor_copy(self.s_a2n[pair][64 * hl:64 * hl + 64,
                                                  qb * 512:(qb + 1) * 512],
                                 pa[0:64, :]),
                   reads=[pab], writes=[self.b_a2n[pair]], deps=[so],
                   tag=f"aevac_b{bi}p{pair}q{qb}h{hl}")

    def _qb_denom(self, bi, pair, qb):
        sch, b = self.sch, self.b
        nr1 = qb
        pb0 = (0, 0, 8, 24)[qb]
        for k in range(1, nr1):
            sch.op("dve", lambda e, k=k, nr1=nr1, pb0=pb0: e.tensor_tensor(
                out=bass.AP(self.s_mprt, pb0 + k, [[48, 128], [nr1, 8]]),
                in0=bass.AP(self.s_mprt, pb0 + k - 1, [[48, 128], [nr1, 8]]),
                in1=bass.AP(self.s_mprt, pb0 + k, [[48, 128], [nr1, 8]]),
                op=ALU.max),
                reads=[b["mprt"]], writes=[b["mprt"]],
                tag=f"mcmb_b{bi}p{pair}q{qb}k{k}")
        if nr1 > 0:
            sch.op("dve", lambda e, nr1=nr1, qb=qb, pb0=pb0: e.tensor_tensor(
                out=self.s_mpair[:, 8 * qb:8 * qb + 8],
                in0=bass.AP(self.s_mprt, pb0 + nr1 - 1, [[48, 128], [nr1, 8]]),
                in1=self.s_mpair[:, 8 * qb:8 * qb + 8],
                op=ALU.max),
                reads=[b["mprt"]], writes=[b["mpair"]],
                tag=f"mcmbf_b{bi}p{pair}q{qb}")
        sch.op("act", lambda e, qb=qb: e.activation(
            out=self.s_emg[:, 8 * qb:8 * qb + 8],
            in_=self.s_mpair[:, 8 * qb:8 * qb + 8], func=AF.Exp, scale=0.125),
            reads=[b["mpair"]], writes=[b["emg"]], tag=f"em_b{bi}p{pair}q{qb}")
        sch.op("dve", lambda e, qb=qb: e.tensor_scalar(
            out=self.s_emgbf[:, 8 * qb:8 * qb + 8],
            in0=self.s_emg[:, 8 * qb:8 * qb + 8],
            scalar1=self.s_ghostb[:, pair:pair + 1], scalar2=None, op0=ALU.mult),
            reads=[b["emg"], b["consts"]], writes=[b["emgbf"]],
            tag=f"emg_b{bi}p{pair}q{qb}")

    def _pair_denoms(self, bi, pair):
        sch, b = self.sch, self.b
        for qb in range(NQB):
            self._one_denom(bi, pair, qb)

    def _denom_bank(self, qb, hl):
        idx = qb * 2 + hl
        if idx < 2:
            return self.p_cp[:, idx * 512:(idx + 1) * 512], self.b[f"cpb{idx}"]
        if idx < 6:
            return self.p_sl[idx - 2][:, :], self.b[f"sl{idx - 2}"]
        return self.p_av[idx - 6][:, :], self.b[f"av{idx - 6}"]

    def _one_denom(self, bi, pair, qb):
        sch, b = self.sch, self.b
        for hl in range(2):
            srow = 32 * hl
            r0 = 64 * hl
            ps, pb = self._denom_bank(qb, hl)
            def dmm(e, ps=ps, hl=hl, qb=qb, srow=srow, r0=r0):
                r = e.matmul(ps[r0:r0 + 64, 0:512],
                             self.s_ones33bf[srow:srow + 1, 0:64],
                             self.s_sig[srow:srow + 1, qb * 512:(qb + 1) * 512],
                             start=True, stop=False, tile_position=(srow, r0))
                for i in range(4):
                    qt = 4 * qb + i
                    for qh in range(2):
                        col = self.s_emgbf[:, qt * 2 + qh:qt * 2 + qh + 1]
                        r = e.matmul(
                            ps[r0:r0 + 64,
                               i * 128 + qh * 64:i * 128 + qh * 64 + 64],
                            col.broadcast_to([128, 64]),
                            self.s_identbf[:, 64 * hl:64 * hl + 64],
                            start=False, stop=(i == 3 and qh == 1),
                            tile_position=(0, r0))
                return r
            sch.op("pe", dmm, reads=[b["emgbf"], b["sig"], b["consts"]], writes=[pb],
                   tag=f"denom_b{bi}p{pair}h{hl}q{qb}")
        for hl in range(2):
            r0 = 64 * hl
            ps, pb = self._denom_bank(qb, hl)
            sch.op("dve", lambda e, ps=ps, r0=r0: e.reciprocal(
                self.s_r[r0:r0 + 64, :], ps[r0:r0 + 64, :]),
                reads=[pb], writes=[b["r"]], tag=f"recip_b{bi}p{pair}h{hl}q{qb}")
        for hl in range(2):
            r0 = 64 * hl
            sch.op("dve", lambda e, hl=hl, qb=qb, pair=pair, r0=r0: e.tensor_tensor(
                out=self.s_a2n[pair][r0:r0 + 64, qb * 512:(qb + 1) * 512],
                in0=self.s_a2n[pair][r0:r0 + 64, qb * 512:(qb + 1) * 512],
                in1=self.s_r[r0:r0 + 64, :], op=ALU.mult),
                reads=[b["r"], self.b_a2n[pair]], writes=[self.b_a2n[pair]],
                tag=f"norm_b{bi}p{pair}h{hl}q{qb}")

    def emit_wo_ln1(self, bi):
        sch, b = self.sch, self.b
        banks = self.p_av + self.p_sl[0:2]
        bnames = ("av0", "av1", "sl0", "sl1")

        def womm(qb, fc):
            ps, pb = banks[fc % 4], b[bnames[fc % 4]]
            np_ = self.npair
            def mm(e, ps=ps, fc=fc, qb=qb, np_=np_):
                r = None
                for p in range(np_):
                    r = e.matmul(ps[:, 0:512],
                                 self.s_wot[:, p, fc * 128:(fc + 1) * 128],
                                 self.s_a2n[p][:, qb * 512:(qb + 1) * 512],
                                 start=(p == 0), stop=(p == NPAIR - 1))
                return r
            sch.op("pe", mm, reads=[b["wot"]] + list(self.b_a2n[:self.npair]),
                   writes=[pb], tag=f"wo_b{bi}q{qb}f{fc}")

        def xres(qb, fc):
            ps, pb = banks[fc % 4], b[bnames[fc % 4]]
            sch.op("dve", lambda e, ps=ps, fc=fc, qb=qb: e.scalar_tensor_tensor(
                out=self.s_xres[:, fc, :], in0=ps[:, 0:512],
                scalar=self.s_bo[:, fc:fc + 1],
                in1=self.s_x0t[:, qb, fc, :],
                op0=ALU.add, op1=ALU.add),
                reads=[pb, b[f"x0t{qb}"], b["consts"]], writes=[b["xres"]],
                tag=f"xres_b{bi}q{qb}f{fc}")
            sch.op("act", lambda e, fc=fc: e.activation(
                out=self.s_xsq[:, fc % 2, :], in_=self.s_xres[:, fc, :],
                func=AF.Square),
                reads=[b["xres"]], writes=[b["xsq"]], tag=f"sq_b{bi}q{qb}f{fc}")

        def stats(qb, fc):
            o1 = (qb % 2) * 512
            b1 = b[f"cpb{qb % 2}"]
            sch.op("pe", lambda e, fc=fc, o1=o1: e.matmul(
                self.p_cp[0:1, o1:o1 + 512], self.s_ones128f[:],
                self.s_xres[:, fc, :],
                start=(fc == 0), stop=(fc == NFC - 1)),
                reads=[b["xres"], b["consts"]], writes=[b1],
                tag=f"sum_b{bi}q{qb}f{fc}")
            sch.op("pe", lambda e, fc=fc, o1=o1: e.matmul(
                self.p_cp[64:65, o1:o1 + 512], self.s_ones128f[:],
                self.s_xsq[:, fc % 2, :],
                start=(fc == 0), stop=(fc == NFC - 1)),
                reads=[b["xsq"], b["consts"]], writes=[b1],
                tag=f"sumsq_b{bi}q{qb}f{fc}")

        def rowchain(qb):
            o1 = (qb % 2) * 512
            b1 = b[f"cpb{qb % 2}"]
            sch.op("act", lambda e, o1=o1: e.activation(
                out=self.s_lrows[:], in_=self.p_cp[0:1, o1:o1 + 512],
                func=AF.Identity, scale=1.0 / D),
                reads=[b1], writes=[b["lrows"]], tag=f"mean_b{bi}q{qb}")
            sch.op("act", lambda e, o1=o1: e.activation(
                out=self.s_ltmp[:], in_=self.p_cp[64:65, o1:o1 + 512],
                func=AF.Identity, scale=1.0 / D),
                reads=[b1], writes=[b["ltmp"]], tag=f"msq_b{bi}q{qb}")
            sch.op("act", lambda e: e.activation(
                out=self.s_laux[:], in_=self.s_lrows[:], func=AF.Square),
                reads=[b["lrows"]], writes=[b["laux"]], tag=f"m2_b{bi}q{qb}")

        def rowchain2(qb):
            sch.op("dve", lambda e: e.scalar_tensor_tensor(
                out=self.s_lrows[:], in0=self.s_ltmp[:], scalar=EPS,
                in1=self.s_laux[:], op0=ALU.add, op1=ALU.subtract),
                reads=[b["ltmp"], b["laux"]], writes=[b["lrows"]],
                tag=f"var_b{bi}q{qb}")
            sch.op("act", lambda e: e.activation(
                out=self.s_ltmp[:], in_=self.s_lrows[:], func=AF.Sqrt),
                reads=[b["lrows"]], writes=[b["ltmp"]], tag=f"sqrt_b{bi}q{qb}")
            sch.op("dve", lambda e: e.reciprocal(self.s_laux[:], self.s_ltmp[:]),
                   reads=[b["ltmp"]], writes=[b["laux"]], tag=f"rstd_b{bi}q{qb}")

        def bmean(qb):
            sch.op("pe", lambda e: e.matmul(
                self.p_sl[2][:, :], self.s_ones1f[:],
                self.s_lrows[:], start=True, stop=True),
                reads=[b["lrows"], b["consts"]], writes=[b["sl2"]],
                tag=f"bmean_b{bi}q{qb}")

        def brstd(qb):
            sch.op("pe", lambda e: e.matmul(
                self.p_sl[3][:, :], self.s_ones1f[:],
                self.s_laux[:], start=True, stop=True),
                reads=[b["laux"], b["consts"]], writes=[b["sl3"]],
                tag=f"brstd_b{bi}q{qb}")

        subscr = [self.s_r, self.s_oh]
        subbuf = [b["r"], b["oh"]]

        def sub(qb, fc):
            sch.op("dve", lambda e, fc=fc: e.scalar_tensor_tensor(
                out=subscr[fc % 2][:], in0=self.p_sl[2][:, :],
                scalar=-1.0, in1=self.s_xres[:, fc, :],
                op0=ALU.mult, op1=ALU.add),
                reads=[b["xres"], b["sl2"]], writes=[subbuf[fc % 2]],
                tag=f"sub_b{bi}q{qb}f{fc}")

        def x1(qb, fc):
            sch.op("dve", lambda e, fc=fc: e.tensor_tensor(
                out=self.s_x1qb[:, fc, :], in0=self.p_sl[3][:, :],
                in1=subscr[fc % 2][:], op=ALU.mult),
                reads=[subbuf[fc % 2], b["sl3"]], writes=[b["x1qb"]],
                tag=f"x1_b{bi}q{qb}f{fc}")
            dst = bass.AP(self.d_x1sc, fc * 128 * (NB * S) + bi * S + qb * 512,
                          [[NB * S, 128], [1, 512]])
            if "stx1" not in self.skip:
                sch.op("sp", lambda e, fc=fc, dst=dst:
                       e.dma_start(out=dst, in_=self.s_x1qb[:, fc, :]),
                       reads=[b["x1qb"]], writes=[b["x1sc"]], is_dma=True,
                       tag=f"st_x1_b{bi}q{qb}f{fc}")

        def gmm(qb):
            o1 = (qb % 2) * 512
            if "gmm" in self.skip:
                return
            def g(e, qb=qb, o1=o1):
                r = None
                for tt in range(4):
                    for fc in range(NFC):
                        r = e.matmul(self.p_cp[:, o1 + tt * 8:o1 + tt * 8 + 8],
                                     self.s_x1qb[:, fc, tt * 128:(tt + 1) * 128],
                                     self.s_wgbf[:, fc, :],
                                     start=(fc == 0), stop=(fc == NFC - 1))
                return r
            sch.op("pe", g, reads=[b["x1qb"], b["consts"]],
                   writes=[b[f"cpb{qb % 2}"]], tag=f"gate_b{bi}q{qb}")
            sch.op("dve", lambda e, qb=qb, o1=o1: e.tensor_copy(
                self.s_lall[:, qb * 32:qb * 32 + 32], self.p_cp[:, o1:o1 + 32]),
                reads=[b[f"cpb{qb % 2}"]], writes=[b["lall"]],
                tag=f"levac_b{bi}q{qb}")

        def tail_head(qp):
            rowchain(qp)

        def tail_mid(qp, fc):
            sub(qp, fc)
            if fc > 0:
                x1(qp, fc - 1)

        def tail_end(qp):
            x1(qp, NFC - 1)
            gmm(qp)

        for qb in range(NQB):
            if qb > 0:
                tail_head(qb - 1)
            womm(qb, 0)
            if qb > 0:
                bmean(qb - 1)
                rowchain2(qb - 1)
            womm(qb, 1)
            if qb > 0:
                brstd(qb - 1)
            for fc in range(NFC):
                if qb > 0:
                    tail_mid(qb - 1, fc)
                xres(qb, fc)
                if fc + 2 < NFC:
                    womm(qb, fc + 2)
                if fc > 0:
                    stats(qb, fc - 1)
            if qb > 0:
                tail_end(qb - 1)
            stats(qb, NFC - 1)
            if bi + 1 < self.nb:
                self._x0_chunk_load(bi + 1, qb)
        tail_head(NQB - 1)
        bmean(NQB - 1)
        rowchain2(NQB - 1)
        brstd(NQB - 1)
        for fc in range(NFC):
            tail_mid(NQB - 1, fc)
        tail_end(NQB - 1)
        sch.op("sp", lambda e: e.nop(), reads=[b["x1qb"]], writes=[b["x1qb"]],
               flushq="q_sp", tag=f"x1flush_b{bi}")
        self.dbg(f"x1qb_b{bi}", self.s_x1qb[:], [128, NFC, 512], BF16, [b["x1qb"]])
        self.dbg(f"xres_b{bi}", self.s_xres[:], [128, NFC, 512], F32, [b["xres"]])
        self.dbg(f"lall_b{bi}", self.s_lall[:], [128, 512], F32, [b["lall"]])

    def emit_gate_batch(self, bi):
        sch, b = self.sch, self.b
        l3 = self.s_lall.ap().rearrange("p (t e) -> p t e", e=8)
        sch.op("dve", lambda e, l3=l3: e.reduce_max(self.s_xsq[:, 0, 0:64], l3, axis=AX.X),
               reads=[b["lall"]], writes=[b["xsq"]], tag=f"gmax_b{bi}")
        mx3 = self.s_xsq.ap()[:, 0, 0:64].rearrange("p (t o) -> p t o", o=1)
        sch.op("dve", lambda e, l3=l3, mx3=mx3: e.tensor_tensor(
            out=self.s_oh.ap().rearrange("p (t e) -> p t e", e=8), in0=l3,
            in1=mx3.broadcast_to([128, 64, 8]), op=ALU.is_equal),
            reads=[b["lall"], b["xsq"]], writes=[b["oh"]], tag=f"oneh_b{bi}")
        for g in range(4):
            def trg(e, g=g, src=self.s_oh):
                r = None
                for i in range(4):
                    tt = 4 * g + i
                    r = e.transpose(self.p_cp[0:8, i * 128:(i + 1) * 128],
                                    src[:, tt * 8:tt * 8 + 8], self.s_identf[:])
                return r
            sch.op("pe", trg, reads=[b["oh"], b["consts"]], writes=[b["cpb0"]],
                   tag=f"ohT_b{bi}g{g}")
            sch.op("dve", lambda e, g=g: e.tensor_copy(
                self.s_gateT[0:8, bi * S + g * 512:bi * S + (g + 1) * 512],
                self.p_cp[0:8, 0:512]),
                reads=[b["cpb0"]], writes=[b["gateT"]], tag=f"ohTe_b{bi}g{g}")
            def trl(e, g=g):
                r = None
                for i in range(4):
                    tt = 4 * g + i
                    r = e.transpose(self.p_cp[0:8, 512 + i * 128:640 + i * 128],
                                    self.s_lall[:, tt * 8:tt * 8 + 8], self.s_identf[:])
                return r
            sch.op("pe", trl, reads=[b["lall"], b["consts"]], writes=[b["cpb1"]],
                   tag=f"lT_b{bi}g{g}")
            sch.op("act", lambda e, g=g: e.activation(
                out=self.s_gateT[32:40, bi * S + g * 512:bi * S + (g + 1) * 512],
                in_=self.p_cp[0:8, 512:1024], func=AF.Exp),
                reads=[b["cpb1"]], writes=[b["gateT"]], tag=f"exT_b{bi}g{g}")

    def emit_routing(self):
        sch, b = self.sch, self.b
        T = self.nb * S
        sch.op("dve", lambda e: e.tensor_tensor_scan(
            out=self.s_scan[:, 0:T], data0=self.s_gateT[0:8, 0:T],
            data1=self.s_gateT[0:8, 0:T], initial=0.0, op0=ALU.add, op1=ALU.bypass),
            reads=[b["gateT"]], writes=[b["scan"]], tag="scan")
        sch.op("dve", lambda e: e.drain(), tag="scan_drain")
        if self.use_cc:
            sch.op("gp", lambda e: e.dma_start(out=self.d_ccin.ap(),
                                               in_=self.s_scan[0:8, T - 1:T]),
                   reads=[b["scan"]], writes=[b["ccin"]], is_dma=True, tag="ccin")
            cco = sch.op("gp", lambda e: e.collective_compute(
                "AllGather", ALU.bypass, replica_groups=self.cc_rg,
                ins=[self.d_ccin.ap().rearrange("a b -> b a").opt()],
                outs=[self.d_ccout.ap().opt()]),
                reads=[b["ccin"]], writes=[b["ccout"]], is_dma=True, tag="allgather")
            cco.delta = 1
            sch.op("gp", lambda e: e.dma_start(out=self.s_agc[:], in_=self.d_ccout.ap()),
                   reads=[b["ccout"]], writes=[b["agc"]], is_dma=True, tag="ld_agc")
            sch.op("pe", lambda e: e.matmul(self.p_sl[0][0:8, 0:1], self.s_agc[:],
                                            self.s_premask[:], start=True, stop=True),
                   reads=[b["agc"], b["consts"]], writes=[b["sl0"]], tag="cbmm")
            sch.op("dve", lambda e: e.tensor_copy(self.s_cb[:], self.p_sl[0][0:8, 0:1]),
                   reads=[b["sl0"]], writes=[b["cb"]], tag="cbevac")
        else:
            sch.op("dve", lambda e: e.memset(self.s_cb[:], 0.0),
                   writes=[b["cb"]], tag="cbzero")
        sch.op("dve", lambda e: e.tensor_scalar(
            out=self.s_sel[:, 0:T], in0=self.s_scan[:, 0:T], scalar1=self.s_cb[:],
            scalar2=float(CAP + 1), op0=ALU.add, op1=ALU.is_lt),
            reads=[b["scan"], b["cb"]], writes=[b["sel"]], tag="sel")
        sch.op("dve", lambda e: e.tensor_tensor(
            out=self.s_so[32:40, 0:T], in0=self.s_gateT[0:8, 0:T],
            in1=self.s_sel[:, 0:T],
            op=ALU.mult), reads=[b["gateT"], b["sel"]], writes=[b["so"]], tag="so")
        sch.op("dve", lambda e: e.tensor_tensor(
            out=self.s_gateT[0:8, 0:T], in0=self.s_so[32:40, 0:T],
            in1=self.s_gateT[32:40, 0:T],
            op=ALU.mult), reads=[b["so"], b["gateT"]], writes=[b["gateT"]], tag="soe")
        nhalf = T // 1024
        for tb in range(T // 512):
            row, col = (0, tb * 512) if tb < nhalf else (32, (tb - nhalf) * 512)
            ps = self.p_sl[tb % 2]
            pb = b[f"sl{tb % 2}"]
            sch.op("pe", lambda e, ps=ps, tb=tb: e.matmul(
                ps[0:1, 0:512], self.s_ones128bf[0:8, :],
                self.s_gateT[0:8, tb * 512:(tb + 1) * 512], start=True, stop=True),
                reads=[b["gateT"], b["consts"]], writes=[pb], tag=f"nummm{tb}")
            sch.op("dve", lambda e, ps=ps, row=row, col=col: e.tensor_copy(
                self.s_num[row:row + 1, col:col + 512], ps[0:1, 0:512]),
                reads=[pb], writes=[b["num"]], tag=f"nume{tb}")
            sch.op("pe", lambda e, ps=ps, tb=tb: e.matmul(
                ps[0:1, 0:512], self.s_ones128bf[32:40, :],
                self.s_gateT[32:40, tb * 512:(tb + 1) * 512], start=True, stop=True),
                reads=[b["gateT"], b["consts"]], writes=[pb], tag=f"zmm{tb}")
            sch.op("dve", lambda e, ps=ps, row=row, col=col: e.tensor_copy(
                self.s_zrow[row:row + 1, col:col + 512], ps[0:1, 0:512]),
                reads=[pb], writes=[b["zrow"]], tag=f"ze{tb}")
        half = T // 2
        for hf, row in ((0, 0), (1, 32)):
            sch.op("dve", lambda e, row=row, half=half: e.reciprocal(
                self.s_mgt[row:row + 1, 0:half], self.s_zrow[row:row + 1, 0:half]),
                reads=[b["zrow"]], writes=[b["mgt"], b["scan"]], tag=f"zrec{row}")
            sch.op("dve", lambda e, row=row, hf=hf, half=half: e.tensor_tensor(
                out=self.s_mgbf[0:1, hf * half:(hf + 1) * half],
                in0=self.s_num[row:row + 1, 0:half],
                in1=self.s_mgt[row:row + 1, 0:half], op=ALU.mult),
                reads=[b["num"], b["mgt"]], writes=[b["mgbf"]], tag=f"mg{row}")
        self.dbg("mg", self.s_mgbf[:], [1, T], BF16, [b["mgbf"]])
        self.dbg("scan", self.s_scan[:], [8, T], F32, [b["scan"]])

    def fence(self):
        sch = self.sch
        last = {}
        for o in sch.order:
            last[o.stream] = o
        ops = []
        for st in ("pe", "dve", "act", "gp", "sp"):
            deps = [v for k, v in last.items() if k != st]
            fn = (lambda e: e.engine_nop()) if st in ("dve", "gp") else None
            ops.append(sch.op(st, fn, deps=deps, tag=f"fence_{st}",
                              flushq="q_sp" if st == "sp" else None))

    def emit_phase_b(self):
        sch, b = self.sch, self.b
        self.fence()
        wsrc = self.d["wexp"].ap().rearrange("(c p) f -> p c f", p=128)
        sch.op("gp", lambda e, i=wsrc: e.dma_start(out=self.s_wexp[:], in_=i),
               writes=[b["wexp"]], is_dma=True, tag="ld_wexp")
        T = self.nb * S
        banks = [self.p_av[0], self.p_av[1], self.p_sl[0]]
        bnames = ("av0", "av1", "sl0")

        def loads(tb):
            x1, bx1 = self.s_x1tb[tb % 2], b[f"x1tb{tb % 2}"]
            for fc in range(NFC):
                src = bass.AP(self.d_x1sc, fc * 128 * T + tb * 512,
                              [[T, 128], [1, 512]])
                sch.op("sp", lambda e, src=src, x1=x1, fc=fc:
                       e.dma_start(out=x1[:, fc, :], in_=src),
                       reads=[b["x1sc"]], writes=[bx1], is_dma=True,
                       tag=f"ld_x1_{tb}f{fc}")

        def mgb(tb):
            col = tb * 512
            sch.op("pe", lambda e, col=col: e.matmul(
                self.p_sl[3][:, :], self.s_ones1bf[:],
                self.s_mgbf[0:1, col:col + 512], start=True, stop=True),
                reads=[b["mgbf"], b["consts"]], writes=[b["sl3"]], tag=f"mgb{tb}")
            x1, bx1 = self.s_x1tb[tb % 2], b[f"x1tb{tb % 2}"]
            x13 = x1.ap().rearrange("p c t -> p c t")
            mg3 = self.p_sl[3].ap()[:, :].rearrange("p (o t) -> p o t", o=1)
            sch.op("dve", lambda e, x13=x13, mg3=mg3, tb=tb: e.tensor_tensor(
                out=self.s_x1mg.ap()[:, tb % 2], in0=mg3.broadcast_to([128, NFC, 512]),
                in1=x13, op=ALU.mult),
                reads=[bx1, b["sl3"]], writes=[b["x1mg"]], tag=f"x1mg{tb}")

        def emm(tb, fo):
            ps, pb = banks[fo % 3], b[bnames[fo % 3]]
            col = tb * 512
            def mm(e, ps=ps, fo=fo, col=col, tb=tb):
                for c in range(NFC):
                    e.matmul(ps[:, 0:512],
                             self.s_wexp[:, c, fo * 128:(fo + 1) * 128],
                             self.s_x1mg[:, tb % 2, c, :], start=(c == 0),
                             stop=False)
                return e.matmul(ps[:, 0:512],
                                self.s_berowbf[:, fo * 128:(fo + 1) * 128],
                                self.s_mgbf[0:1, col:col + 512],
                                start=False, stop=True)
            sch.op("pe", mm, reads=[b["wexp"], b["x1mg"], b["mgbf"], b["consts"]],
                   writes=[pb], tag=f"exp_mm{tb}f{fo}")

        def xres2(tb, fo):
            ps, pb = banks[fo % 3], b[bnames[fo % 3]]
            x1, bx1 = self.s_x1tb[tb % 2], b[f"x1tb{tb % 2}"]
            sch.op("dve", lambda e, ps=ps, fo=fo, x1=x1: e.tensor_tensor(
                out=self.s_xres2[:, fo, :], in0=ps[:, 0:512], in1=x1[:, fo, :],
                op=ALU.add),
                reads=[pb, bx1], writes=[b["xres2"]], tag=f"xres2_{tb}f{fo}")
            sch.op("act", lambda e, fo=fo: e.activation(
                out=self.s_xsqb[:, fo % 2, :], in_=self.s_xres2[:, fo, :],
                func=AF.Square),
                reads=[b["xres2"]], writes=[b["xsqb"]], tag=f"sq2_{tb}f{fo}")

        def stats2(tb, fo):
            o1 = (tb % 2) * 512
            b1 = b[f"cpb{tb % 2}"]
            sch.op("pe", lambda e, fo=fo, o1=o1: e.matmul(
                self.p_cp[0:1, o1:o1 + 512], self.s_ones128f[:],
                self.s_xres2[:, fo, :],
                start=(fo == 0), stop=(fo == NFC - 1)),
                reads=[b["xres2"], b["consts"]], writes=[b1], tag=f"sum2_{tb}f{fo}")
            sch.op("pe", lambda e, fo=fo, o1=o1: e.matmul(
                self.p_cp[64:65, o1:o1 + 512], self.s_ones128f[:],
                self.s_xsqb[:, fo % 2, :],
                start=(fo == 0), stop=(fo == NFC - 1)),
                reads=[b["xsqb"], b["consts"]], writes=[b1], tag=f"sumsq2_{tb}f{fo}")

        def rowchainA(tb):
            o1 = (tb % 2) * 512
            b1 = b[f"cpb{tb % 2}"]
            sch.op("act", lambda e, o1=o1: e.activation(
                out=self.s_lrows[:], in_=self.p_cp[0:1, o1:o1 + 512],
                func=AF.Identity, scale=1.0 / D),
                reads=[b1], writes=[b["lrows"]], tag=f"mean2_{tb}")
            sch.op("act", lambda e, o1=o1: e.activation(
                out=self.s_ltmp[:], in_=self.p_cp[64:65, o1:o1 + 512],
                func=AF.Identity, scale=1.0 / D),
                reads=[b1], writes=[b["ltmp"]], tag=f"msq2_{tb}")
            sch.op("act", lambda e: e.activation(
                out=self.s_laux[:], in_=self.s_lrows[:], func=AF.Square),
                reads=[b["lrows"]], writes=[b["laux"]], tag=f"m22_{tb}")

        def rowchain2(tb):
            sch.op("dve", lambda e: e.scalar_tensor_tensor(
                out=self.s_lrows[:], in0=self.s_ltmp[:], scalar=EPS,
                in1=self.s_laux[:], op0=ALU.add, op1=ALU.subtract),
                reads=[b["ltmp"], b["laux"]], writes=[b["lrows"]], tag=f"var2_{tb}")
            sch.op("act", lambda e: e.activation(
                out=self.s_ltmp[:], in_=self.s_lrows[:], func=AF.Sqrt),
                reads=[b["lrows"]], writes=[b["ltmp"]], tag=f"sqrt2_{tb}")
            sch.op("dve", lambda e: e.reciprocal(self.s_laux[:], self.s_ltmp[:]),
                   reads=[b["ltmp"]], writes=[b["laux"]], tag=f"rstd2_{tb}")

        def bmean2(tb):
            sch.op("pe", lambda e: e.matmul(
                self.p_sl[2][:, :], self.s_ones1f[:],
                self.s_lrows[:], start=True, stop=True),
                reads=[b["lrows"], b["consts"]], writes=[b["sl2"]],
                tag=f"bmean2_{tb}")

        def brstd2(tb):
            sch.op("pe", lambda e: e.matmul(
                self.p_sl[1][:, :], self.s_ones1f[:],
                self.s_laux[:], start=True, stop=True),
                reads=[b["laux"], b["consts"]], writes=[b["sl1"]],
                tag=f"brstd2_{tb}")

        def sub2(tb, fo):
            sch.op("dve", lambda e, fo=fo: e.scalar_tensor_tensor(
                out=self.s_r[:] if fo % 2 == 0 else self.s_oh[:],
                in0=self.p_sl[2][:, :],
                scalar=-1.0, in1=self.s_xres2[:, fo, :],
                op0=ALU.mult, op1=ALU.add),
                reads=[b["xres2"], b["sl2"]],
                writes=[b["r"] if fo % 2 == 0 else b["oh"]],
                tag=f"sub2_{tb}f{fo}")

        def x2(tb, fo):
            bi, qb = tb // 4, tb % 4
            sch.op("dve", lambda e, fo=fo: e.tensor_tensor(
                out=self.s_x2o[:, fo, :], in0=self.p_sl[1][:, :],
                in1=self.s_r[:] if fo % 2 == 0 else self.s_oh[:], op=ALU.mult),
                reads=[b["r"] if fo % 2 == 0 else b["oh"], b["sl1"]],
                writes=[b["x2o"]], tag=f"x2_{tb}f{fo}")
            dst = bass.AP(self.d_out, bi * D * S + fo * 128 * S + qb * 512,
                          [[S, 128], [1, 512]])
            sch.op("act", lambda e, dst=dst, fo=fo:
                   e.dma_start(out=dst, in_=self.s_x2o[:, fo, :]),
                   reads=[b["x2o"]], is_dma=True, tag=f"st_x2_{tb}f{fo}")

        NTB = T // 512
        for tb in range(NTB):
            loads(tb)
            mgb(tb)
            if tb > 0:
                rowchainA(tb - 1)
            emm(tb, 0)
            if tb > 0:
                bmean2(tb - 1)
                rowchain2(tb - 1)
            emm(tb, 1)
            if tb > 0:
                brstd2(tb - 1)
                sch.op("act", lambda e: e.nop(), reads=[b["x2o"]],
                       writes=[b["x2o"]], flushq="q_act", tag=f"x2flush_{tb}")
            for fo in range(NFC):
                if tb > 0:
                    sub2(tb - 1, fo)
                    if fo > 0:
                        x2(tb - 1, fo - 1)
                xres2(tb, fo)
                if fo + 2 < NFC:
                    emm(tb, fo + 2)
                if fo > 0:
                    stats2(tb, fo - 1)
            if tb > 0:
                x2(tb - 1, NFC - 1)
            stats2(tb, NFC - 1)
        tb = NTB - 1
        rowchainA(tb)
        bmean2(tb)
        rowchain2(tb)
        brstd2(tb)
        sch.op("act", lambda e: e.nop(), reads=[b["x2o"]],
               writes=[b["x2o"]], flushq="q_act", tag="x2flush_last")
        for fo in range(NFC):
            sub2(tb, fo)
            if fo > 0:
                x2(tb, fo - 1)
        x2(tb, NFC - 1)

    def realize(self):
        sch = self.sch
        counts = sch.finalize()
        nc = self.nc
        streams = {}
        for o in sch.order:
            streams.setdefault(o.stream, []).append(o)

        with contextlib.ExitStack() as es:
            sems = {}
            for name in ["s_pe", "s_dve", "s_act", "s_gp", "s_sp", "q_sp", "q_gp",
                         "q_act", "s_cc"]:
                sems[name] = es.enter_context(nc.semaphore(name))
            block = es.enter_context(nc.Block())

            def replay(eng, ops, pad_raw=False, chain_dma=None):
                watermark = {}
                pos = {}
                slot = 0
                for o in ops:
                    if o.flushq is not None and o.flushval:
                        if watermark.get(o.flushq, 0) < o.flushval:
                            eng.wait_ge(sems[o.flushq], o.flushval)
                            watermark[o.flushq] = o.flushval
                    if chain_dma and o.is_dma and o.sem == chain_dma \
                            and o.incval is not None and o.incval > 16:
                        v = o.incval - 16
                        if watermark.get(o.sem, 0) < v:
                            eng.wait_ge(sems[o.sem], v)
                            watermark[o.sem] = v
                    waited_pe = False
                    for d in o.deps:
                        if d.stream == o.stream and not d.is_dma and not o.is_dma:
                            continue
                        if d.incval is None:
                            continue
                        if watermark.get(d.sem, 0) >= d.incval:
                            continue
                        eng.wait_ge(sems[d.sem], d.incval)
                        watermark[d.sem] = d.incval
                        if d.sem == "s_pe":
                            waited_pe = True
                    if pad_raw and waited_pe and o.fn is not None:
                        for _ in range(6):
                            eng.engine_nop()
                        slot += 6
                    if pad_raw and o.fn is not None:
                        need = 0
                        for d in o.deps:
                            if d.stream == o.stream and not d.is_dma and id(d) in pos:
                                need = max(need, pos[id(d)] + 3 - slot)
                        for _ in range(need):
                            eng.engine_nop()
                            slot += 1
                    if o.fn is None:
                        continue
                    inst = o.fn(eng)
                    pos[id(o)] = slot
                    slot += 1
                    if o.incval is not None:
                        assert inst is not None, o.tag
                        inst.then_inc(sems[o.sem], o.delta)

            @block.tensor
            def _(eng):
                replay(eng, streams.get("pe", []))

            @block.vector
            def _(eng):
                replay(eng, streams.get("dve", []), pad_raw=True)

            @block.scalar
            def _(eng):
                replay(eng, streams.get("act", []))

            @block.gpsimd
            def _(eng):
                replay(eng, streams.get("gp", []), chain_dma="q_gp")

            @block.sync
            def _(eng):
                replay(eng, streams.get("sp", []))
        return nc


def build(nb=NB, npair=NPAIR, use_cc=True, debug=(), skip=()):
    bld = Builder(nb=nb, npair=npair, use_cc=use_cc, debug=debug, skip=skip)
    bld.emit_all()
    nc = bld.realize()
    return nc, bld


def kernel(**inputs):
    in_maps = stage(inputs)
    nc, _ = build()
    res = run_bass_kernel_spmd(nc, in_maps, core_ids=list(range(NCORES)))
    out = np.empty((32, 2048, 1024), dtype=np.float32)
    for c in range(NCORES):
        out[c * NB:(c + 1) * NB] = res.results[c]["x2t"].transpose(0, 2, 1)
    return out


# version 28
# speedup vs baseline: 1.0319x; 1.0012x over previous
import contextlib
import numpy as np
import ml_dtypes
import concourse.bass as bass
import concourse.mybir as mybir
from concourse.bass_utils import run_bass_kernel_spmd
from concourse.replica_groups import maybe_share_collective_output_space

F32 = mybir.dt.float32
BF16 = mybir.dt.bfloat16
AF = mybir.ActivationFunctionType
ALU = mybir.AluOpType
AX = mybir.AxisListType
BF = ml_dtypes.bfloat16

NCORES = 8
NB = 4
S = 2048
D = 1024
NH = 16
HD = 64
NPAIR = 8
NQB = 4
NKT = 16
NFC = 8
CAP = 100
EPS = 1e-5
NEG = -1.0e30


class _Op:
    __slots__ = ("stream", "fn", "deps", "is_dma", "delta", "sem", "incval",
                 "needed", "idx", "tag", "flushq", "flushval")

    def __init__(self, stream, fn, is_dma, tag):
        self.flushq = None
        self.flushval = None
        self.stream = stream
        self.fn = fn
        self.deps = []
        self.is_dma = is_dma
        self.delta = 16 if is_dma else 1
        self.sem = None
        self.incval = None
        self.needed = False
        self.idx = None
        self.tag = tag


class Buf:
    __slots__ = ("name", "last_w", "readers", "rng", "overlaps")

    def __init__(self, name, rng=None):
        self.name = name
        self.last_w = None
        self.readers = []
        self.rng = rng
        self.overlaps = []


class Sched:
    def __init__(self):
        self.order = []
        self.bufs = []

    def link_overlaps(self):
        for i, a in enumerate(self.bufs):
            if a.rng is None:
                continue
            for c in self.bufs[i + 1:]:
                if c.rng is None:
                    continue
                if a.rng[0] < c.rng[1] and c.rng[0] < a.rng[1]:
                    a.overlaps.append(c)
                    c.overlaps.append(a)

    def op(self, stream, fn, reads=(), writes=(), deps=(), is_dma=False, tag="",
           flushq=None):
        o = _Op(stream, fn, is_dma, tag)
        o.flushq = flushq
        o.idx = len(self.order)
        dd = list(deps)
        for b in reads:
            if b.last_w is not None:
                dd.append(b.last_w)
        for b in writes:
            for bb in [b] + b.overlaps:
                if bb.last_w is not None:
                    dd.append(bb.last_w)
                dd.extend(bb.readers)
        seen = set()
        for d in dd:
            if d is None or id(d) in seen or d is o:
                continue
            seen.add(id(d))
            assert d.idx < o.idx
            o.deps.append(d)
        for b in reads:
            b.readers.append(o)
        for b in writes:
            b.last_w = o
            b.readers = []
            for bb in b.overlaps:
                bb.last_w = None
                bb.readers = []
        self.order.append(o)
        return o

    def finalize(self):
        for o in self.order:
            if o.tag == "allgather":
                o.sem = "s_cc"
            elif o.is_dma:
                o.sem = {"sp": "q_sp", "act": "q_act"}.get(o.stream, "q_gp")
            else:
                o.sem = "s_" + o.stream
        for o in self.order:
            for d in o.deps:
                if d.stream == o.stream and not d.is_dma and not o.is_dma:
                    continue
                d.needed = True
        counts = {}
        for o in self.order:
            if o.flushq is not None:
                o.flushval = counts.get(o.flushq, 0)
            if o.needed or o.is_dma:
                counts[o.sem] = counts.get(o.sem, 0) + o.delta
                o.incval = counts[o.sem]
        return counts



def _consts():
    identf = np.eye(128, dtype=np.float32)
    identbf = np.eye(128, dtype=BF)
    ones1f = np.ones((1, 128), dtype=np.float32)
    ones1bf = np.ones((1, 128), dtype=BF)
    ones128f = np.ones((128, 1), dtype=np.float32)
    ones128bf = np.ones((128, 1), dtype=BF)
    k = np.arange(128)[:, None]
    mtq = np.zeros((128, 1280), dtype=np.float32)
    mqk = np.zeros((128, 1280), dtype=np.float32)
    for j in range(4):
        off, w = j * (j + 1) // 2 * 128, (j + 1) * 128
        q5 = np.arange(w)[None, :]
        mtq[:, off:off + w] = np.where(q5 >= j * 128 + k, 0.0, NEG)
        mqk[:, off:off + w] = np.where(q5 <= j * 128 + k, 0.0, NEG)
    mqk2a = np.concatenate([mqk[0:64], mqk[0:64]], axis=0)
    mqk2b = np.concatenate([mqk[64:128], mqk[64:128]], axis=0)
    return {
        "identf": identf, "identbf": identbf,
        "ones1f": ones1f, "ones1bf": ones1bf,
        "ones128f": ones128f, "ones128bf": ones128bf,
        "maskstq": mtq.astype(BF),
        "masksqk2a": mqk2a.astype(BF), "masksqk2b": mqk2b.astype(BF),
    }


def stage(inputs):
    x0 = np.asarray(inputs["L0_attn_inputs"], dtype=np.float32)
    wqkv = np.asarray(inputs["L0_attn_Wqkv_w"], dtype=np.float32)
    bqkv = np.asarray(inputs["L0_attn_Wqkv_b"], dtype=np.float32)
    wo = np.asarray(inputs["L0_attn_Wo_w"], dtype=np.float32)
    bo = np.asarray(inputs["L0_attn_Wo_b"], dtype=np.float32)
    ghost = np.asarray(inputs["L0_attn_ghost"], dtype=np.float32)
    wg = np.asarray(inputs["L0_ffn_w_gate"], dtype=np.float32)
    we = np.asarray(inputs["L0_ffn_w_expert"], dtype=np.float32)
    be = np.asarray(inputs["L0_ffn_b_expert"], dtype=np.float32)

    shared = dict(_consts())
    shared["wqkvt"] = np.ascontiguousarray(wqkv.T)
    shared["wqkvb_pc"] = np.ascontiguousarray(bqkv.reshape(24, 128).T)
    shared["bv_row"] = np.ascontiguousarray(bqkv[2048:].reshape(1, 1024))
    shared["wot"] = np.ascontiguousarray(wo.T)
    shared["wob_pc"] = np.ascontiguousarray(bo.reshape(8, 128).T)
    shared["wgate3"] = np.ascontiguousarray(wg.reshape(8, 128, 8).transpose(1, 0, 2))
    shared["wexp"] = np.ascontiguousarray(we)
    shared["bexp_row"] = np.ascontiguousarray(be.reshape(1, 1024))
    gh2 = np.empty((128, 8), dtype=np.float32)
    gh2[0:64, :] = ghost[0::2][None, :]
    gh2[64:128, :] = ghost[1::2][None, :]
    shared["ghostb"] = gh2

    in_maps = []
    for c in range(NCORES):
        m = dict(shared)
        xb = x0[c * NB:(c + 1) * NB]
        m["x0t"] = np.ascontiguousarray(xb.transpose(0, 2, 1))
        m["premask"] = (np.arange(8) < c).astype(np.float32).reshape(8, 1)
        in_maps.append(m)
    return in_maps



class Builder:
    def __init__(self, nb=NB, npair=NPAIR, use_cc=True, debug=(), skip=()):
        self.skip = set(skip)
        self.nb = nb
        self.npair = npair
        self.use_cc = use_cc
        self.debug = set(debug)
        self.dbg_outs = {}
        self.nc = bass.Bass("TRN2", target_bir_lowering=False)
        self.sch = Sched()
        self.sb_ranges = {}
        self._sb_off = 16512
        self._mk_tensors()

    def _sb(self, name, shape, dtype, off=None):
        nc = self.nc
        per = int(np.prod(shape[1:])) * mybir.dt.size(dtype)
        if off is None:
            off = self._sb_off
            self._sb_off = (off + per + 31) & ~31
        h = nc.alloc_sbuf_tensor_at(name, shape, dtype, offset=off)
        self.sb_ranges[name] = (off, off + per)
        return h

    def _mk_tensors(self):
        nc = self.nc
        self.d = {}
        for name, shp, dt in [
            ("x0t", [NB, D, S], F32), ("wqkvt", [D, 3 * D], F32),
            ("wqkvb_pc", [128, 24], F32), ("bv_row", [1, D], F32),
            ("wot", [D, D], F32), ("wob_pc", [128, 8], F32),
            ("wgate3", [128, 8, 8], F32), ("wexp", [D, D], F32),
            ("bexp_row", [1, D], F32), ("ghostb", [128, 8], F32),
            ("identf", [128, 128], F32), ("identbf", [128, 128], BF16),
            ("ones1f", [1, 128], F32), ("ones1bf", [1, 128], BF16),
            ("ones128f", [128, 1], F32), ("ones128bf", [128, 1], BF16),
            ("maskstq", [128, 1280], BF16), ("masksqk2a", [128, 1280], BF16),
            ("masksqk2b", [128, 1280], BF16),
            ("premask", [8, 1], F32),
        ]:
            self.d[name] = nc.dram_tensor(name, shp, dt, kind="ExternalInput")
        self.d_out = nc.dram_tensor("x2t", [NB, D, S], F32, kind="ExternalOutput")
        self.d_x1sc = nc.dram_tensor("x1sc", [D, NB * S], BF16)
        if self.use_cc:
            rg = [list(range(NCORES))]
            space = maybe_share_collective_output_space("AllGather", rg)
            self.cc_rg = rg
            self.d_ccin = nc.dram_tensor("ccin", [8, 1], F32)
            self.d_ccout = nc.dram_tensor("ccout", [8, 8], F32, addr_space=space)

        self.s_identf = self._sb("identf", [128, 128], F32)
        self.s_identbf = self._sb("identbf", [128, 128], BF16)
        self.s_ones1f = self._sb("ones1f", [1, 128], F32)
        self.s_ones1bf = self._sb("ones1bf", [1, 128], BF16)
        self.s_ones128f = self._sb("ones128f", [128, 1], F32)
        self.s_ones128bf = self._sb("ones128bf", [128, 1], BF16)
        self.s_mtq = self._sb("maskstq", [128, 1280], BF16)
        self.s_mqk2 = [self._sb("masksqk2a", [128, 1280], BF16),
                       self._sb("masksqk2b", [128, 1280], BF16)]
        self.s_ghostb = self._sb("ghostb", [128, 8], F32)
        self.s_bqkv = self._sb("bqkv", [128, 24], F32)
        self.s_bo = self._sb("bo", [128, 8], F32)
        self.s_wg = self._sb("wg", [128, 8, 8], F32)
        self.s_wgbf = self._sb("wgbf", [128, 8, 8], BF16)
        self.s_bvrowbf = self._sb("bvrowbf", [1, D], BF16)
        self.s_berowbf = self._sb("berowbf", [1, D], BF16)
        self.s_premask = self._sb("premask", [8, 1], F32)
        self.s_agc = self._sb("agc", [8, 8], F32)
        self.s_cb = self._sb("cb", [8, 1], F32)
        self.s_ones33 = self._sb("ones33", [33, 128], F32)
        self.s_ones33bf = self._sb("ones33bf", [33, 128], BF16)
        self.s_lrows = self._sb("lrows", [1, 512], F32)
        self.s_ltmp = self._sb("ltmp", [1, 512], F32)
        self.s_laux = self._sb("laux", [1, 512], F32)
        self.s_gateT = self._sb("gateT", [40, NB * S], BF16)
        base = self._sb_off
        o = base
        def at(name, shape, dtype):
            nonlocal o
            h = self._sb(name, shape, dtype, off=o)
            o += (int(np.prod(shape[1:])) * mybir.dt.size(dtype) + 31) & ~31
            return h
        self.s_x0t = at("x0t", [128, NQB, NFC, 512], BF16)
        self.s_rowtmp = self._sb("rowtmp", [1, D], F32, off=base)
        self.s_wqp = [at(f"wqp{i}", [128, NFC, 3, 128], BF16) for i in range(2)]
        self.s_qd = [at(f"qd{h}", [128, S], BF16) for h in range(2)]
        self.s_kbd = [at(f"kbd{h}", [128, S], BF16) for h in range(2)]
        self.s_qbd2 = [at(f"qbd2{h}", [128, S], BF16) for h in range(2)]
        self.s_kt = at("kt", [128, S], BF16)
        self.s_vp = at("vp", [128, NKT, 130], BF16)
        self.s_vt = at("vt", [128, S], BF16)
        self.s_et = [[at(f"et{h}{p}", [128, 512], BF16) for p in range(2)] for h in range(2)]
        self.s_a2n = [at(f"a2n{p}", [128, S], BF16) for p in range(NPAIR)]
        self.s_wot = at("wot", [128, NPAIR, D], BF16)
        self.s_xres = at("xres", [128, NFC, 512], F32)
        self.s_xsq = at("xsq", [128, 2, 512], F32)
        self.s_x1qb = at("x1qb", [128, NFC, 512], BF16)
        self.s_lall = at("lall", [128, 512], F32)
        self.s_oh = at("oh", [128, 512], F32)
        self.s_sig = at("sig", [33, S], BF16)
        self.s_mpair = at("mpair", [128, 32], F32)
        self.s_mprt = at("mprt", [128, 48], F32)
        self.s_emg = at("emg", [128, 64], F32)
        self.s_emgbf = at("emgbf", [128, 32], BF16)
        self.s_r = at("r", [128, 512], F32)
        class1_end = o
        o = base
        self.s_scan = at("scan", [8, NB * S], F32)
        self.s_sel = at("sel", [8, NB * S], BF16)
        self.s_so = at("so", [40, NB * S], BF16)
        self.s_num = at("num", [33, NB * S // 2], F32)
        self.s_zrow = at("zrow", [33, NB * S // 2], F32)
        post_end = o
        self.s_mgbf = at("mgbf", [1, NB * S], BF16)
        cls4_o = o
        self.s_mgt = self._sb("mgt", [33, NB * S // 2], F32,
                              off=base)
        o = cls4_o
        cls4_end = o
        o = base
        self.s_wexp = at("wexp", [128, NFC, D], BF16)
        self.s_x1tb = [at(f"x1tb{i}", [128, NFC, 512], BF16) for i in range(2)]
        self.s_x1mg = at("x1mg", [128, 2, NFC, 512], BF16)
        self.s_xres2 = at("xres2", [128, NFC, 512], F32)
        self.s_x2o = at("x2o", [128, NFC, 512], F32)
        self.s_xsqb = at("xsqb", [128, 2, 512], F32)
        cls5_end = o
        assert cls5_end <= post_end, (cls5_end, post_end)
        self.sbuf_peak = max(class1_end, cls4_end, cls5_end)
        self.sbuf_regions = dict(base=base, class1=class1_end, class3=post_end,
                                 cls4=cls4_end, cls5=cls5_end)
        assert self.sbuf_peak <= 229376, (self.sbuf_peak, self.sbuf_regions)

        self.p_cp = nc.alloc_psum_tensor("cp", [128, 1024], F32)
        self.p_sl = [nc.alloc_psum_tensor(f"sl{i}", [128, 512], F32) for i in range(4)]
        self.p_av = [nc.alloc_psum_tensor(f"av{h}", [128, 512], F32) for h in range(2)]

        rng = self.sb_ranges
        x0lo = rng["x0t"][0]
        x0chunk = NFC * 512 * 2
        name2rng = {
            "wqp0": rng["wqp0"], "wqp1": rng["wqp1"],
            "qd0": rng["qd0"], "qd1": rng["qd1"],
            "kbd0": rng["kbd0"], "kbd1": rng["kbd1"],
            "qbd20": rng["qbd20"], "qbd21": rng["qbd21"],
            "kt": rng["kt"], "vp": rng["vp"], "vt": rng["vt"],
            "et00": rng["et00"], "et01": rng["et01"],
            "et10": rng["et10"], "et11": rng["et11"],
            "wot": rng["wot"], "xres": rng["xres"],
            "xsq": rng["xsq"], "x1qb": rng["x1qb"], "lall": rng["lall"],
            "oh": rng["oh"], "sig": rng["sig"], "mpair": rng["mpair"], "mprt": rng["mprt"],
            "emg": rng["emg"], "emgbf": rng["emgbf"], "r": rng["r"],
            "gateT": rng["gateT"], "scan": rng["scan"],
            "sel": rng["sel"], "so": rng["so"], "num": rng["num"],
            "zrow": rng["zrow"], "mgbf": rng["mgbf"], "mgt": rng["mgt"],
            "wexp": rng["wexp"], "x1tb0": rng["x1tb0"], "x1tb1": rng["x1tb1"],
            "x1mg": rng["x1mg"], "xres2": rng["xres2"], "x2o": rng["x2o"],
            "xsqb": rng["xsqb"], "lrows": rng["lrows"], "ltmp": rng["ltmp"],
            "laux": rng["laux"], "rowtmp": rng["rowtmp"],
            "consts": (16512, rng["gateT"][0]),
        }
        for qb in range(NQB):
            name2rng[f"x0t{qb}"] = (x0lo + qb * x0chunk, x0lo + (qb + 1) * x0chunk)
        eglo = rng["emgbf"][0]
        for q in range(NQB):
            name2rng[f"emgbf{q}"] = (eglo + q * 16, eglo + (q + 1) * 16)
        self.b = {}
        for n in ["cpb0", "cpb1", "sl0", "sl1", "sl2", "sl3",
                  "av0", "av1", "x1sc", "ccin", "ccout", "agc", "cb"]:
            self.b[n] = Buf(n)
            self.sch.bufs.append(self.b[n])
        for n, r in name2rng.items():
            self.b[n] = Buf(n, r)
            self.sch.bufs.append(self.b[n])
        self.b_a2n = []
        for p in range(NPAIR):
            bb = Buf(f"a2n{p}", rng[f"a2n{p}"])
            self.b_a2n.append(bb)
            self.sch.bufs.append(bb)
        self.sch.link_overlaps()

    def dbg(self, name, src_ap, shape, dtype, reads):
        if name not in self.debug:
            return
        t = self.nc.dram_tensor("dbg_" + name, shape, dtype, kind="ExternalOutput")
        self.dbg_outs[name] = name
        self.sch.op("sp", lambda e, t=t, a=src_ap: e.dma_start(out=t.ap(), in_=a),
                    reads=reads, is_dma=True, tag="dbg_" + name)

    def emit_all(self):
        sch, b = self.sch, self.b
        nc = self.nc

        def ld(sb, dname, cast=False, tag=""):
            src = self.d[dname].ap()
            stream = "gp" if cast else "sp"
            return sch.op(stream, lambda e, o=sb, i=src: e.dma_start(out=o[:], in_=i),
                          writes=[b["consts"]], is_dma=True, tag="ld_" + dname)

        for nm, sb in [("identf", self.s_identf), ("identbf", self.s_identbf),
                       ("ones1f", self.s_ones1f), ("ones1bf", self.s_ones1bf),
                       ("ones128f", self.s_ones128f), ("ones128bf", self.s_ones128bf),
                       ("maskstq", self.s_mtq), ("masksqk2a", self.s_mqk2[0]),
                       ("masksqk2b", self.s_mqk2[1]),
                       ("ghostb", self.s_ghostb), ("wqkvb_pc", self.s_bqkv),
                       ("wob_pc", self.s_bo), ("wgate3", self.s_wg),
                       ("premask", self.s_premask)]:
            ld(sb, nm)
        sch.op("sp", lambda e: e.dma_start(out=self.s_rowtmp[:],
                                           in_=self.d["bv_row"].ap()),
               writes=[b["rowtmp"]], is_dma=True, tag="ld_bv")
        sch.op("dve", lambda e: e.tensor_copy(self.s_bvrowbf[:], self.s_rowtmp[:]),
               reads=[b["rowtmp"]], writes=[b["consts"]], tag="cast_bv")
        sch.op("sp", lambda e: e.dma_start(out=self.s_rowtmp[:],
                                           in_=self.d["bexp_row"].ap()),
               reads=[b["consts"]], writes=[b["rowtmp"]], is_dma=True, tag="ld_be")
        sch.op("dve", lambda e: e.tensor_copy(self.s_berowbf[:], self.s_rowtmp[:]),
               reads=[b["rowtmp"]], writes=[b["consts"]], tag="cast_be")
        sch.op("dve", lambda e: e.memset(self.s_ones33[0:1, :], 1.0),
               writes=[b["consts"]], tag="ones33a")
        sch.op("dve", lambda e: e.memset(self.s_ones33[32:33, :], 1.0),
               writes=[b["consts"]], tag="ones33b")
        sch.op("dve", lambda e: e.memset(self.s_ones33bf[0:1, :], 1.0),
               writes=[b["consts"]], tag="ones33c")
        sch.op("dve", lambda e: e.memset(self.s_ones33bf[32:33, :], 1.0),
               writes=[b["consts"]], tag="ones33d")
        sch.op("dve", lambda e: e.tensor_copy(self.s_wgbf[:], self.s_wg[:]),
               reads=[b["consts"]], writes=[b["consts"]], tag="cast_wg")
        for nm, sb in [("kbd0", self.s_kbd[0]), ("kbd1", self.s_kbd[1]),
                       ("qbd20", self.s_qbd2[0]), ("qbd21", self.s_qbd2[1])]:
            sch.op("dve", lambda e, sb=sb: e.memset(sb[:], 0.0),
                   writes=[b[nm]], tag=f"z_{nm}")

        for bi in range(self.nb):
            self.emit_batch(bi)
        self.emit_routing()
        self.emit_phase_b()

        last_sp = [o for o in sch.order if o.is_dma and o.stream == "sp"][-1]
        last_act = [o for o in sch.order if o.is_dma and o.stream == "act"][-1]
        sch.op("sp", None, deps=[last_sp, last_act], tag="final_wait")
        sch.op("dve", lambda e: e.engine_nop(), deps=[last_act, last_sp],
               tag="final_wait2")

    def _x0_chunk_load(self, bi, qb):
        sch, b = self.sch, self.b
        src = self.d["x0t"].ap()[bi].rearrange(
            "(c p) t -> p c t", p=128)[:, :, qb * 512:(qb + 1) * 512]
        sch.op("gp", lambda e, i=src, qb=qb: e.dma_start(
            out=self.s_x0t[:, qb, :, :], in_=i),
            writes=[b[f"x0t{qb}"]], is_dma=True, tag=f"ld_x0t_b{bi}c{qb}")

    def _wq_load(self, bi, pair):
        sch, b = self.sch, self.b
        wq = self.s_wqp[pair % 2]
        bwq = b[f"wqp{pair % 2}"]
        for blk in range(3):
            off = blk * 1024 + pair * 128
            srcw = bass.AP(self.d["wqkvt"], off,
                           [[3072, 128], [128 * 3072, 8], [1, 128]])
            dst = wq[:, :, blk, :]
            sch.op("gp", lambda e, o=dst, i=srcw: e.dma_start(out=o, in_=i),
                   writes=[bwq], is_dma=True, tag=f"ld_wq_b{bi}p{pair}k{blk}")

    def emit_batch(self, bi):
        sch, b = self.sch, self.b
        if bi == 0:
            for qb in range(NQB):
                self._x0_chunk_load(bi, qb)
            wsrc = self.d["wot"].ap().rearrange("(c p) f -> p c f", p=128)
            sch.op("gp", lambda e, i=wsrc: e.dma_start(out=self.s_wot[:], in_=i),
                   writes=[b["wot"]], is_dma=True, tag=f"ld_wot_b{bi}")

        for pair in range(self.npair):
            self.emit_pair(bi, pair, skip_wload=(bi > 0 and pair < 2))
        if bi + 1 < self.nb:
            self._wq_load(bi + 1, 0)
            self._wq_load(bi + 1, 1)
        self.emit_wo_ln1(bi)
        self.emit_gate_batch(bi)

    def emit_pair(self, bi, pair, skip_wload=False):
        sch, b = self.sch, self.b
        wq = self.s_wqp[pair % 2]
        bwq = b[f"wqp{pair % 2}"]
        if not skip_wload:
            self._wq_load(bi, pair)

        for blk in range(2):
            for qb in range(NQB):
                ps = self.p_sl[qb][:, :]
                def mmgrp(e, ps=ps, wq=wq, blk=blk, qb=qb):
                    r = None
                    for c in range(NFC):
                        r = e.matmul(ps, wq[:, c, blk, :],
                                     self.s_x0t[:, qb, c, :],
                                     start=(c == 0), stop=(c == NFC - 1))
                    return r
                sch.op("pe", mmgrp, reads=[bwq, b[f"x0t{qb}"]], writes=[b[f"sl{qb}"]],
                       tag=f"qkproj_b{bi}p{pair}x{blk}q{qb}")
                if blk == 0:
                    for hl in range(2):
                        r0 = 64 * hl
                        dst = self.s_qd[hl]
                        bias = self.s_bqkv[r0:r0 + 64, pair:pair + 1]
                        psh = self.p_sl[qb].ap()[r0:r0 + 64, :]
                        sch.op("act", lambda e, psh=psh, dst=dst, qb=qb, bias=bias,
                               r0=r0: e.activation(
                                   out=dst[r0:r0 + 64, qb * 512:(qb + 1) * 512],
                                   in_=psh, func=AF.Identity, bias=bias, scale=1.0),
                               reads=[b[f"sl{qb}"], b["consts"]],
                               writes=[b[f"qd{hl}"]],
                               tag=f"qevac_b{bi}p{pair}h{hl}q{qb}")
                else:
                    bias = self.s_bqkv[:, 8 + pair:9 + pair]
                    sch.op("act", lambda e, ps=ps, qb=qb, bias=bias:
                           e.activation(out=self.s_kt[:, qb * 512:(qb + 1) * 512],
                                        in_=ps, func=AF.Identity, bias=bias,
                                        scale=1.0),
                           reads=[b[f"sl{qb}"], b["consts"]], writes=[b["kt"]],
                           tag=f"kevac_b{bi}p{pair}q{qb}")
            def half_view(sb, p0, h):
                return sb.ap()[p0:p0 + 64, :].rearrange(
                    "p (k h c) -> p k h c", h=2, c=64)[:, :, h:h + 1, :]
            if blk == 0:
                def qdma(dst, src, srcbuf, dstbuf, tag):
                    sch.op("sp", lambda e, o=dst, i=src: e.dma_start(out=o, in_=i),
                           reads=[b[srcbuf]], writes=[b[dstbuf]],
                           is_dma=True, tag=tag)
                t = f"_b{bi}p{pair}"
                qdma(self.s_qd[0].ap()[64:128, :], self.s_qd[0].ap()[0:64, :],
                     "qd0", "qd0", "bq_qd0" + t)
                qdma(self.s_qd[1].ap()[0:64, :], self.s_qd[1].ap()[64:128, :],
                     "qd1", "qd1", "bq_qd1" + t)
                qdma(half_view(self.s_qbd2[0], 0, 0), half_view(self.s_qd[0], 0, 0),
                     "qd0", "qbd20", "bq_q20a" + t)
                qdma(half_view(self.s_qbd2[0], 64, 1), half_view(self.s_qd[1], 64, 0),
                     "qd1", "qbd20", "bq_q20b" + t)
                qdma(half_view(self.s_qbd2[1], 0, 0), half_view(self.s_qd[0], 0, 1),
                     "qd0", "qbd21", "bq_q21a" + t)
                qdma(half_view(self.s_qbd2[1], 64, 1), half_view(self.s_qd[1], 64, 1),
                     "qd1", "qbd21", "bq_q21b" + t)
                sch.op("sp", lambda e: e.nop(),
                       reads=[b["qd0"], b["qd1"]],
                       writes=[b["qd0"], b["qd1"], b["qbd20"], b["qbd21"]],
                       flushq="q_sp", tag=f"qflush_b{bi}p{pair}")
            else:
                def kdma(dst, src, dstbuf, tag):
                    sch.op("gp", lambda e, o=dst, i=src: e.dma_start(out=o, in_=i),
                           reads=[b["kt"]], writes=[b[dstbuf]],
                           is_dma=True, tag=tag)
                t = f"_b{bi}p{pair}"
                kdma(half_view(self.s_kbd[0], 0, 0), half_view(self.s_kt, 0, 0),
                     "kbd0", "bk_k0a" + t)
                kdma(half_view(self.s_kbd[0], 64, 1), half_view(self.s_kt, 0, 1),
                     "kbd0", "bk_k0b" + t)
                kdma(half_view(self.s_kbd[1], 0, 0), half_view(self.s_kt, 64, 0),
                     "kbd1", "bk_k1a" + t)
                kdma(half_view(self.s_kbd[1], 64, 1), half_view(self.s_kt, 64, 1),
                     "kbd1", "bk_k1b" + t)
                sch.op("gp", lambda e: e.engine_nop(),
                       reads=[b["kt"]],
                       writes=[b["kbd0"], b["kbd1"]],
                       flushq="q_gp", tag=f"kflush_b{bi}p{pair}")

        vones_ap = bass.AP(self.s_vp, 64, [[NKT * 130, 128], [130, NKT], [65, 2]])
        sch.op("dve", lambda e, a=vones_ap: e.memset(a, 1.0),
               writes=[b["vp"]], tag=f"vones_b{bi}p{pair}")
        for qb in range(NQB):
            ps = self.p_sl[qb][:, :]
            def vmm(e, ps=ps, wq=wq, qb=qb):
                r = None
                for c in range(NFC):
                    r = e.matmul(ps, wq[:, c, 2, :],
                                 self.s_x0t[:, qb, c, :],
                                 start=(c == 0), stop=(c == NFC - 1))
                return r
            sch.op("pe", vmm, reads=[bwq, b[f"x0t{qb}"]], writes=[b[f"sl{qb}"]],
                   tag=f"vproj_b{bi}p{pair}q{qb}")
            bias = self.s_bqkv[:, 16 + pair:17 + pair]
            sch.op("act", lambda e, ps=ps, qb=qb, bias=bias:
                   e.activation(out=self.s_vt[:, qb * 512:(qb + 1) * 512], in_=ps,
                                func=AF.Identity, bias=bias, scale=1.0),
                   reads=[b[f"sl{qb}"], b["consts"]], writes=[b["vt"]],
                   tag=f"vevac_b{bi}p{pair}q{qb}")
        for tt in range(NKT):
            psb = self.p_sl[tt % 4].ap()[:, 0:64].bitcast(BF16)
            sch.op("pe", lambda e, psb=psb, tt=tt: e.transpose(
                psb, self.s_vt[:, tt * 128:(tt + 1) * 128], self.s_identbf[:]),
                reads=[b["vt"], b["consts"]], writes=[b[f"sl{tt % 4}"]],
                tag=f"vtr_b{bi}p{pair}t{tt}")
            dst = bass.AP(self.s_vp, tt * 130, [[NKT * 130, 128], [65, 2], [1, 64]])
            sch.op("dve", lambda e, psb=psb, dst=dst:
                   e.tensor_copy(dst, psb.rearrange("p (h d) -> p h d", h=2)),
                   reads=[b[f"sl{tt % 4}"]], writes=[b["vp"]],
                   tag=f"vevacT_b{bi}p{pair}t{tt}")

        for qb in range(NQB):
            nkt = 4 * qb + 4
            cs_items = self._cs_items(bi, pair, qb)
            ncs, done = len(cs_items), 0
            for kt in range(nkt + 1):
                if kt == 2 and qb > 0:
                    self._qb_denom(bi, pair, qb - 1)
                for hl in range(2):
                    if kt < nkt:
                        sb_i = hl * 2 + kt % 2
                        ps, pb = self.p_sl[sb_i], b[f"sl{sb_i}"]
                        diag = kt - 4 * qb
                        def stgrp(e, ps=ps, hl=hl, kt=kt, qb=qb, diag=diag):
                            r = e.matmul(ps[:, :],
                                         self.s_kbd[hl][:, kt * 128:(kt + 1) * 128],
                                         self.s_qd[hl][:, qb * 512:(qb + 1) * 512],
                                         start=True, stop=(diag < 0))
                            if diag >= 0:
                                off = diag * (diag + 1) // 2 * 128
                                w = (diag + 1) * 128
                                r = e.matmul(ps[:, 0:w], self.s_identbf[:],
                                             self.s_mtq[:, off:off + w], start=False,
                                             stop=True)
                            return r
                        sch.op("pe", stgrp,
                               reads=[b[f"qd{hl}"], b[f"kbd{hl}"], b["consts"]],
                               writes=[pb], tag=f"st_b{bi}p{pair}q{qb}k{kt}h{hl}")
                    if kt > 0:
                        ktp = kt - 1
                        et = self.s_et[hl][ktp % 2]
                        eb = b[f"et{hl}{ktp % 2}"]
                        pa, pab = self.p_av[hl], b[f"av{hl}"]
                        vsl = self.s_vp[:, ktp, hl * 65:hl * 65 + 65]
                        sch.op("pe", lambda e, pa=pa, vsl=vsl, et=et, ktp=ktp, nkt=nkt:
                               e.matmul(pa[0:65, :], vsl, et[:],
                                        start=(ktp == 0), stop=(ktp == nkt - 1)),
                               reads=[b["vp"], eb], writes=[pab],
                               tag=f"av_b{bi}p{pair}q{qb}k{ktp}h{hl}")
                    if kt < nkt:
                        sb_i = hl * 2 + kt % 2
                        ps, pb = self.p_sl[sb_i], b[f"sl{sb_i}"]
                        et = self.s_et[hl][kt % 2]
                        eb = b[f"et{hl}{kt % 2}"]
                        sch.op("act", lambda e, ps=ps, et=et:
                               e.activation(out=et[:], in_=ps[:, :], func=AF.Exp,
                                            scale=0.125),
                               reads=[pb], writes=[eb],
                               tag=f"exp_b{bi}p{pair}q{qb}k{kt}h{hl}")
                tgt = (kt + 1) * ncs // (nkt + 1)
                while done < tgt:
                    cs_items[done]()
                    done += 1

            self._av_evac(bi, pair, qb)
        self._qb_denom(bi, pair, NQB - 1)
        self._pair_denoms(bi, pair)
        self.dbg(f"a2n_b{bi}p{pair}", self.s_a2n[pair][:], [128, S], BF16,
                 [self.b_a2n[pair]])
        self.dbg(f"sig_b{bi}p{pair}", self.s_sig[:], [33, S], BF16, [b["sig"]])
        self.dbg(f"mpair_b{bi}p{pair}", self.s_mpair[:], [128, 32], F32,
                 [b["mpair"]])
        self.dbg(f"qt_b{bi}p{pair}", self.s_qd[0][:], [128, S], BF16, [b["qd0"]])
        self.dbg(f"vp_b{bi}p{pair}", self.s_vp[:], [128, NKT, 130], BF16,
                 [b["vp"]])

    def _cs_items(self, bi, pair, qb):
        sch, b = self.sch, self.b
        items = []
        idx = [0]
        for i in range(4):
            qt = 4 * qb + i
            for qh in range(2):
                jj = qt % 4
                ext = (qt + 1) * 128
                nr = (ext + 511) // 512
                for rd in range(nr):
                    lo = rd * 512
                    w = min(ext - lo, 512)
                    dlast = (rd == nr - 1)
                    def item(qt=qt, qh=qh, rd=rd, lo=lo, w=w, dlast=dlast, jj=jj):
                        cb = idx[0] % 2
                        po = cb * 512
                        idx[0] += 1
                        def cpg(e, lo=lo, w=w, qh=qh, qt=qt, po=po,
                                dlast=dlast, jj=jj):
                            if not dlast:
                                return e.matmul(
                                    self.p_cp[:, po:po + w],
                                    self.s_qbd2[qh][:, qt * 128:(qt + 1) * 128],
                                    self.s_kt[:, lo:lo + w],
                                    start=True, stop=True)
                            off = jj * (jj + 1) // 2 * 128
                            e.matmul(self.p_cp[:, po:po + w],
                                     self.s_qbd2[qh][:, qt * 128:(qt + 1) * 128],
                                     self.s_kt[:, lo:lo + w],
                                     start=True, stop=False)
                            return e.matmul(self.p_cp[:, po:po + w],
                                            self.s_identbf[:],
                                            self.s_mqk2[qh][:, off:off + w],
                                            start=False, stop=True)
                        cbank = b[f"cpb{cb}"]
                        sch.op("pe", cpg,
                               reads=[b[f"qbd2{qh}"], b["kt"], b["consts"]],
                               writes=[cbank],
                               tag=f"cs_b{bi}p{pair}t{qt}h{qh}r{rd}")
                        mcol = self.s_mpair[:, qt * 2 + qh:qt * 2 + qh + 1]
                        if rd == 0:
                            sch.op("dve", lambda e, mcol=mcol, w=w, po=po:
                                   e.reduce_max(mcol, self.p_cp[:, po:po + w],
                                                axis=AX.X),
                                   reads=[cbank], writes=[b["mpair"]],
                                   tag=f"max_b{bi}p{pair}t{qt}h{qh}")
                        else:
                            nr1 = (qt // 4 + 1) - 1
                            pc = (0, 0, 8, 24)[qt // 4] \
                                + ((qt % 4) * 2 + qh) * nr1 + (rd - 1)
                            mp = self.s_mprt[:, pc:pc + 1]
                            sch.op("dve", lambda e, mp=mp, w=w, po=po:
                                   e.reduce_max(mp, self.p_cp[:, po:po + w],
                                                axis=AX.X),
                                   reads=[cbank], writes=[b["mprt"]],
                                   tag=f"maxp_b{bi}p{pair}t{qt}h{qh}r{rd}")
                    items.append(item)
        return items

    def _av_evac(self, bi, pair, qb):
        sch, b = self.sch, self.b
        for hl in range(2):
            pa, pab = self.p_av[hl], b[f"av{hl}"]
            srow = 32 * hl
            so = sch.op("act", lambda e, pa=pa, srow=srow, qb=qb:
                        e.copy(out=self.s_sig[srow:srow + 1, qb * 512:(qb + 1) * 512],
                               in_=pa[64:65, :]),
                        reads=[pab], writes=[b["sig"]],
                        tag=f"sigevac_b{bi}p{pair}q{qb}h{hl}")
            sch.op("dve", lambda e, pa=pa, hl=hl, qb=qb, pair=pair:
                   e.tensor_copy(self.s_a2n[pair][64 * hl:64 * hl + 64,
                                                  qb * 512:(qb + 1) * 512],
                                 pa[0:64, :]),
                   reads=[pab], writes=[self.b_a2n[pair]], deps=[so],
                   tag=f"aevac_b{bi}p{pair}q{qb}h{hl}")

    def _qb_denom(self, bi, pair, qb):
        sch, b = self.sch, self.b
        nr1 = qb
        pb0 = (0, 0, 8, 24)[qb]
        for k in range(1, nr1):
            sch.op("dve", lambda e, k=k, nr1=nr1, pb0=pb0: e.tensor_tensor(
                out=bass.AP(self.s_mprt, pb0 + k, [[48, 128], [nr1, 8]]),
                in0=bass.AP(self.s_mprt, pb0 + k - 1, [[48, 128], [nr1, 8]]),
                in1=bass.AP(self.s_mprt, pb0 + k, [[48, 128], [nr1, 8]]),
                op=ALU.max),
                reads=[b["mprt"]], writes=[b["mprt"]],
                tag=f"mcmb_b{bi}p{pair}q{qb}k{k}")
        if nr1 > 0:
            sch.op("dve", lambda e, nr1=nr1, qb=qb, pb0=pb0: e.tensor_tensor(
                out=self.s_mpair[:, 8 * qb:8 * qb + 8],
                in0=bass.AP(self.s_mprt, pb0 + nr1 - 1, [[48, 128], [nr1, 8]]),
                in1=self.s_mpair[:, 8 * qb:8 * qb + 8],
                op=ALU.max),
                reads=[b["mprt"]], writes=[b["mpair"]],
                tag=f"mcmbf_b{bi}p{pair}q{qb}")
        sch.op("act", lambda e, qb=qb: e.activation(
            out=self.s_emg[:, 8 * qb:8 * qb + 8],
            in_=self.s_mpair[:, 8 * qb:8 * qb + 8], func=AF.Exp, scale=0.125),
            reads=[b["mpair"]], writes=[b["emg"]], tag=f"em_b{bi}p{pair}q{qb}")
        sch.op("dve", lambda e, qb=qb: e.tensor_scalar(
            out=self.s_emgbf[:, 8 * qb:8 * qb + 8],
            in0=self.s_emg[:, 8 * qb:8 * qb + 8],
            scalar1=self.s_ghostb[:, pair:pair + 1], scalar2=None, op0=ALU.mult),
            reads=[b["emg"], b["consts"]], writes=[b[f"emgbf{qb}"]],
            tag=f"emg_b{bi}p{pair}q{qb}")

    def _pair_denoms(self, bi, pair):
        sch, b = self.sch, self.b
        for qb in range(NQB):
            self._one_denom(bi, pair, qb)

    def _denom_bank(self, qb, hl):
        idx = qb * 2 + hl
        if idx < 2:
            return self.p_cp[:, idx * 512:(idx + 1) * 512], self.b[f"cpb{idx}"]
        if idx < 6:
            return self.p_sl[idx - 2][:, :], self.b[f"sl{idx - 2}"]
        return self.p_av[idx - 6][:, :], self.b[f"av{idx - 6}"]

    def _one_denom(self, bi, pair, qb):
        sch, b = self.sch, self.b
        for hl in range(2):
            srow = 32 * hl
            r0 = 64 * hl
            ps, pb = self._denom_bank(qb, hl)
            def dmm(e, ps=ps, hl=hl, qb=qb, srow=srow, r0=r0):
                r = e.matmul(ps[r0:r0 + 64, 0:512],
                             self.s_ones33bf[srow:srow + 1, 0:64],
                             self.s_sig[srow:srow + 1, qb * 512:(qb + 1) * 512],
                             start=True, stop=False, tile_position=(srow, r0))
                for i in range(4):
                    qt = 4 * qb + i
                    for qh in range(2):
                        col = self.s_emgbf[:, qt * 2 + qh:qt * 2 + qh + 1]
                        r = e.matmul(
                            ps[r0:r0 + 64,
                               i * 128 + qh * 64:i * 128 + qh * 64 + 64],
                            col.broadcast_to([128, 64]),
                            self.s_identbf[:, 64 * hl:64 * hl + 64],
                            start=False, stop=(i == 3 and qh == 1),
                            tile_position=(0, r0))
                return r
            sch.op("pe", dmm, reads=[b[f"emgbf{qb}"], b["sig"], b["consts"]],
                   writes=[pb], tag=f"denom_b{bi}p{pair}h{hl}q{qb}")
        for hl in range(2):
            r0 = 64 * hl
            ps, pb = self._denom_bank(qb, hl)
            sch.op("dve", lambda e, ps=ps, r0=r0: e.reciprocal(
                self.s_r[r0:r0 + 64, :], ps[r0:r0 + 64, :]),
                reads=[pb], writes=[b["r"]], tag=f"recip_b{bi}p{pair}h{hl}q{qb}")
        for hl in range(2):
            r0 = 64 * hl
            sch.op("dve", lambda e, hl=hl, qb=qb, pair=pair, r0=r0: e.tensor_tensor(
                out=self.s_a2n[pair][r0:r0 + 64, qb * 512:(qb + 1) * 512],
                in0=self.s_a2n[pair][r0:r0 + 64, qb * 512:(qb + 1) * 512],
                in1=self.s_r[r0:r0 + 64, :], op=ALU.mult),
                reads=[b["r"], self.b_a2n[pair]], writes=[self.b_a2n[pair]],
                tag=f"norm_b{bi}p{pair}h{hl}q{qb}")

    def emit_wo_ln1(self, bi):
        sch, b = self.sch, self.b
        banks = self.p_av + self.p_sl[0:2]
        bnames = ("av0", "av1", "sl0", "sl1")

        def womm(qb, fc):
            ps, pb = banks[fc % 4], b[bnames[fc % 4]]
            np_ = self.npair
            def mm(e, ps=ps, fc=fc, qb=qb, np_=np_):
                r = None
                for p in range(np_):
                    r = e.matmul(ps[:, 0:512],
                                 self.s_wot[:, p, fc * 128:(fc + 1) * 128],
                                 self.s_a2n[p][:, qb * 512:(qb + 1) * 512],
                                 start=(p == 0), stop=(p == NPAIR - 1))
                return r
            sch.op("pe", mm, reads=[b["wot"]] + list(self.b_a2n[:self.npair]),
                   writes=[pb], tag=f"wo_b{bi}q{qb}f{fc}")

        def xres(qb, fc):
            ps, pb = banks[fc % 4], b[bnames[fc % 4]]
            sch.op("dve", lambda e, ps=ps, fc=fc, qb=qb: e.scalar_tensor_tensor(
                out=self.s_xres[:, fc, :], in0=ps[:, 0:512],
                scalar=self.s_bo[:, fc:fc + 1],
                in1=self.s_x0t[:, qb, fc, :],
                op0=ALU.add, op1=ALU.add),
                reads=[pb, b[f"x0t{qb}"], b["consts"]], writes=[b["xres"]],
                tag=f"xres_b{bi}q{qb}f{fc}")
            sch.op("act", lambda e, fc=fc: e.activation(
                out=self.s_xsq[:, fc % 2, :], in_=self.s_xres[:, fc, :],
                func=AF.Square),
                reads=[b["xres"]], writes=[b["xsq"]], tag=f"sq_b{bi}q{qb}f{fc}")

        def stats(qb, fc):
            o1 = (qb % 2) * 512
            b1 = b[f"cpb{qb % 2}"]
            sch.op("pe", lambda e, fc=fc, o1=o1: e.matmul(
                self.p_cp[0:1, o1:o1 + 512], self.s_ones128f[:],
                self.s_xres[:, fc, :],
                start=(fc == 0), stop=(fc == NFC - 1)),
                reads=[b["xres"], b["consts"]], writes=[b1],
                tag=f"sum_b{bi}q{qb}f{fc}")
            sch.op("pe", lambda e, fc=fc, o1=o1: e.matmul(
                self.p_cp[64:65, o1:o1 + 512], self.s_ones128f[:],
                self.s_xsq[:, fc % 2, :],
                start=(fc == 0), stop=(fc == NFC - 1)),
                reads=[b["xsq"], b["consts"]], writes=[b1],
                tag=f"sumsq_b{bi}q{qb}f{fc}")

        def rowchain(qb):
            o1 = (qb % 2) * 512
            b1 = b[f"cpb{qb % 2}"]
            sch.op("act", lambda e, o1=o1: e.activation(
                out=self.s_lrows[:], in_=self.p_cp[0:1, o1:o1 + 512],
                func=AF.Identity, scale=1.0 / D),
                reads=[b1], writes=[b["lrows"]], tag=f"mean_b{bi}q{qb}")
            sch.op("act", lambda e, o1=o1: e.activation(
                out=self.s_ltmp[:], in_=self.p_cp[64:65, o1:o1 + 512],
                func=AF.Identity, scale=1.0 / D),
                reads=[b1], writes=[b["ltmp"]], tag=f"msq_b{bi}q{qb}")
            sch.op("act", lambda e: e.activation(
                out=self.s_laux[:], in_=self.s_lrows[:], func=AF.Square),
                reads=[b["lrows"]], writes=[b["laux"]], tag=f"m2_b{bi}q{qb}")

        def rowchain2(qb):
            sch.op("dve", lambda e: e.scalar_tensor_tensor(
                out=self.s_lrows[:], in0=self.s_ltmp[:], scalar=EPS,
                in1=self.s_laux[:], op0=ALU.add, op1=ALU.subtract),
                reads=[b["ltmp"], b["laux"]], writes=[b["lrows"]],
                tag=f"var_b{bi}q{qb}")
            sch.op("act", lambda e: e.activation(
                out=self.s_ltmp[:], in_=self.s_lrows[:], func=AF.Sqrt),
                reads=[b["lrows"]], writes=[b["ltmp"]], tag=f"sqrt_b{bi}q{qb}")
            sch.op("dve", lambda e: e.reciprocal(self.s_laux[:], self.s_ltmp[:]),
                   reads=[b["ltmp"]], writes=[b["laux"]], tag=f"rstd_b{bi}q{qb}")

        def bmean(qb):
            sch.op("pe", lambda e: e.matmul(
                self.p_sl[2][:, :], self.s_ones1f[:],
                self.s_lrows[:], start=True, stop=True),
                reads=[b["lrows"], b["consts"]], writes=[b["sl2"]],
                tag=f"bmean_b{bi}q{qb}")

        def brstd(qb):
            sch.op("pe", lambda e: e.matmul(
                self.p_sl[3][:, :], self.s_ones1f[:],
                self.s_laux[:], start=True, stop=True),
                reads=[b["laux"], b["consts"]], writes=[b["sl3"]],
                tag=f"brstd_b{bi}q{qb}")

        subscr = [self.s_r, self.s_oh]
        subbuf = [b["r"], b["oh"]]

        def sub(qb, fc):
            sch.op("dve", lambda e, fc=fc: e.scalar_tensor_tensor(
                out=subscr[fc % 2][:], in0=self.p_sl[2][:, :],
                scalar=-1.0, in1=self.s_xres[:, fc, :],
                op0=ALU.mult, op1=ALU.add),
                reads=[b["xres"], b["sl2"]], writes=[subbuf[fc % 2]],
                tag=f"sub_b{bi}q{qb}f{fc}")

        def x1(qb, fc):
            sch.op("dve", lambda e, fc=fc: e.tensor_tensor(
                out=self.s_x1qb[:, fc, :], in0=self.p_sl[3][:, :],
                in1=subscr[fc % 2][:], op=ALU.mult),
                reads=[subbuf[fc % 2], b["sl3"]], writes=[b["x1qb"]],
                tag=f"x1_b{bi}q{qb}f{fc}")
            dst = bass.AP(self.d_x1sc, fc * 128 * (NB * S) + bi * S + qb * 512,
                          [[NB * S, 128], [1, 512]])
            if "stx1" not in self.skip:
                sch.op("sp", lambda e, fc=fc, dst=dst:
                       e.dma_start(out=dst, in_=self.s_x1qb[:, fc, :]),
                       reads=[b["x1qb"]], writes=[b["x1sc"]], is_dma=True,
                       tag=f"st_x1_b{bi}q{qb}f{fc}")

        def gmm(qb):
            o1 = (qb % 2) * 512
            if "gmm" in self.skip:
                return
            def g(e, qb=qb, o1=o1):
                r = None
                for tt in range(4):
                    for fc in range(NFC):
                        r = e.matmul(self.p_cp[:, o1 + tt * 8:o1 + tt * 8 + 8],
                                     self.s_x1qb[:, fc, tt * 128:(tt + 1) * 128],
                                     self.s_wgbf[:, fc, :],
                                     start=(fc == 0), stop=(fc == NFC - 1))
                return r
            sch.op("pe", g, reads=[b["x1qb"], b["consts"]],
                   writes=[b[f"cpb{qb % 2}"]], tag=f"gate_b{bi}q{qb}")
            sch.op("dve", lambda e, qb=qb, o1=o1: e.tensor_copy(
                self.s_lall[:, qb * 32:qb * 32 + 32], self.p_cp[:, o1:o1 + 32]),
                reads=[b[f"cpb{qb % 2}"]], writes=[b["lall"]],
                tag=f"levac_b{bi}q{qb}")

        def tail_head(qp):
            rowchain(qp)

        def tail_mid(qp, fc):
            sub(qp, fc)
            if fc > 0:
                x1(qp, fc - 1)

        def tail_end(qp):
            x1(qp, NFC - 1)
            gmm(qp)

        for qb in range(NQB):
            if qb > 0:
                tail_head(qb - 1)
            womm(qb, 0)
            if qb > 0:
                bmean(qb - 1)
                rowchain2(qb - 1)
            womm(qb, 1)
            if qb > 0:
                brstd(qb - 1)
            for fc in range(NFC):
                if qb > 0:
                    tail_mid(qb - 1, fc)
                xres(qb, fc)
                if fc + 2 < NFC:
                    womm(qb, fc + 2)
                if fc > 0:
                    stats(qb, fc - 1)
            if qb > 0:
                tail_end(qb - 1)
            stats(qb, NFC - 1)
            if bi + 1 < self.nb:
                self._x0_chunk_load(bi + 1, qb)
        tail_head(NQB - 1)
        bmean(NQB - 1)
        rowchain2(NQB - 1)
        brstd(NQB - 1)
        for fc in range(NFC):
            tail_mid(NQB - 1, fc)
        tail_end(NQB - 1)
        sch.op("sp", lambda e: e.nop(), reads=[b["x1qb"]], writes=[b["x1qb"]],
               flushq="q_sp", tag=f"x1flush_b{bi}")
        self.dbg(f"x1qb_b{bi}", self.s_x1qb[:], [128, NFC, 512], BF16, [b["x1qb"]])
        self.dbg(f"xres_b{bi}", self.s_xres[:], [128, NFC, 512], F32, [b["xres"]])
        self.dbg(f"lall_b{bi}", self.s_lall[:], [128, 512], F32, [b["lall"]])

    def emit_gate_batch(self, bi):
        sch, b = self.sch, self.b
        l3 = self.s_lall.ap().rearrange("p (t e) -> p t e", e=8)
        sch.op("dve", lambda e, l3=l3: e.reduce_max(self.s_xsq[:, 0, 0:64], l3, axis=AX.X),
               reads=[b["lall"]], writes=[b["xsq"]], tag=f"gmax_b{bi}")
        mx3 = self.s_xsq.ap()[:, 0, 0:64].rearrange("p (t o) -> p t o", o=1)
        sch.op("dve", lambda e, l3=l3, mx3=mx3: e.tensor_tensor(
            out=self.s_oh.ap().rearrange("p (t e) -> p t e", e=8), in0=l3,
            in1=mx3.broadcast_to([128, 64, 8]), op=ALU.is_equal),
            reads=[b["lall"], b["xsq"]], writes=[b["oh"]], tag=f"oneh_b{bi}")
        for g in range(4):
            def trg(e, g=g, src=self.s_oh):
                r = None
                for i in range(4):
                    tt = 4 * g + i
                    r = e.transpose(self.p_cp[0:8, i * 128:(i + 1) * 128],
                                    src[:, tt * 8:tt * 8 + 8], self.s_identf[:])
                return r
            sch.op("pe", trg, reads=[b["oh"], b["consts"]], writes=[b["cpb0"]],
                   tag=f"ohT_b{bi}g{g}")
            sch.op("dve", lambda e, g=g: e.tensor_copy(
                self.s_gateT[0:8, bi * S + g * 512:bi * S + (g + 1) * 512],
                self.p_cp[0:8, 0:512]),
                reads=[b["cpb0"]], writes=[b["gateT"]], tag=f"ohTe_b{bi}g{g}")
            def trl(e, g=g):
                r = None
                for i in range(4):
                    tt = 4 * g + i
                    r = e.transpose(self.p_cp[0:8, 512 + i * 128:640 + i * 128],
                                    self.s_lall[:, tt * 8:tt * 8 + 8], self.s_identf[:])
                return r
            sch.op("pe", trl, reads=[b["lall"], b["consts"]], writes=[b["cpb1"]],
                   tag=f"lT_b{bi}g{g}")
            sch.op("act", lambda e, g=g: e.activation(
                out=self.s_gateT[32:40, bi * S + g * 512:bi * S + (g + 1) * 512],
                in_=self.p_cp[0:8, 512:1024], func=AF.Exp),
                reads=[b["cpb1"]], writes=[b["gateT"]], tag=f"exT_b{bi}g{g}")

    def emit_routing(self):
        sch, b = self.sch, self.b
        T = self.nb * S
        sch.op("dve", lambda e: e.tensor_tensor_scan(
            out=self.s_scan[:, 0:T], data0=self.s_gateT[0:8, 0:T],
            data1=self.s_gateT[0:8, 0:T], initial=0.0, op0=ALU.add, op1=ALU.bypass),
            reads=[b["gateT"]], writes=[b["scan"]], tag="scan")
        sch.op("dve", lambda e: e.drain(), tag="scan_drain")
        if self.use_cc:
            sch.op("gp", lambda e: e.dma_start(out=self.d_ccin.ap(),
                                               in_=self.s_scan[0:8, T - 1:T]),
                   reads=[b["scan"]], writes=[b["ccin"]], is_dma=True, tag="ccin")
            cco = sch.op("gp", lambda e: e.collective_compute(
                "AllGather", ALU.bypass, replica_groups=self.cc_rg,
                ins=[self.d_ccin.ap().rearrange("a b -> b a").opt()],
                outs=[self.d_ccout.ap().opt()]),
                reads=[b["ccin"]], writes=[b["ccout"]], is_dma=True, tag="allgather")
            cco.delta = 1
            sch.op("gp", lambda e: e.dma_start(out=self.s_agc[:], in_=self.d_ccout.ap()),
                   reads=[b["ccout"]], writes=[b["agc"]], is_dma=True, tag="ld_agc")
            sch.op("pe", lambda e: e.matmul(self.p_sl[0][0:8, 0:1], self.s_agc[:],
                                            self.s_premask[:], start=True, stop=True),
                   reads=[b["agc"], b["consts"]], writes=[b["sl0"]], tag="cbmm")
            sch.op("dve", lambda e: e.tensor_copy(self.s_cb[:], self.p_sl[0][0:8, 0:1]),
                   reads=[b["sl0"]], writes=[b["cb"]], tag="cbevac")
        else:
            sch.op("dve", lambda e: e.memset(self.s_cb[:], 0.0),
                   writes=[b["cb"]], tag="cbzero")
        sch.op("dve", lambda e: e.tensor_scalar(
            out=self.s_sel[:, 0:T], in0=self.s_scan[:, 0:T], scalar1=self.s_cb[:],
            scalar2=float(CAP + 1), op0=ALU.add, op1=ALU.is_lt),
            reads=[b["scan"], b["cb"]], writes=[b["sel"]], tag="sel")
        sch.op("dve", lambda e: e.tensor_tensor(
            out=self.s_so[32:40, 0:T], in0=self.s_gateT[0:8, 0:T],
            in1=self.s_sel[:, 0:T],
            op=ALU.mult), reads=[b["gateT"], b["sel"]], writes=[b["so"]], tag="so")
        sch.op("dve", lambda e: e.tensor_tensor(
            out=self.s_gateT[0:8, 0:T], in0=self.s_so[32:40, 0:T],
            in1=self.s_gateT[32:40, 0:T],
            op=ALU.mult), reads=[b["so"], b["gateT"]], writes=[b["gateT"]], tag="soe")
        nhalf = T // 1024
        for tb in range(T // 512):
            row, col = (0, tb * 512) if tb < nhalf else (32, (tb - nhalf) * 512)
            ps = self.p_sl[tb % 2]
            pb = b[f"sl{tb % 2}"]
            sch.op("pe", lambda e, ps=ps, tb=tb: e.matmul(
                ps[0:1, 0:512], self.s_ones128bf[0:8, :],
                self.s_gateT[0:8, tb * 512:(tb + 1) * 512], start=True, stop=True),
                reads=[b["gateT"], b["consts"]], writes=[pb], tag=f"nummm{tb}")
            sch.op("dve", lambda e, ps=ps, row=row, col=col: e.tensor_copy(
                self.s_num[row:row + 1, col:col + 512], ps[0:1, 0:512]),
                reads=[pb], writes=[b["num"]], tag=f"nume{tb}")
            sch.op("pe", lambda e, ps=ps, tb=tb: e.matmul(
                ps[0:1, 0:512], self.s_ones128bf[32:40, :],
                self.s_gateT[32:40, tb * 512:(tb + 1) * 512], start=True, stop=True),
                reads=[b["gateT"], b["consts"]], writes=[pb], tag=f"zmm{tb}")
            sch.op("dve", lambda e, ps=ps, row=row, col=col: e.tensor_copy(
                self.s_zrow[row:row + 1, col:col + 512], ps[0:1, 0:512]),
                reads=[pb], writes=[b["zrow"]], tag=f"ze{tb}")
        half = T // 2
        for hf, row in ((0, 0), (1, 32)):
            sch.op("dve", lambda e, row=row, half=half: e.reciprocal(
                self.s_mgt[row:row + 1, 0:half], self.s_zrow[row:row + 1, 0:half]),
                reads=[b["zrow"]], writes=[b["mgt"], b["scan"]], tag=f"zrec{row}")
            sch.op("dve", lambda e, row=row, hf=hf, half=half: e.tensor_tensor(
                out=self.s_mgbf[0:1, hf * half:(hf + 1) * half],
                in0=self.s_num[row:row + 1, 0:half],
                in1=self.s_mgt[row:row + 1, 0:half], op=ALU.mult),
                reads=[b["num"], b["mgt"]], writes=[b["mgbf"]], tag=f"mg{row}")
        self.dbg("mg", self.s_mgbf[:], [1, T], BF16, [b["mgbf"]])
        self.dbg("scan", self.s_scan[:], [8, T], F32, [b["scan"]])

    def fence(self):
        sch = self.sch
        last = {}
        for o in sch.order:
            last[o.stream] = o
        ops = []
        for st in ("pe", "dve", "act", "gp", "sp"):
            deps = [v for k, v in last.items() if k != st]
            fn = (lambda e: e.engine_nop()) if st in ("dve", "gp") else None
            ops.append(sch.op(st, fn, deps=deps, tag=f"fence_{st}",
                              flushq="q_sp" if st == "sp" else None))

    def emit_phase_b(self):
        sch, b = self.sch, self.b
        self.fence()
        wsrc = self.d["wexp"].ap().rearrange("(c p) f -> p c f", p=128)
        sch.op("gp", lambda e, i=wsrc: e.dma_start(out=self.s_wexp[:], in_=i),
               writes=[b["wexp"]], is_dma=True, tag="ld_wexp")
        T = self.nb * S
        banks = [self.p_av[0], self.p_av[1], self.p_sl[0]]
        bnames = ("av0", "av1", "sl0")

        def loads(tb):
            x1, bx1 = self.s_x1tb[tb % 2], b[f"x1tb{tb % 2}"]
            for fc in range(NFC):
                src = bass.AP(self.d_x1sc, fc * 128 * T + tb * 512,
                              [[T, 128], [1, 512]])
                sch.op("sp", lambda e, src=src, x1=x1, fc=fc:
                       e.dma_start(out=x1[:, fc, :], in_=src),
                       reads=[b["x1sc"]], writes=[bx1], is_dma=True,
                       tag=f"ld_x1_{tb}f{fc}")

        def mgb(tb):
            col = tb * 512
            sch.op("pe", lambda e, col=col: e.matmul(
                self.p_sl[3][:, :], self.s_ones1bf[:],
                self.s_mgbf[0:1, col:col + 512], start=True, stop=True),
                reads=[b["mgbf"], b["consts"]], writes=[b["sl3"]], tag=f"mgb{tb}")
            x1, bx1 = self.s_x1tb[tb % 2], b[f"x1tb{tb % 2}"]
            x13 = x1.ap().rearrange("p c t -> p c t")
            mg3 = self.p_sl[3].ap()[:, :].rearrange("p (o t) -> p o t", o=1)
            sch.op("dve", lambda e, x13=x13, mg3=mg3, tb=tb: e.tensor_tensor(
                out=self.s_x1mg.ap()[:, tb % 2], in0=mg3.broadcast_to([128, NFC, 512]),
                in1=x13, op=ALU.mult),
                reads=[bx1, b["sl3"]], writes=[b["x1mg"]], tag=f"x1mg{tb}")

        def emm(tb, fo):
            ps, pb = banks[fo % 3], b[bnames[fo % 3]]
            col = tb * 512
            def mm(e, ps=ps, fo=fo, col=col, tb=tb):
                for c in range(NFC):
                    e.matmul(ps[:, 0:512],
                             self.s_wexp[:, c, fo * 128:(fo + 1) * 128],
                             self.s_x1mg[:, tb % 2, c, :], start=(c == 0),
                             stop=False)
                return e.matmul(ps[:, 0:512],
                                self.s_berowbf[:, fo * 128:(fo + 1) * 128],
                                self.s_mgbf[0:1, col:col + 512],
                                start=False, stop=True)
            sch.op("pe", mm, reads=[b["wexp"], b["x1mg"], b["mgbf"], b["consts"]],
                   writes=[pb], tag=f"exp_mm{tb}f{fo}")

        def xres2(tb, fo):
            ps, pb = banks[fo % 3], b[bnames[fo % 3]]
            x1, bx1 = self.s_x1tb[tb % 2], b[f"x1tb{tb % 2}"]
            sch.op("dve", lambda e, ps=ps, fo=fo, x1=x1: e.tensor_tensor(
                out=self.s_xres2[:, fo, :], in0=ps[:, 0:512], in1=x1[:, fo, :],
                op=ALU.add),
                reads=[pb, bx1], writes=[b["xres2"]], tag=f"xres2_{tb}f{fo}")
            sch.op("act", lambda e, fo=fo: e.activation(
                out=self.s_xsqb[:, fo % 2, :], in_=self.s_xres2[:, fo, :],
                func=AF.Square),
                reads=[b["xres2"]], writes=[b["xsqb"]], tag=f"sq2_{tb}f{fo}")

        def stats2(tb, fo):
            o1 = (tb % 2) * 512
            b1 = b[f"cpb{tb % 2}"]
            sch.op("pe", lambda e, fo=fo, o1=o1: e.matmul(
                self.p_cp[0:1, o1:o1 + 512], self.s_ones128f[:],
                self.s_xres2[:, fo, :],
                start=(fo == 0), stop=(fo == NFC - 1)),
                reads=[b["xres2"], b["consts"]], writes=[b1], tag=f"sum2_{tb}f{fo}")
            sch.op("pe", lambda e, fo=fo, o1=o1: e.matmul(
                self.p_cp[64:65, o1:o1 + 512], self.s_ones128f[:],
                self.s_xsqb[:, fo % 2, :],
                start=(fo == 0), stop=(fo == NFC - 1)),
                reads=[b["xsqb"], b["consts"]], writes=[b1], tag=f"sumsq2_{tb}f{fo}")

        def rowchainA(tb):
            o1 = (tb % 2) * 512
            b1 = b[f"cpb{tb % 2}"]
            sch.op("act", lambda e, o1=o1: e.activation(
                out=self.s_lrows[:], in_=self.p_cp[0:1, o1:o1 + 512],
                func=AF.Identity, scale=1.0 / D),
                reads=[b1], writes=[b["lrows"]], tag=f"mean2_{tb}")
            sch.op("act", lambda e, o1=o1: e.activation(
                out=self.s_ltmp[:], in_=self.p_cp[64:65, o1:o1 + 512],
                func=AF.Identity, scale=1.0 / D),
                reads=[b1], writes=[b["ltmp"]], tag=f"msq2_{tb}")
            sch.op("act", lambda e: e.activation(
                out=self.s_laux[:], in_=self.s_lrows[:], func=AF.Square),
                reads=[b["lrows"]], writes=[b["laux"]], tag=f"m22_{tb}")

        def rowchain2(tb):
            sch.op("dve", lambda e: e.scalar_tensor_tensor(
                out=self.s_lrows[:], in0=self.s_ltmp[:], scalar=EPS,
                in1=self.s_laux[:], op0=ALU.add, op1=ALU.subtract),
                reads=[b["ltmp"], b["laux"]], writes=[b["lrows"]], tag=f"var2_{tb}")
            sch.op("act", lambda e: e.activation(
                out=self.s_ltmp[:], in_=self.s_lrows[:], func=AF.Sqrt),
                reads=[b["lrows"]], writes=[b["ltmp"]], tag=f"sqrt2_{tb}")
            sch.op("dve", lambda e: e.reciprocal(self.s_laux[:], self.s_ltmp[:]),
                   reads=[b["ltmp"]], writes=[b["laux"]], tag=f"rstd2_{tb}")

        def bmean2(tb):
            sch.op("pe", lambda e: e.matmul(
                self.p_sl[2][:, :], self.s_ones1f[:],
                self.s_lrows[:], start=True, stop=True),
                reads=[b["lrows"], b["consts"]], writes=[b["sl2"]],
                tag=f"bmean2_{tb}")

        def brstd2(tb):
            sch.op("pe", lambda e: e.matmul(
                self.p_sl[1][:, :], self.s_ones1f[:],
                self.s_laux[:], start=True, stop=True),
                reads=[b["laux"], b["consts"]], writes=[b["sl1"]],
                tag=f"brstd2_{tb}")

        def sub2(tb, fo):
            sch.op("dve", lambda e, fo=fo: e.scalar_tensor_tensor(
                out=self.s_r[:] if fo % 2 == 0 else self.s_oh[:],
                in0=self.p_sl[2][:, :],
                scalar=-1.0, in1=self.s_xres2[:, fo, :],
                op0=ALU.mult, op1=ALU.add),
                reads=[b["xres2"], b["sl2"]],
                writes=[b["r"] if fo % 2 == 0 else b["oh"]],
                tag=f"sub2_{tb}f{fo}")

        def x2(tb, fo):
            bi, qb = tb // 4, tb % 4
            sch.op("dve", lambda e, fo=fo: e.tensor_tensor(
                out=self.s_x2o[:, fo, :], in0=self.p_sl[1][:, :],
                in1=self.s_r[:] if fo % 2 == 0 else self.s_oh[:], op=ALU.mult),
                reads=[b["r"] if fo % 2 == 0 else b["oh"], b["sl1"]],
                writes=[b["x2o"]], tag=f"x2_{tb}f{fo}")
            dst = bass.AP(self.d_out, bi * D * S + fo * 128 * S + qb * 512,
                          [[S, 128], [1, 512]])
            sch.op("act", lambda e, dst=dst, fo=fo:
                   e.dma_start(out=dst, in_=self.s_x2o[:, fo, :]),
                   reads=[b["x2o"]], is_dma=True, tag=f"st_x2_{tb}f{fo}")

        NTB = T // 512
        for tb in range(NTB):
            loads(tb)
            mgb(tb)
            if tb > 0:
                rowchainA(tb - 1)
            emm(tb, 0)
            if tb > 0:
                bmean2(tb - 1)
                rowchain2(tb - 1)
            emm(tb, 1)
            if tb > 0:
                brstd2(tb - 1)
                sch.op("act", lambda e: e.nop(), reads=[b["x2o"]],
                       writes=[b["x2o"]], flushq="q_act", tag=f"x2flush_{tb}")
            for fo in range(NFC):
                if tb > 0:
                    sub2(tb - 1, fo)
                    if fo > 0:
                        x2(tb - 1, fo - 1)
                xres2(tb, fo)
                if fo + 2 < NFC:
                    emm(tb, fo + 2)
                if fo > 0:
                    stats2(tb, fo - 1)
            if tb > 0:
                x2(tb - 1, NFC - 1)
            stats2(tb, NFC - 1)
        tb = NTB - 1
        rowchainA(tb)
        bmean2(tb)
        rowchain2(tb)
        brstd2(tb)
        sch.op("act", lambda e: e.nop(), reads=[b["x2o"]],
               writes=[b["x2o"]], flushq="q_act", tag="x2flush_last")
        for fo in range(NFC):
            sub2(tb, fo)
            if fo > 0:
                x2(tb, fo - 1)
        x2(tb, NFC - 1)

    def realize(self):
        sch = self.sch
        counts = sch.finalize()
        nc = self.nc
        streams = {}
        for o in sch.order:
            streams.setdefault(o.stream, []).append(o)

        with contextlib.ExitStack() as es:
            sems = {}
            for name in ["s_pe", "s_dve", "s_act", "s_gp", "s_sp", "q_sp", "q_gp",
                         "q_act", "s_cc"]:
                sems[name] = es.enter_context(nc.semaphore(name))
            block = es.enter_context(nc.Block())

            def replay(eng, ops, pad_raw=False, chain_dma=None):
                watermark = {}
                pos = {}
                slot = 0
                for o in ops:
                    if o.flushq is not None and o.flushval:
                        if watermark.get(o.flushq, 0) < o.flushval:
                            eng.wait_ge(sems[o.flushq], o.flushval)
                            watermark[o.flushq] = o.flushval
                    if chain_dma and o.is_dma and o.sem == chain_dma \
                            and o.incval is not None and o.incval > 16:
                        v = o.incval - 16
                        if watermark.get(o.sem, 0) < v:
                            eng.wait_ge(sems[o.sem], v)
                            watermark[o.sem] = v
                    waited_pe = False
                    for d in o.deps:
                        if d.stream == o.stream and not d.is_dma and not o.is_dma:
                            continue
                        if d.incval is None:
                            continue
                        if watermark.get(d.sem, 0) >= d.incval:
                            continue
                        eng.wait_ge(sems[d.sem], d.incval)
                        watermark[d.sem] = d.incval
                        if d.sem == "s_pe":
                            waited_pe = True
                    if pad_raw and waited_pe and o.fn is not None:
                        for _ in range(6):
                            eng.engine_nop()
                        slot += 6
                    if pad_raw and o.fn is not None:
                        need = 0
                        for d in o.deps:
                            if d.stream == o.stream and not d.is_dma and id(d) in pos:
                                need = max(need, pos[id(d)] + 3 - slot)
                        for _ in range(need):
                            eng.engine_nop()
                            slot += 1
                    if o.fn is None:
                        continue
                    inst = o.fn(eng)
                    pos[id(o)] = slot
                    slot += 1
                    if o.incval is not None:
                        assert inst is not None, o.tag
                        inst.then_inc(sems[o.sem], o.delta)

            @block.tensor
            def _(eng):
                replay(eng, streams.get("pe", []))

            @block.vector
            def _(eng):
                replay(eng, streams.get("dve", []), pad_raw=True)

            @block.scalar
            def _(eng):
                replay(eng, streams.get("act", []))

            @block.gpsimd
            def _(eng):
                replay(eng, streams.get("gp", []), chain_dma="q_gp")

            @block.sync
            def _(eng):
                replay(eng, streams.get("sp", []))
        return nc


def build(nb=NB, npair=NPAIR, use_cc=True, debug=(), skip=()):
    bld = Builder(nb=nb, npair=npair, use_cc=use_cc, debug=debug, skip=skip)
    bld.emit_all()
    nc = bld.realize()
    return nc, bld


def kernel(**inputs):
    in_maps = stage(inputs)
    nc, _ = build()
    res = run_bass_kernel_spmd(nc, in_maps, core_ids=list(range(NCORES)))
    out = np.empty((32, 2048, 1024), dtype=np.float32)
    for c in range(NCORES):
        out[c * NB:(c + 1) * NB] = res.results[c]["x2t"].transpose(0, 2, 1)
    return out


# version 29 (final)
# speedup vs baseline: 1.0469x; 1.0145x over previous
import contextlib
import numpy as np
import ml_dtypes
import concourse.bass as bass
import concourse.mybir as mybir
from concourse.bass_utils import run_bass_kernel_spmd
from concourse.replica_groups import maybe_share_collective_output_space

F32 = mybir.dt.float32
BF16 = mybir.dt.bfloat16
AF = mybir.ActivationFunctionType
ALU = mybir.AluOpType
AX = mybir.AxisListType
BF = ml_dtypes.bfloat16

NCORES = 8
NB = 4
S = 2048
D = 1024
NH = 16
HD = 64
NPAIR = 8
NQB = 4
NKT = 16
NFC = 8
CAP = 100
EPS = 1e-5
NEG = -1.0e30


class _Op:
    __slots__ = ("stream", "fn", "deps", "is_dma", "delta", "sem", "incval",
                 "needed", "idx", "tag", "flushq", "flushval")

    def __init__(self, stream, fn, is_dma, tag):
        self.flushq = None
        self.flushval = None
        self.stream = stream
        self.fn = fn
        self.deps = []
        self.is_dma = is_dma
        self.delta = 16 if is_dma else 1
        self.sem = None
        self.incval = None
        self.needed = False
        self.idx = None
        self.tag = tag


class Buf:
    __slots__ = ("name", "last_w", "readers", "rng", "overlaps")

    def __init__(self, name, rng=None):
        self.name = name
        self.last_w = None
        self.readers = []
        self.rng = rng
        self.overlaps = []


class Sched:
    def __init__(self):
        self.order = []
        self.bufs = []

    def link_overlaps(self):
        for i, a in enumerate(self.bufs):
            if a.rng is None:
                continue
            for c in self.bufs[i + 1:]:
                if c.rng is None:
                    continue
                if a.rng[0] < c.rng[1] and c.rng[0] < a.rng[1]:
                    a.overlaps.append(c)
                    c.overlaps.append(a)

    def op(self, stream, fn, reads=(), writes=(), deps=(), is_dma=False, tag="",
           flushq=None):
        o = _Op(stream, fn, is_dma, tag)
        o.flushq = flushq
        o.idx = len(self.order)
        dd = list(deps)
        for b in reads:
            if b.last_w is not None:
                dd.append(b.last_w)
        for b in writes:
            for bb in [b] + b.overlaps:
                if bb.last_w is not None:
                    dd.append(bb.last_w)
                dd.extend(bb.readers)
        seen = set()
        for d in dd:
            if d is None or id(d) in seen or d is o:
                continue
            seen.add(id(d))
            assert d.idx < o.idx
            o.deps.append(d)
        for b in reads:
            b.readers.append(o)
        for b in writes:
            b.last_w = o
            b.readers = []
            for bb in b.overlaps:
                bb.last_w = None
                bb.readers = []
        self.order.append(o)
        return o

    def finalize(self):
        for o in self.order:
            if o.tag == "allgather":
                o.sem = "s_cc"
            elif o.is_dma:
                o.sem = {"sp": "q_sp", "act": "q_act"}.get(o.stream, "q_gp")
            else:
                o.sem = "s_" + o.stream
        for o in self.order:
            for d in o.deps:
                if d.stream == o.stream and not d.is_dma and not o.is_dma:
                    continue
                d.needed = True
        counts = {}
        for o in self.order:
            if o.flushq is not None:
                o.flushval = counts.get(o.flushq, 0)
            if o.needed or o.is_dma:
                counts[o.sem] = counts.get(o.sem, 0) + o.delta
                o.incval = counts[o.sem]
        return counts



def _consts():
    identf = np.eye(128, dtype=np.float32)
    identbf = np.eye(128, dtype=BF)
    ones1f = np.ones((1, 128), dtype=np.float32)
    ones1bf = np.ones((1, 128), dtype=BF)
    ones128f = np.ones((128, 1), dtype=np.float32)
    ones128bf = np.ones((128, 1), dtype=BF)
    k = np.arange(128)[:, None]
    mtq = np.zeros((128, 1280), dtype=np.float32)
    mqk = np.zeros((128, 1280), dtype=np.float32)
    for j in range(4):
        off, w = j * (j + 1) // 2 * 128, (j + 1) * 128
        q5 = np.arange(w)[None, :]
        mtq[:, off:off + w] = np.where(q5 >= j * 128 + k, 0.0, NEG)
        mqk[:, off:off + w] = np.where(q5 <= j * 128 + k, 0.0, NEG)
    mqk2a = np.concatenate([mqk[0:64], mqk[0:64]], axis=0)
    mqk2b = np.concatenate([mqk[64:128], mqk[64:128]], axis=0)
    return {
        "identf": identf, "identbf": identbf,
        "ones1f": ones1f, "ones1bf": ones1bf,
        "ones128f": ones128f, "ones128bf": ones128bf,
        "maskstq": mtq.astype(BF),
        "masksqk2a": mqk2a.astype(BF), "masksqk2b": mqk2b.astype(BF),
    }


def stage(inputs):
    x0 = np.asarray(inputs["L0_attn_inputs"], dtype=np.float32)
    wqkv = np.asarray(inputs["L0_attn_Wqkv_w"], dtype=np.float32)
    bqkv = np.asarray(inputs["L0_attn_Wqkv_b"], dtype=np.float32)
    wo = np.asarray(inputs["L0_attn_Wo_w"], dtype=np.float32)
    bo = np.asarray(inputs["L0_attn_Wo_b"], dtype=np.float32)
    ghost = np.asarray(inputs["L0_attn_ghost"], dtype=np.float32)
    wg = np.asarray(inputs["L0_ffn_w_gate"], dtype=np.float32)
    we = np.asarray(inputs["L0_ffn_w_expert"], dtype=np.float32)
    be = np.asarray(inputs["L0_ffn_b_expert"], dtype=np.float32)

    shared = dict(_consts())
    shared["wqkvt"] = np.ascontiguousarray(wqkv.T)
    shared["wqkvb_pc"] = np.ascontiguousarray(bqkv.reshape(24, 128).T)
    shared["bv_row"] = np.ascontiguousarray(bqkv[2048:].reshape(1, 1024))
    shared["wot"] = np.ascontiguousarray(wo.T)
    shared["wob_pc"] = np.ascontiguousarray(bo.reshape(8, 128).T)
    shared["wgate3"] = np.ascontiguousarray(wg.reshape(8, 128, 8).transpose(1, 0, 2))
    shared["wexp"] = np.ascontiguousarray(we)
    shared["bexp_row"] = np.ascontiguousarray(be.reshape(1, 1024))
    gh2 = np.empty((128, 8), dtype=np.float32)
    gh2[0:64, :] = ghost[0::2][None, :]
    gh2[64:128, :] = ghost[1::2][None, :]
    shared["ghostb"] = gh2

    in_maps = []
    for c in range(NCORES):
        m = dict(shared)
        xb = x0[c * NB:(c + 1) * NB]
        m["x0t"] = np.ascontiguousarray(xb.transpose(0, 2, 1))
        m["premask"] = (np.arange(8) < c).astype(np.float32).reshape(8, 1)
        in_maps.append(m)
    return in_maps



class Builder:
    def __init__(self, nb=NB, npair=NPAIR, use_cc=True, debug=(), skip=()):
        self.skip = set(skip)
        self.nb = nb
        self.npair = npair
        self.use_cc = use_cc
        self.debug = set(debug)
        self.dbg_outs = {}
        self.nc = bass.Bass("TRN2", target_bir_lowering=False)
        self.sch = Sched()
        self.sb_ranges = {}
        self._sb_off = 16512
        self._mk_tensors()

    def _sb(self, name, shape, dtype, off=None):
        nc = self.nc
        per = int(np.prod(shape[1:])) * mybir.dt.size(dtype)
        if off is None:
            off = self._sb_off
            self._sb_off = (off + per + 31) & ~31
        h = nc.alloc_sbuf_tensor_at(name, shape, dtype, offset=off)
        self.sb_ranges[name] = (off, off + per)
        return h

    def _mk_tensors(self):
        nc = self.nc
        self.d = {}
        for name, shp, dt in [
            ("x0t", [NB, D, S], F32), ("wqkvt", [D, 3 * D], F32),
            ("wqkvb_pc", [128, 24], F32), ("bv_row", [1, D], F32),
            ("wot", [D, D], F32), ("wob_pc", [128, 8], F32),
            ("wgate3", [128, 8, 8], F32), ("wexp", [D, D], F32),
            ("bexp_row", [1, D], F32), ("ghostb", [128, 8], F32),
            ("identf", [128, 128], F32), ("identbf", [128, 128], BF16),
            ("ones1f", [1, 128], F32), ("ones1bf", [1, 128], BF16),
            ("ones128f", [128, 1], F32), ("ones128bf", [128, 1], BF16),
            ("maskstq", [128, 1280], BF16), ("masksqk2a", [128, 1280], BF16),
            ("masksqk2b", [128, 1280], BF16),
            ("premask", [8, 1], F32),
        ]:
            self.d[name] = nc.dram_tensor(name, shp, dt, kind="ExternalInput")
        self.d_out = nc.dram_tensor("x2t", [NB, D, S], F32, kind="ExternalOutput")
        self.d_x1sc = nc.dram_tensor("x1sc", [D, NB * S], BF16)
        if self.use_cc:
            rg = [list(range(NCORES))]
            space = maybe_share_collective_output_space("AllGather", rg)
            self.cc_rg = rg
            self.d_ccin = nc.dram_tensor("ccin", [8, 1], F32)
            self.d_ccout = nc.dram_tensor("ccout", [8, 8], F32, addr_space=space)

        self.s_identf = self._sb("identf", [128, 128], F32)
        self.s_identbf = self._sb("identbf", [128, 128], BF16)
        self.s_ones1f = self._sb("ones1f", [1, 128], F32)
        self.s_ones1bf = self._sb("ones1bf", [1, 128], BF16)
        self.s_ones128f = self._sb("ones128f", [128, 1], F32)
        self.s_ones128bf = self._sb("ones128bf", [128, 1], BF16)
        self.s_mtq = self._sb("maskstq", [128, 1280], BF16)
        self.s_mqk2 = [self._sb("masksqk2a", [128, 1280], BF16),
                       self._sb("masksqk2b", [128, 1280], BF16)]
        self.s_ghostb = self._sb("ghostb", [128, 8], F32)
        self.s_bqkv = self._sb("bqkv", [128, 24], F32)
        self.s_bo = self._sb("bo", [128, 8], F32)
        self.s_wg = self._sb("wg", [128, 8, 8], F32)
        self.s_wgbf = self._sb("wgbf", [128, 8, 8], BF16)
        self.s_bvrowbf = self._sb("bvrowbf", [1, D], BF16)
        self.s_berowbf = self._sb("berowbf", [1, D], BF16)
        self.s_premask = self._sb("premask", [8, 1], F32)
        self.s_agc = self._sb("agc", [8, 8], F32)
        self.s_cb = self._sb("cb", [8, 1], F32)
        self.s_ones33 = self._sb("ones33", [33, 128], F32)
        self.s_ones33bf = self._sb("ones33bf", [33, 128], BF16)
        self.s_lrows = self._sb("lrows", [1, 512], F32)
        self.s_ltmp = self._sb("ltmp", [1, 512], F32)
        self.s_laux = self._sb("laux", [1, 512], F32)
        self.s_gateT = self._sb("gateT", [40, NB * S], BF16)
        base = self._sb_off
        o = base
        def at(name, shape, dtype):
            nonlocal o
            h = self._sb(name, shape, dtype, off=o)
            o += (int(np.prod(shape[1:])) * mybir.dt.size(dtype) + 31) & ~31
            return h
        self.s_x0t = at("x0t", [128, NQB, NFC, 512], BF16)
        self.s_rowtmp = self._sb("rowtmp", [1, D], F32, off=base)
        self.s_wqp = [at(f"wqp{i}", [128, NFC, 3, 128], BF16) for i in range(2)]
        self.s_qd = [at(f"qd{h}", [128, S], BF16) for h in range(2)]
        self.s_kbd = [at(f"kbd{h}", [128, S], BF16) for h in range(2)]
        self.s_qbd2 = [at(f"qbd2{h}", [128, S], BF16) for h in range(2)]
        self.s_kt = at("kt", [128, S], BF16)
        self.s_vp = at("vp", [128, NKT, 130], BF16)
        self.s_vt = at("vt", [128, S], BF16)
        self.s_et = [[at(f"et{h}{p}", [128, 512], BF16) for p in range(2)] for h in range(2)]
        self.s_a2n = [at(f"a2n{p}", [128, S], BF16) for p in range(NPAIR)]
        self.s_wot = at("wot", [128, NPAIR, D], BF16)
        self.s_xres = at("xres", [128, NFC, 512], F32)
        self.s_xsq = at("xsq", [128, 2, 512], F32)
        self.s_x1qb = at("x1qb", [128, NFC, 512], BF16)
        self.s_lall = at("lall", [128, 512], F32)
        self.s_oh = at("oh", [128, 512], F32)
        self.s_sig = at("sig", [33, S], BF16)
        self.s_mpair = at("mpair", [128, 32], F32)
        self.s_mprt = at("mprt", [128, 48], F32)
        self.s_emg = at("emg", [128, 64], F32)
        self.s_emgbf = at("emgbf", [128, 32], BF16)
        self.s_r = at("r", [128, 512], F32)
        class1_end = o
        o = base
        self.s_scan = at("scan", [8, NB * S], F32)
        self.s_sel = at("sel", [8, NB * S], BF16)
        self.s_so = at("so", [40, NB * S], BF16)
        self.s_num = at("num", [33, NB * S // 2], F32)
        self.s_zrow = at("zrow", [33, NB * S // 2], F32)
        post_end = o
        self.s_mgbf = at("mgbf", [1, NB * S], BF16)
        cls4_o = o
        self.s_mgt = self._sb("mgt", [33, NB * S // 2], F32,
                              off=base)
        o = cls4_o
        cls4_end = o
        o = base
        self.s_wexp = at("wexp", [128, NFC, D], BF16)
        self.s_x1tb = [at(f"x1tb{i}", [128, NFC, 512], BF16) for i in range(2)]
        self.s_x1mg = at("x1mg", [128, 2, NFC, 512], BF16)
        self.s_xres2 = at("xres2", [128, NFC, 512], F32)
        self.s_x2o = at("x2o", [128, NFC, 512], F32)
        self.s_xsqb = at("xsqb", [128, 2, 512], F32)
        cls5_end = o
        assert cls5_end <= post_end, (cls5_end, post_end)
        self.sbuf_peak = max(class1_end, cls4_end, cls5_end)
        self.sbuf_regions = dict(base=base, class1=class1_end, class3=post_end,
                                 cls4=cls4_end, cls5=cls5_end)
        assert self.sbuf_peak <= 229376, (self.sbuf_peak, self.sbuf_regions)

        self.p_cp = nc.alloc_psum_tensor("cp", [128, 1024], F32)
        self.p_sl = [nc.alloc_psum_tensor(f"sl{i}", [128, 512], F32) for i in range(4)]
        self.p_av = [nc.alloc_psum_tensor(f"av{h}", [128, 512], F32) for h in range(2)]

        rng = self.sb_ranges
        x0lo = rng["x0t"][0]
        x0chunk = NFC * 512 * 2
        name2rng = {
            "wqp0": rng["wqp0"], "wqp1": rng["wqp1"],
            "qd0": rng["qd0"], "qd1": rng["qd1"],
            "kbd0": rng["kbd0"], "kbd1": rng["kbd1"],
            "qbd20": rng["qbd20"], "qbd21": rng["qbd21"],
            "kt": rng["kt"], "vp": rng["vp"], "vt": rng["vt"],
            "et00": rng["et00"], "et01": rng["et01"],
            "et10": rng["et10"], "et11": rng["et11"],
            "wot": rng["wot"], "xres": rng["xres"],
            "xsq": rng["xsq"], "x1qb": rng["x1qb"], "lall": rng["lall"],
            "oh": rng["oh"], "sig": rng["sig"], "mpair": rng["mpair"], "mprt": rng["mprt"],
            "emg": rng["emg"], "emgbf": rng["emgbf"], "r": rng["r"],
            "gateT": rng["gateT"], "scan": rng["scan"],
            "sel": rng["sel"], "so": rng["so"], "num": rng["num"],
            "zrow": rng["zrow"], "mgbf": rng["mgbf"], "mgt": rng["mgt"],
            "wexp": rng["wexp"], "x1tb0": rng["x1tb0"], "x1tb1": rng["x1tb1"],
            "x1mg": rng["x1mg"], "xres2": rng["xres2"], "x2o": rng["x2o"],
            "xsqb": rng["xsqb"], "lrows": rng["lrows"], "ltmp": rng["ltmp"],
            "laux": rng["laux"], "rowtmp": rng["rowtmp"],
            "consts": (16512, rng["gateT"][0]),
        }
        for qb in range(NQB):
            name2rng[f"x0t{qb}"] = (x0lo + qb * x0chunk, x0lo + (qb + 1) * x0chunk)
        eglo = rng["emgbf"][0]
        for q in range(NQB):
            name2rng[f"emgbf{q}"] = (eglo + q * 16, eglo + (q + 1) * 16)
        sglo = rng["sig"][0]
        for q in range(NQB):
            name2rng[f"sig{q}"] = (sglo + q * 1024, sglo + (q + 1) * 1024)
        self.b = {}
        for n in ["cpb0", "cpb1", "sl0", "sl1", "sl2", "sl3",
                  "av0", "av1", "x1sc", "ccin", "ccout", "agc", "cb"]:
            self.b[n] = Buf(n)
            self.sch.bufs.append(self.b[n])
        for n, r in name2rng.items():
            self.b[n] = Buf(n, r)
            self.sch.bufs.append(self.b[n])
        self.b_a2n = []
        for p in range(NPAIR):
            bb = Buf(f"a2n{p}", rng[f"a2n{p}"])
            self.b_a2n.append(bb)
            self.sch.bufs.append(bb)
        self.sch.link_overlaps()

    def dbg(self, name, src_ap, shape, dtype, reads):
        if name not in self.debug:
            return
        t = self.nc.dram_tensor("dbg_" + name, shape, dtype, kind="ExternalOutput")
        self.dbg_outs[name] = name
        self.sch.op("sp", lambda e, t=t, a=src_ap: e.dma_start(out=t.ap(), in_=a),
                    reads=reads, is_dma=True, tag="dbg_" + name)

    def emit_all(self):
        sch, b = self.sch, self.b
        nc = self.nc

        def ld(sb, dname, cast=False, tag=""):
            src = self.d[dname].ap()
            stream = "gp" if cast else "sp"
            return sch.op(stream, lambda e, o=sb, i=src: e.dma_start(out=o[:], in_=i),
                          writes=[b["consts"]], is_dma=True, tag="ld_" + dname)

        for nm, sb in [("identf", self.s_identf), ("identbf", self.s_identbf),
                       ("ones1f", self.s_ones1f), ("ones1bf", self.s_ones1bf),
                       ("ones128f", self.s_ones128f), ("ones128bf", self.s_ones128bf),
                       ("maskstq", self.s_mtq), ("masksqk2a", self.s_mqk2[0]),
                       ("masksqk2b", self.s_mqk2[1]),
                       ("ghostb", self.s_ghostb), ("wqkvb_pc", self.s_bqkv),
                       ("wob_pc", self.s_bo), ("wgate3", self.s_wg),
                       ("premask", self.s_premask)]:
            ld(sb, nm)
        sch.op("sp", lambda e: e.dma_start(out=self.s_rowtmp[:],
                                           in_=self.d["bv_row"].ap()),
               writes=[b["rowtmp"]], is_dma=True, tag="ld_bv")
        sch.op("dve", lambda e: e.tensor_copy(self.s_bvrowbf[:], self.s_rowtmp[:]),
               reads=[b["rowtmp"]], writes=[b["consts"]], tag="cast_bv")
        sch.op("sp", lambda e: e.dma_start(out=self.s_rowtmp[:],
                                           in_=self.d["bexp_row"].ap()),
               reads=[b["consts"]], writes=[b["rowtmp"]], is_dma=True, tag="ld_be")
        sch.op("dve", lambda e: e.tensor_copy(self.s_berowbf[:], self.s_rowtmp[:]),
               reads=[b["rowtmp"]], writes=[b["consts"]], tag="cast_be")
        sch.op("dve", lambda e: e.memset(self.s_ones33[0:1, :], 1.0),
               writes=[b["consts"]], tag="ones33a")
        sch.op("dve", lambda e: e.memset(self.s_ones33[32:33, :], 1.0),
               writes=[b["consts"]], tag="ones33b")
        sch.op("dve", lambda e: e.memset(self.s_ones33bf[0:1, :], 1.0),
               writes=[b["consts"]], tag="ones33c")
        sch.op("dve", lambda e: e.memset(self.s_ones33bf[32:33, :], 1.0),
               writes=[b["consts"]], tag="ones33d")
        sch.op("dve", lambda e: e.tensor_copy(self.s_wgbf[:], self.s_wg[:]),
               reads=[b["consts"]], writes=[b["consts"]], tag="cast_wg")
        for nm, sb in [("kbd0", self.s_kbd[0]), ("kbd1", self.s_kbd[1]),
                       ("qbd20", self.s_qbd2[0]), ("qbd21", self.s_qbd2[1])]:
            sch.op("dve", lambda e, sb=sb: e.memset(sb[:], 0.0),
                   writes=[b[nm]], tag=f"z_{nm}")

        for bi in range(self.nb):
            self.emit_batch(bi)
        self.emit_routing()
        self.emit_phase_b()

        last_sp = [o for o in sch.order if o.is_dma and o.stream == "sp"][-1]
        last_act = [o for o in sch.order if o.is_dma and o.stream == "act"][-1]
        sch.op("sp", None, deps=[last_sp, last_act], tag="final_wait")
        sch.op("dve", lambda e: e.engine_nop(), deps=[last_act, last_sp],
               tag="final_wait2")

    def _x0_chunk_load(self, bi, qb):
        sch, b = self.sch, self.b
        src = self.d["x0t"].ap()[bi].rearrange(
            "(c p) t -> p c t", p=128)[:, :, qb * 512:(qb + 1) * 512]
        sch.op("gp", lambda e, i=src, qb=qb: e.dma_start(
            out=self.s_x0t[:, qb, :, :], in_=i),
            writes=[b[f"x0t{qb}"]], is_dma=True, tag=f"ld_x0t_b{bi}c{qb}")

    def _wq_load(self, bi, pair):
        sch, b = self.sch, self.b
        wq = self.s_wqp[pair % 2]
        bwq = b[f"wqp{pair % 2}"]
        for blk in range(3):
            off = blk * 1024 + pair * 128
            srcw = bass.AP(self.d["wqkvt"], off,
                           [[3072, 128], [128 * 3072, 8], [1, 128]])
            dst = wq[:, :, blk, :]
            sch.op("gp", lambda e, o=dst, i=srcw: e.dma_start(out=o, in_=i),
                   writes=[bwq], is_dma=True, tag=f"ld_wq_b{bi}p{pair}k{blk}")

    def emit_batch(self, bi):
        sch, b = self.sch, self.b
        if bi == 0:
            for qb in range(NQB):
                self._x0_chunk_load(bi, qb)
            wsrc = self.d["wot"].ap().rearrange("(c p) f -> p c f", p=128)
            sch.op("gp", lambda e, i=wsrc: e.dma_start(out=self.s_wot[:], in_=i),
                   writes=[b["wot"]], is_dma=True, tag=f"ld_wot_b{bi}")

        for pair in range(self.npair):
            self.emit_pair(bi, pair, skip_wload=(bi > 0 and pair < 2))
        if bi + 1 < self.nb:
            self._wq_load(bi + 1, 0)
            self._wq_load(bi + 1, 1)
        self.emit_wo_ln1(bi)
        self.emit_gate_batch(bi)

    def emit_pair(self, bi, pair, skip_wload=False):
        sch, b = self.sch, self.b
        wq = self.s_wqp[pair % 2]
        bwq = b[f"wqp{pair % 2}"]
        if not skip_wload:
            self._wq_load(bi, pair)

        for blk in range(2):
            for qb in range(NQB):
                ps = self.p_sl[qb][:, :]
                def mmgrp(e, ps=ps, wq=wq, blk=blk, qb=qb):
                    r = None
                    for c in range(NFC):
                        r = e.matmul(ps, wq[:, c, blk, :],
                                     self.s_x0t[:, qb, c, :],
                                     start=(c == 0), stop=(c == NFC - 1))
                    return r
                sch.op("pe", mmgrp, reads=[bwq, b[f"x0t{qb}"]], writes=[b[f"sl{qb}"]],
                       tag=f"qkproj_b{bi}p{pair}x{blk}q{qb}")
                if blk == 0:
                    for hl in range(2):
                        r0 = 64 * hl
                        dst = self.s_qd[hl]
                        bias = self.s_bqkv[r0:r0 + 64, pair:pair + 1]
                        psh = self.p_sl[qb].ap()[r0:r0 + 64, :]
                        sch.op("act", lambda e, psh=psh, dst=dst, qb=qb, bias=bias,
                               r0=r0: e.activation(
                                   out=dst[r0:r0 + 64, qb * 512:(qb + 1) * 512],
                                   in_=psh, func=AF.Identity, bias=bias, scale=1.0),
                               reads=[b[f"sl{qb}"], b["consts"]],
                               writes=[b[f"qd{hl}"]],
                               tag=f"qevac_b{bi}p{pair}h{hl}q{qb}")
                else:
                    bias = self.s_bqkv[:, 8 + pair:9 + pair]
                    sch.op("act", lambda e, ps=ps, qb=qb, bias=bias:
                           e.activation(out=self.s_kt[:, qb * 512:(qb + 1) * 512],
                                        in_=ps, func=AF.Identity, bias=bias,
                                        scale=1.0),
                           reads=[b[f"sl{qb}"], b["consts"]], writes=[b["kt"]],
                           tag=f"kevac_b{bi}p{pair}q{qb}")
            def half_view(sb, p0, h):
                return sb.ap()[p0:p0 + 64, :].rearrange(
                    "p (k h c) -> p k h c", h=2, c=64)[:, :, h:h + 1, :]
            if blk == 0:
                def qdma(dst, src, srcbuf, dstbuf, tag):
                    sch.op("sp", lambda e, o=dst, i=src: e.dma_start(out=o, in_=i),
                           reads=[b[srcbuf]], writes=[b[dstbuf]],
                           is_dma=True, tag=tag)
                t = f"_b{bi}p{pair}"
                qdma(self.s_qd[0].ap()[64:128, :], self.s_qd[0].ap()[0:64, :],
                     "qd0", "qd0", "bq_qd0" + t)
                qdma(self.s_qd[1].ap()[0:64, :], self.s_qd[1].ap()[64:128, :],
                     "qd1", "qd1", "bq_qd1" + t)
                qdma(half_view(self.s_qbd2[0], 0, 0), half_view(self.s_qd[0], 0, 0),
                     "qd0", "qbd20", "bq_q20a" + t)
                qdma(half_view(self.s_qbd2[0], 64, 1), half_view(self.s_qd[1], 64, 0),
                     "qd1", "qbd20", "bq_q20b" + t)
                qdma(half_view(self.s_qbd2[1], 0, 0), half_view(self.s_qd[0], 0, 1),
                     "qd0", "qbd21", "bq_q21a" + t)
                qdma(half_view(self.s_qbd2[1], 64, 1), half_view(self.s_qd[1], 64, 1),
                     "qd1", "qbd21", "bq_q21b" + t)
                sch.op("sp", lambda e: e.nop(),
                       reads=[b["qd0"], b["qd1"]],
                       writes=[b["qd0"], b["qd1"], b["qbd20"], b["qbd21"]],
                       flushq="q_sp", tag=f"qflush_b{bi}p{pair}")
            else:
                def kdma(dst, src, dstbuf, tag):
                    sch.op("gp", lambda e, o=dst, i=src: e.dma_start(out=o, in_=i),
                           reads=[b["kt"]], writes=[b[dstbuf]],
                           is_dma=True, tag=tag)
                t = f"_b{bi}p{pair}"
                kdma(half_view(self.s_kbd[0], 0, 0), half_view(self.s_kt, 0, 0),
                     "kbd0", "bk_k0a" + t)
                kdma(half_view(self.s_kbd[0], 64, 1), half_view(self.s_kt, 0, 1),
                     "kbd0", "bk_k0b" + t)
                kdma(half_view(self.s_kbd[1], 0, 0), half_view(self.s_kt, 64, 0),
                     "kbd1", "bk_k1a" + t)
                kdma(half_view(self.s_kbd[1], 64, 1), half_view(self.s_kt, 64, 1),
                     "kbd1", "bk_k1b" + t)
                sch.op("gp", lambda e: e.engine_nop(),
                       reads=[b["kt"]],
                       writes=[b["kbd0"], b["kbd1"]],
                       flushq="q_gp", tag=f"kflush_b{bi}p{pair}")

        vones_ap = bass.AP(self.s_vp, 64, [[NKT * 130, 128], [130, NKT], [65, 2]])
        sch.op("dve", lambda e, a=vones_ap: e.memset(a, 1.0),
               writes=[b["vp"]], tag=f"vones_b{bi}p{pair}")
        for qb in range(NQB):
            ps = self.p_sl[qb][:, :]
            def vmm(e, ps=ps, wq=wq, qb=qb):
                r = None
                for c in range(NFC):
                    r = e.matmul(ps, wq[:, c, 2, :],
                                 self.s_x0t[:, qb, c, :],
                                 start=(c == 0), stop=(c == NFC - 1))
                return r
            sch.op("pe", vmm, reads=[bwq, b[f"x0t{qb}"]], writes=[b[f"sl{qb}"]],
                   tag=f"vproj_b{bi}p{pair}q{qb}")
            bias = self.s_bqkv[:, 16 + pair:17 + pair]
            sch.op("act", lambda e, ps=ps, qb=qb, bias=bias:
                   e.activation(out=self.s_vt[:, qb * 512:(qb + 1) * 512], in_=ps,
                                func=AF.Identity, bias=bias, scale=1.0),
                   reads=[b[f"sl{qb}"], b["consts"]], writes=[b["vt"]],
                   tag=f"vevac_b{bi}p{pair}q{qb}")
        for tt in range(NKT):
            psb = self.p_sl[tt % 4].ap()[:, 0:64].bitcast(BF16)
            sch.op("pe", lambda e, psb=psb, tt=tt: e.transpose(
                psb, self.s_vt[:, tt * 128:(tt + 1) * 128], self.s_identbf[:]),
                reads=[b["vt"], b["consts"]], writes=[b[f"sl{tt % 4}"]],
                tag=f"vtr_b{bi}p{pair}t{tt}")
            dst = bass.AP(self.s_vp, tt * 130, [[NKT * 130, 128], [65, 2], [1, 64]])
            sch.op("dve", lambda e, psb=psb, dst=dst:
                   e.tensor_copy(dst, psb.rearrange("p (h d) -> p h d", h=2)),
                   reads=[b[f"sl{tt % 4}"]], writes=[b["vp"]],
                   tag=f"vevacT_b{bi}p{pair}t{tt}")

        for qb in range(NQB):
            nkt = 4 * qb + 4
            cs_items = self._cs_items(bi, pair, qb)
            ncs, done = len(cs_items), 0
            for kt in range(nkt + 1):
                if kt == 2 and qb > 0:
                    self._qb_denom(bi, pair, qb - 1)
                for hl in range(2):
                    if kt < nkt:
                        sb_i = hl * 2 + kt % 2
                        ps, pb = self.p_sl[sb_i], b[f"sl{sb_i}"]
                        diag = kt - 4 * qb
                        def stgrp(e, ps=ps, hl=hl, kt=kt, qb=qb, diag=diag):
                            r = e.matmul(ps[:, :],
                                         self.s_kbd[hl][:, kt * 128:(kt + 1) * 128],
                                         self.s_qd[hl][:, qb * 512:(qb + 1) * 512],
                                         start=True, stop=(diag < 0))
                            if diag >= 0:
                                off = diag * (diag + 1) // 2 * 128
                                w = (diag + 1) * 128
                                r = e.matmul(ps[:, 0:w], self.s_identbf[:],
                                             self.s_mtq[:, off:off + w], start=False,
                                             stop=True)
                            return r
                        sch.op("pe", stgrp,
                               reads=[b[f"qd{hl}"], b[f"kbd{hl}"], b["consts"]],
                               writes=[pb], tag=f"st_b{bi}p{pair}q{qb}k{kt}h{hl}")
                    if kt > 0:
                        ktp = kt - 1
                        et = self.s_et[hl][ktp % 2]
                        eb = b[f"et{hl}{ktp % 2}"]
                        pa, pab = self.p_av[hl], b[f"av{hl}"]
                        vsl = self.s_vp[:, ktp, hl * 65:hl * 65 + 65]
                        sch.op("pe", lambda e, pa=pa, vsl=vsl, et=et, ktp=ktp, nkt=nkt:
                               e.matmul(pa[0:65, :], vsl, et[:],
                                        start=(ktp == 0), stop=(ktp == nkt - 1)),
                               reads=[b["vp"], eb], writes=[pab],
                               tag=f"av_b{bi}p{pair}q{qb}k{ktp}h{hl}")
                    if kt < nkt:
                        sb_i = hl * 2 + kt % 2
                        ps, pb = self.p_sl[sb_i], b[f"sl{sb_i}"]
                        et = self.s_et[hl][kt % 2]
                        eb = b[f"et{hl}{kt % 2}"]
                        sch.op("act", lambda e, ps=ps, et=et:
                               e.activation(out=et[:], in_=ps[:, :], func=AF.Exp,
                                            scale=0.125),
                               reads=[pb], writes=[eb],
                               tag=f"exp_b{bi}p{pair}q{qb}k{kt}h{hl}")
                tgt = (kt + 1) * ncs // (nkt + 1)
                while done < tgt:
                    cs_items[done]()
                    done += 1

            self._av_evac(bi, pair, qb)
        self._qb_denom(bi, pair, NQB - 1)
        self._pair_denoms(bi, pair)
        self.dbg(f"a2n_b{bi}p{pair}", self.s_a2n[pair][:], [128, S], BF16,
                 [self.b_a2n[pair]])
        self.dbg(f"sig_b{bi}p{pair}", self.s_sig[:], [33, S], BF16, [b["sig"]])
        self.dbg(f"mpair_b{bi}p{pair}", self.s_mpair[:], [128, 32], F32,
                 [b["mpair"]])
        self.dbg(f"qt_b{bi}p{pair}", self.s_qd[0][:], [128, S], BF16, [b["qd0"]])
        self.dbg(f"vp_b{bi}p{pair}", self.s_vp[:], [128, NKT, 130], BF16,
                 [b["vp"]])

    def _cs_items(self, bi, pair, qb):
        sch, b = self.sch, self.b
        items = []
        idx = [0]
        for i in range(4):
            qt = 4 * qb + i
            for qh in range(2):
                jj = qt % 4
                ext = (qt + 1) * 128
                nr = (ext + 511) // 512
                for rd in range(nr):
                    lo = rd * 512
                    w = min(ext - lo, 512)
                    dlast = (rd == nr - 1)
                    def item(qt=qt, qh=qh, rd=rd, lo=lo, w=w, dlast=dlast, jj=jj):
                        cb = idx[0] % 2
                        po = cb * 512
                        idx[0] += 1
                        def cpg(e, lo=lo, w=w, qh=qh, qt=qt, po=po,
                                dlast=dlast, jj=jj):
                            if not dlast:
                                return e.matmul(
                                    self.p_cp[:, po:po + w],
                                    self.s_qbd2[qh][:, qt * 128:(qt + 1) * 128],
                                    self.s_kt[:, lo:lo + w],
                                    start=True, stop=True)
                            off = jj * (jj + 1) // 2 * 128
                            e.matmul(self.p_cp[:, po:po + w],
                                     self.s_qbd2[qh][:, qt * 128:(qt + 1) * 128],
                                     self.s_kt[:, lo:lo + w],
                                     start=True, stop=False)
                            return e.matmul(self.p_cp[:, po:po + w],
                                            self.s_identbf[:],
                                            self.s_mqk2[qh][:, off:off + w],
                                            start=False, stop=True)
                        cbank = b[f"cpb{cb}"]
                        sch.op("pe", cpg,
                               reads=[b[f"qbd2{qh}"], b["kt"], b["consts"]],
                               writes=[cbank],
                               tag=f"cs_b{bi}p{pair}t{qt}h{qh}r{rd}")
                        mcol = self.s_mpair[:, qt * 2 + qh:qt * 2 + qh + 1]
                        if rd == 0:
                            sch.op("dve", lambda e, mcol=mcol, w=w, po=po:
                                   e.reduce_max(mcol, self.p_cp[:, po:po + w],
                                                axis=AX.X),
                                   reads=[cbank], writes=[b["mpair"]],
                                   tag=f"max_b{bi}p{pair}t{qt}h{qh}")
                        else:
                            nr1 = (qt // 4 + 1) - 1
                            pc = (0, 0, 8, 24)[qt // 4] \
                                + ((qt % 4) * 2 + qh) * nr1 + (rd - 1)
                            mp = self.s_mprt[:, pc:pc + 1]
                            sch.op("dve", lambda e, mp=mp, w=w, po=po:
                                   e.reduce_max(mp, self.p_cp[:, po:po + w],
                                                axis=AX.X),
                                   reads=[cbank], writes=[b["mprt"]],
                                   tag=f"maxp_b{bi}p{pair}t{qt}h{qh}r{rd}")
                    items.append(item)
        return items

    def _av_evac(self, bi, pair, qb):
        sch, b = self.sch, self.b
        for hl in range(2):
            pa, pab = self.p_av[hl], b[f"av{hl}"]
            srow = 32 * hl
            so = sch.op("act", lambda e, pa=pa, srow=srow, qb=qb:
                        e.copy(out=self.s_sig[srow:srow + 1, qb * 512:(qb + 1) * 512],
                               in_=pa[64:65, :]),
                        reads=[pab], writes=[b[f"sig{qb}"]],
                        tag=f"sigevac_b{bi}p{pair}q{qb}h{hl}")
            sch.op("dve", lambda e, pa=pa, hl=hl, qb=qb, pair=pair:
                   e.tensor_copy(self.s_a2n[pair][64 * hl:64 * hl + 64,
                                                  qb * 512:(qb + 1) * 512],
                                 pa[0:64, :]),
                   reads=[pab], writes=[self.b_a2n[pair]], deps=[so],
                   tag=f"aevac_b{bi}p{pair}q{qb}h{hl}")

    def _qb_denom(self, bi, pair, qb):
        sch, b = self.sch, self.b
        nr1 = qb
        pb0 = (0, 0, 8, 24)[qb]
        for k in range(1, nr1):
            sch.op("dve", lambda e, k=k, nr1=nr1, pb0=pb0: e.tensor_tensor(
                out=bass.AP(self.s_mprt, pb0 + k, [[48, 128], [nr1, 8]]),
                in0=bass.AP(self.s_mprt, pb0 + k - 1, [[48, 128], [nr1, 8]]),
                in1=bass.AP(self.s_mprt, pb0 + k, [[48, 128], [nr1, 8]]),
                op=ALU.max),
                reads=[b["mprt"]], writes=[b["mprt"]],
                tag=f"mcmb_b{bi}p{pair}q{qb}k{k}")
        if nr1 > 0:
            sch.op("dve", lambda e, nr1=nr1, qb=qb, pb0=pb0: e.tensor_tensor(
                out=self.s_mpair[:, 8 * qb:8 * qb + 8],
                in0=bass.AP(self.s_mprt, pb0 + nr1 - 1, [[48, 128], [nr1, 8]]),
                in1=self.s_mpair[:, 8 * qb:8 * qb + 8],
                op=ALU.max),
                reads=[b["mprt"]], writes=[b["mpair"]],
                tag=f"mcmbf_b{bi}p{pair}q{qb}")
        sch.op("act", lambda e, qb=qb: e.activation(
            out=self.s_emg[:, 8 * qb:8 * qb + 8],
            in_=self.s_mpair[:, 8 * qb:8 * qb + 8], func=AF.Exp, scale=0.125),
            reads=[b["mpair"]], writes=[b["emg"]], tag=f"em_b{bi}p{pair}q{qb}")
        sch.op("dve", lambda e, qb=qb: e.tensor_scalar(
            out=self.s_emgbf[:, 8 * qb:8 * qb + 8],
            in0=self.s_emg[:, 8 * qb:8 * qb + 8],
            scalar1=self.s_ghostb[:, pair:pair + 1], scalar2=None, op0=ALU.mult),
            reads=[b["emg"], b["consts"]], writes=[b[f"emgbf{qb}"]],
            tag=f"emg_b{bi}p{pair}q{qb}")

    def _pair_denoms(self, bi, pair):
        sch, b = self.sch, self.b
        for qb in range(NQB):
            self._one_denom(bi, pair, qb)

    def _denom_bank(self, qb, hl):
        idx = qb * 2 + hl
        if idx < 2:
            return self.p_cp[:, idx * 512:(idx + 1) * 512], self.b[f"cpb{idx}"]
        if idx < 6:
            return self.p_sl[idx - 2][:, :], self.b[f"sl{idx - 2}"]
        return self.p_av[idx - 6][:, :], self.b[f"av{idx - 6}"]

    def _one_denom(self, bi, pair, qb):
        sch, b = self.sch, self.b
        for hl in range(2):
            srow = 32 * hl
            r0 = 64 * hl
            ps, pb = self._denom_bank(qb, hl)
            def dmm(e, ps=ps, hl=hl, qb=qb, srow=srow, r0=r0):
                r = e.matmul(ps[r0:r0 + 64, 0:512],
                             self.s_ones33bf[srow:srow + 1, 0:64],
                             self.s_sig[srow:srow + 1, qb * 512:(qb + 1) * 512],
                             start=True, stop=False, tile_position=(srow, r0))
                for i in range(4):
                    qt = 4 * qb + i
                    for qh in range(2):
                        col = self.s_emgbf[:, qt * 2 + qh:qt * 2 + qh + 1]
                        r = e.matmul(
                            ps[r0:r0 + 64,
                               i * 128 + qh * 64:i * 128 + qh * 64 + 64],
                            col.broadcast_to([128, 64]),
                            self.s_identbf[:, 64 * hl:64 * hl + 64],
                            start=False, stop=(i == 3 and qh == 1),
                            tile_position=(0, r0))
                return r
            sch.op("pe", dmm, reads=[b[f"emgbf{qb}"], b[f"sig{qb}"], b["consts"]],
                   writes=[pb], tag=f"denom_b{bi}p{pair}h{hl}q{qb}")
        for hl in range(2):
            r0 = 64 * hl
            ps, pb = self._denom_bank(qb, hl)
            sch.op("dve", lambda e, ps=ps, r0=r0: e.reciprocal(
                self.s_r[r0:r0 + 64, :], ps[r0:r0 + 64, :]),
                reads=[pb], writes=[b["r"]], tag=f"recip_b{bi}p{pair}h{hl}q{qb}")
        for hl in range(2):
            r0 = 64 * hl
            sch.op("dve", lambda e, hl=hl, qb=qb, pair=pair, r0=r0: e.tensor_tensor(
                out=self.s_a2n[pair][r0:r0 + 64, qb * 512:(qb + 1) * 512],
                in0=self.s_a2n[pair][r0:r0 + 64, qb * 512:(qb + 1) * 512],
                in1=self.s_r[r0:r0 + 64, :], op=ALU.mult),
                reads=[b["r"], self.b_a2n[pair]], writes=[self.b_a2n[pair]],
                tag=f"norm_b{bi}p{pair}h{hl}q{qb}")

    def emit_wo_ln1(self, bi):
        sch, b = self.sch, self.b
        banks = self.p_av + self.p_sl[0:2]
        bnames = ("av0", "av1", "sl0", "sl1")

        def womm(qb, fc):
            ps, pb = banks[fc % 4], b[bnames[fc % 4]]
            np_ = self.npair
            def mm(e, ps=ps, fc=fc, qb=qb, np_=np_):
                r = None
                for p in range(np_):
                    r = e.matmul(ps[:, 0:512],
                                 self.s_wot[:, p, fc * 128:(fc + 1) * 128],
                                 self.s_a2n[p][:, qb * 512:(qb + 1) * 512],
                                 start=(p == 0), stop=(p == NPAIR - 1))
                return r
            sch.op("pe", mm, reads=[b["wot"]] + list(self.b_a2n[:self.npair]),
                   writes=[pb], tag=f"wo_b{bi}q{qb}f{fc}")

        def xres(qb, fc):
            ps, pb = banks[fc % 4], b[bnames[fc % 4]]
            sch.op("dve", lambda e, ps=ps, fc=fc, qb=qb: e.scalar_tensor_tensor(
                out=self.s_xres[:, fc, :], in0=ps[:, 0:512],
                scalar=self.s_bo[:, fc:fc + 1],
                in1=self.s_x0t[:, qb, fc, :],
                op0=ALU.add, op1=ALU.add),
                reads=[pb, b[f"x0t{qb}"], b["consts"]], writes=[b["xres"]],
                tag=f"xres_b{bi}q{qb}f{fc}")
            sch.op("act", lambda e, fc=fc: e.activation(
                out=self.s_xsq[:, fc % 2, :], in_=self.s_xres[:, fc, :],
                func=AF.Square),
                reads=[b["xres"]], writes=[b["xsq"]], tag=f"sq_b{bi}q{qb}f{fc}")

        def stats(qb, fc):
            o1 = (qb % 2) * 512
            b1 = b[f"cpb{qb % 2}"]
            sch.op("pe", lambda e, fc=fc, o1=o1: e.matmul(
                self.p_cp[0:1, o1:o1 + 512], self.s_ones128f[:],
                self.s_xres[:, fc, :],
                start=(fc == 0), stop=(fc == NFC - 1)),
                reads=[b["xres"], b["consts"]], writes=[b1],
                tag=f"sum_b{bi}q{qb}f{fc}")
            sch.op("pe", lambda e, fc=fc, o1=o1: e.matmul(
                self.p_cp[64:65, o1:o1 + 512], self.s_ones128f[:],
                self.s_xsq[:, fc % 2, :],
                start=(fc == 0), stop=(fc == NFC - 1)),
                reads=[b["xsq"], b["consts"]], writes=[b1],
                tag=f"sumsq_b{bi}q{qb}f{fc}")

        def rowchain(qb):
            o1 = (qb % 2) * 512
            b1 = b[f"cpb{qb % 2}"]
            sch.op("act", lambda e, o1=o1: e.activation(
                out=self.s_lrows[:], in_=self.p_cp[0:1, o1:o1 + 512],
                func=AF.Identity, scale=1.0 / D),
                reads=[b1], writes=[b["lrows"]], tag=f"mean_b{bi}q{qb}")
            sch.op("act", lambda e, o1=o1: e.activation(
                out=self.s_ltmp[:], in_=self.p_cp[64:65, o1:o1 + 512],
                func=AF.Identity, scale=1.0 / D),
                reads=[b1], writes=[b["ltmp"]], tag=f"msq_b{bi}q{qb}")
            sch.op("act", lambda e: e.activation(
                out=self.s_laux[:], in_=self.s_lrows[:], func=AF.Square),
                reads=[b["lrows"]], writes=[b["laux"]], tag=f"m2_b{bi}q{qb}")

        def rowchain2(qb):
            sch.op("dve", lambda e: e.scalar_tensor_tensor(
                out=self.s_lrows[:], in0=self.s_ltmp[:], scalar=EPS,
                in1=self.s_laux[:], op0=ALU.add, op1=ALU.subtract),
                reads=[b["ltmp"], b["laux"]], writes=[b["lrows"]],
                tag=f"var_b{bi}q{qb}")
            sch.op("act", lambda e: e.activation(
                out=self.s_ltmp[:], in_=self.s_lrows[:], func=AF.Sqrt),
                reads=[b["lrows"]], writes=[b["ltmp"]], tag=f"sqrt_b{bi}q{qb}")
            sch.op("dve", lambda e: e.reciprocal(self.s_laux[:], self.s_ltmp[:]),
                   reads=[b["ltmp"]], writes=[b["laux"]], tag=f"rstd_b{bi}q{qb}")

        def bmean(qb):
            sch.op("pe", lambda e: e.matmul(
                self.p_sl[2][:, :], self.s_ones1f[:],
                self.s_lrows[:], start=True, stop=True),
                reads=[b["lrows"], b["consts"]], writes=[b["sl2"]],
                tag=f"bmean_b{bi}q{qb}")

        def brstd(qb):
            sch.op("pe", lambda e: e.matmul(
                self.p_sl[3][:, :], self.s_ones1f[:],
                self.s_laux[:], start=True, stop=True),
                reads=[b["laux"], b["consts"]], writes=[b["sl3"]],
                tag=f"brstd_b{bi}q{qb}")

        subscr = [self.s_r, self.s_oh]
        subbuf = [b["r"], b["oh"]]

        def sub(qb, fc):
            sch.op("dve", lambda e, fc=fc: e.scalar_tensor_tensor(
                out=subscr[fc % 2][:], in0=self.p_sl[2][:, :],
                scalar=-1.0, in1=self.s_xres[:, fc, :],
                op0=ALU.mult, op1=ALU.add),
                reads=[b["xres"], b["sl2"]], writes=[subbuf[fc % 2]],
                tag=f"sub_b{bi}q{qb}f{fc}")

        def x1(qb, fc):
            sch.op("dve", lambda e, fc=fc: e.tensor_tensor(
                out=self.s_x1qb[:, fc, :], in0=self.p_sl[3][:, :],
                in1=subscr[fc % 2][:], op=ALU.mult),
                reads=[subbuf[fc % 2], b["sl3"]], writes=[b["x1qb"]],
                tag=f"x1_b{bi}q{qb}f{fc}")
            dst = bass.AP(self.d_x1sc, fc * 128 * (NB * S) + bi * S + qb * 512,
                          [[NB * S, 128], [1, 512]])
            if "stx1" not in self.skip:
                sch.op("sp", lambda e, fc=fc, dst=dst:
                       e.dma_start(out=dst, in_=self.s_x1qb[:, fc, :]),
                       reads=[b["x1qb"]], writes=[b["x1sc"]], is_dma=True,
                       tag=f"st_x1_b{bi}q{qb}f{fc}")

        def gmm(qb):
            o1 = (qb % 2) * 512
            if "gmm" in self.skip:
                return
            def g(e, qb=qb, o1=o1):
                r = None
                for tt in range(4):
                    for fc in range(NFC):
                        r = e.matmul(self.p_cp[:, o1 + tt * 8:o1 + tt * 8 + 8],
                                     self.s_x1qb[:, fc, tt * 128:(tt + 1) * 128],
                                     self.s_wgbf[:, fc, :],
                                     start=(fc == 0), stop=(fc == NFC - 1))
                return r
            sch.op("pe", g, reads=[b["x1qb"], b["consts"]],
                   writes=[b[f"cpb{qb % 2}"]], tag=f"gate_b{bi}q{qb}")
            sch.op("dve", lambda e, qb=qb, o1=o1: e.tensor_copy(
                self.s_lall[:, qb * 32:qb * 32 + 32], self.p_cp[:, o1:o1 + 32]),
                reads=[b[f"cpb{qb % 2}"]], writes=[b["lall"]],
                tag=f"levac_b{bi}q{qb}")

        def tail_head(qp):
            rowchain(qp)

        def tail_mid(qp, fc):
            sub(qp, fc)
            if fc > 0:
                x1(qp, fc - 1)

        def tail_end(qp):
            x1(qp, NFC - 1)
            gmm(qp)

        for qb in range(NQB):
            if qb > 0:
                tail_head(qb - 1)
            womm(qb, 0)
            if qb > 0:
                bmean(qb - 1)
                rowchain2(qb - 1)
            womm(qb, 1)
            if qb > 0:
                brstd(qb - 1)
            for fc in range(NFC):
                if qb > 0:
                    tail_mid(qb - 1, fc)
                xres(qb, fc)
                if fc + 2 < NFC:
                    womm(qb, fc + 2)
                if fc > 0:
                    stats(qb, fc - 1)
            if qb > 0:
                tail_end(qb - 1)
            stats(qb, NFC - 1)
            if bi + 1 < self.nb:
                self._x0_chunk_load(bi + 1, qb)
        tail_head(NQB - 1)
        bmean(NQB - 1)
        rowchain2(NQB - 1)
        brstd(NQB - 1)
        for fc in range(NFC):
            tail_mid(NQB - 1, fc)
        tail_end(NQB - 1)
        sch.op("sp", lambda e: e.nop(), reads=[b["x1qb"]], writes=[b["x1qb"]],
               flushq="q_sp", tag=f"x1flush_b{bi}")
        self.dbg(f"x1qb_b{bi}", self.s_x1qb[:], [128, NFC, 512], BF16, [b["x1qb"]])
        self.dbg(f"xres_b{bi}", self.s_xres[:], [128, NFC, 512], F32, [b["xres"]])
        self.dbg(f"lall_b{bi}", self.s_lall[:], [128, 512], F32, [b["lall"]])

    def emit_gate_batch(self, bi):
        sch, b = self.sch, self.b
        l3 = self.s_lall.ap().rearrange("p (t e) -> p t e", e=8)
        sch.op("dve", lambda e, l3=l3: e.reduce_max(self.s_xsq[:, 0, 0:64], l3, axis=AX.X),
               reads=[b["lall"]], writes=[b["xsq"]], tag=f"gmax_b{bi}")
        mx3 = self.s_xsq.ap()[:, 0, 0:64].rearrange("p (t o) -> p t o", o=1)
        sch.op("dve", lambda e, l3=l3, mx3=mx3: e.tensor_tensor(
            out=self.s_oh.ap().rearrange("p (t e) -> p t e", e=8), in0=l3,
            in1=mx3.broadcast_to([128, 64, 8]), op=ALU.is_equal),
            reads=[b["lall"], b["xsq"]], writes=[b["oh"]], tag=f"oneh_b{bi}")
        for g in range(4):
            def trg(e, g=g, src=self.s_oh):
                r = None
                for i in range(4):
                    tt = 4 * g + i
                    r = e.transpose(self.p_cp[0:8, i * 128:(i + 1) * 128],
                                    src[:, tt * 8:tt * 8 + 8], self.s_identf[:])
                return r
            sch.op("pe", trg, reads=[b["oh"], b["consts"]], writes=[b["cpb0"]],
                   tag=f"ohT_b{bi}g{g}")
            sch.op("dve", lambda e, g=g: e.tensor_copy(
                self.s_gateT[0:8, bi * S + g * 512:bi * S + (g + 1) * 512],
                self.p_cp[0:8, 0:512]),
                reads=[b["cpb0"]], writes=[b["gateT"]], tag=f"ohTe_b{bi}g{g}")
            def trl(e, g=g):
                r = None
                for i in range(4):
                    tt = 4 * g + i
                    r = e.transpose(self.p_cp[0:8, 512 + i * 128:640 + i * 128],
                                    self.s_lall[:, tt * 8:tt * 8 + 8], self.s_identf[:])
                return r
            sch.op("pe", trl, reads=[b["lall"], b["consts"]], writes=[b["cpb1"]],
                   tag=f"lT_b{bi}g{g}")
            sch.op("act", lambda e, g=g: e.activation(
                out=self.s_gateT[32:40, bi * S + g * 512:bi * S + (g + 1) * 512],
                in_=self.p_cp[0:8, 512:1024], func=AF.Exp),
                reads=[b["cpb1"]], writes=[b["gateT"]], tag=f"exT_b{bi}g{g}")

    def emit_routing(self):
        sch, b = self.sch, self.b
        T = self.nb * S
        sch.op("dve", lambda e: e.tensor_tensor_scan(
            out=self.s_scan[:, 0:T], data0=self.s_gateT[0:8, 0:T],
            data1=self.s_gateT[0:8, 0:T], initial=0.0, op0=ALU.add, op1=ALU.bypass),
            reads=[b["gateT"]], writes=[b["scan"]], tag="scan")
        sch.op("dve", lambda e: e.drain(), tag="scan_drain")
        if self.use_cc:
            sch.op("gp", lambda e: e.dma_start(out=self.d_ccin.ap(),
                                               in_=self.s_scan[0:8, T - 1:T]),
                   reads=[b["scan"]], writes=[b["ccin"]], is_dma=True, tag="ccin")
            cco = sch.op("gp", lambda e: e.collective_compute(
                "AllGather", ALU.bypass, replica_groups=self.cc_rg,
                ins=[self.d_ccin.ap().rearrange("a b -> b a").opt()],
                outs=[self.d_ccout.ap().opt()]),
                reads=[b["ccin"]], writes=[b["ccout"]], is_dma=True, tag="allgather")
            cco.delta = 1
            sch.op("gp", lambda e: e.dma_start(out=self.s_agc[:], in_=self.d_ccout.ap()),
                   reads=[b["ccout"]], writes=[b["agc"]], is_dma=True, tag="ld_agc")
            sch.op("pe", lambda e: e.matmul(self.p_sl[0][0:8, 0:1], self.s_agc[:],
                                            self.s_premask[:], start=True, stop=True),
                   reads=[b["agc"], b["consts"]], writes=[b["sl0"]], tag="cbmm")
            sch.op("dve", lambda e: e.tensor_copy(self.s_cb[:], self.p_sl[0][0:8, 0:1]),
                   reads=[b["sl0"]], writes=[b["cb"]], tag="cbevac")
        else:
            sch.op("dve", lambda e: e.memset(self.s_cb[:], 0.0),
                   writes=[b["cb"]], tag="cbzero")
        sch.op("dve", lambda e: e.tensor_scalar(
            out=self.s_sel[:, 0:T], in0=self.s_scan[:, 0:T], scalar1=self.s_cb[:],
            scalar2=float(CAP + 1), op0=ALU.add, op1=ALU.is_lt),
            reads=[b["scan"], b["cb"]], writes=[b["sel"]], tag="sel")
        sch.op("dve", lambda e: e.tensor_tensor(
            out=self.s_so[32:40, 0:T], in0=self.s_gateT[0:8, 0:T],
            in1=self.s_sel[:, 0:T],
            op=ALU.mult), reads=[b["gateT"], b["sel"]], writes=[b["so"]], tag="so")
        sch.op("dve", lambda e: e.tensor_tensor(
            out=self.s_gateT[0:8, 0:T], in0=self.s_so[32:40, 0:T],
            in1=self.s_gateT[32:40, 0:T],
            op=ALU.mult), reads=[b["so"], b["gateT"]], writes=[b["gateT"]], tag="soe")
        nhalf = T // 1024
        for tb in range(T // 512):
            row, col = (0, tb * 512) if tb < nhalf else (32, (tb - nhalf) * 512)
            ps = self.p_sl[tb % 2]
            pb = b[f"sl{tb % 2}"]
            sch.op("pe", lambda e, ps=ps, tb=tb: e.matmul(
                ps[0:1, 0:512], self.s_ones128bf[0:8, :],
                self.s_gateT[0:8, tb * 512:(tb + 1) * 512], start=True, stop=True),
                reads=[b["gateT"], b["consts"]], writes=[pb], tag=f"nummm{tb}")
            sch.op("dve", lambda e, ps=ps, row=row, col=col: e.tensor_copy(
                self.s_num[row:row + 1, col:col + 512], ps[0:1, 0:512]),
                reads=[pb], writes=[b["num"]], tag=f"nume{tb}")
            sch.op("pe", lambda e, ps=ps, tb=tb: e.matmul(
                ps[0:1, 0:512], self.s_ones128bf[32:40, :],
                self.s_gateT[32:40, tb * 512:(tb + 1) * 512], start=True, stop=True),
                reads=[b["gateT"], b["consts"]], writes=[pb], tag=f"zmm{tb}")
            sch.op("dve", lambda e, ps=ps, row=row, col=col: e.tensor_copy(
                self.s_zrow[row:row + 1, col:col + 512], ps[0:1, 0:512]),
                reads=[pb], writes=[b["zrow"]], tag=f"ze{tb}")
        half = T // 2
        for hf, row in ((0, 0), (1, 32)):
            sch.op("dve", lambda e, row=row, half=half: e.reciprocal(
                self.s_mgt[row:row + 1, 0:half], self.s_zrow[row:row + 1, 0:half]),
                reads=[b["zrow"]], writes=[b["mgt"], b["scan"]], tag=f"zrec{row}")
            sch.op("dve", lambda e, row=row, hf=hf, half=half: e.tensor_tensor(
                out=self.s_mgbf[0:1, hf * half:(hf + 1) * half],
                in0=self.s_num[row:row + 1, 0:half],
                in1=self.s_mgt[row:row + 1, 0:half], op=ALU.mult),
                reads=[b["num"], b["mgt"]], writes=[b["mgbf"]], tag=f"mg{row}")
        self.dbg("mg", self.s_mgbf[:], [1, T], BF16, [b["mgbf"]])
        self.dbg("scan", self.s_scan[:], [8, T], F32, [b["scan"]])

    def fence(self):
        sch = self.sch
        last = {}
        for o in sch.order:
            last[o.stream] = o
        ops = []
        for st in ("pe", "dve", "act", "gp", "sp"):
            deps = [v for k, v in last.items() if k != st]
            fn = (lambda e: e.engine_nop()) if st in ("dve", "gp") else None
            ops.append(sch.op(st, fn, deps=deps, tag=f"fence_{st}",
                              flushq="q_sp" if st == "sp" else None))

    def emit_phase_b(self):
        sch, b = self.sch, self.b
        self.fence()
        wsrc = self.d["wexp"].ap().rearrange("(c p) f -> p c f", p=128)
        sch.op("gp", lambda e, i=wsrc: e.dma_start(out=self.s_wexp[:], in_=i),
               writes=[b["wexp"]], is_dma=True, tag="ld_wexp")
        T = self.nb * S
        banks = [self.p_av[0], self.p_av[1], self.p_sl[0]]
        bnames = ("av0", "av1", "sl0")

        def loads(tb):
            x1, bx1 = self.s_x1tb[tb % 2], b[f"x1tb{tb % 2}"]
            for fc in range(NFC):
                src = bass.AP(self.d_x1sc, fc * 128 * T + tb * 512,
                              [[T, 128], [1, 512]])
                sch.op("sp", lambda e, src=src, x1=x1, fc=fc:
                       e.dma_start(out=x1[:, fc, :], in_=src),
                       reads=[b["x1sc"]], writes=[bx1], is_dma=True,
                       tag=f"ld_x1_{tb}f{fc}")

        def mgb(tb):
            col = tb * 512
            sch.op("pe", lambda e, col=col: e.matmul(
                self.p_sl[3][:, :], self.s_ones1bf[:],
                self.s_mgbf[0:1, col:col + 512], start=True, stop=True),
                reads=[b["mgbf"], b["consts"]], writes=[b["sl3"]], tag=f"mgb{tb}")
            x1, bx1 = self.s_x1tb[tb % 2], b[f"x1tb{tb % 2}"]
            x13 = x1.ap().rearrange("p c t -> p c t")
            mg3 = self.p_sl[3].ap()[:, :].rearrange("p (o t) -> p o t", o=1)
            sch.op("dve", lambda e, x13=x13, mg3=mg3, tb=tb: e.tensor_tensor(
                out=self.s_x1mg.ap()[:, tb % 2], in0=mg3.broadcast_to([128, NFC, 512]),
                in1=x13, op=ALU.mult),
                reads=[bx1, b["sl3"]], writes=[b["x1mg"]], tag=f"x1mg{tb}")

        def emm(tb, fo):
            ps, pb = banks[fo % 3], b[bnames[fo % 3]]
            col = tb * 512
            def mm(e, ps=ps, fo=fo, col=col, tb=tb):
                for c in range(NFC):
                    e.matmul(ps[:, 0:512],
                             self.s_wexp[:, c, fo * 128:(fo + 1) * 128],
                             self.s_x1mg[:, tb % 2, c, :], start=(c == 0),
                             stop=False)
                return e.matmul(ps[:, 0:512],
                                self.s_berowbf[:, fo * 128:(fo + 1) * 128],
                                self.s_mgbf[0:1, col:col + 512],
                                start=False, stop=True)
            sch.op("pe", mm, reads=[b["wexp"], b["x1mg"], b["mgbf"], b["consts"]],
                   writes=[pb], tag=f"exp_mm{tb}f{fo}")

        def xres2(tb, fo):
            ps, pb = banks[fo % 3], b[bnames[fo % 3]]
            x1, bx1 = self.s_x1tb[tb % 2], b[f"x1tb{tb % 2}"]
            sch.op("dve", lambda e, ps=ps, fo=fo, x1=x1: e.tensor_tensor(
                out=self.s_xres2[:, fo, :], in0=ps[:, 0:512], in1=x1[:, fo, :],
                op=ALU.add),
                reads=[pb, bx1], writes=[b["xres2"]], tag=f"xres2_{tb}f{fo}")
            sch.op("act", lambda e, fo=fo: e.activation(
                out=self.s_xsqb[:, fo % 2, :], in_=self.s_xres2[:, fo, :],
                func=AF.Square),
                reads=[b["xres2"]], writes=[b["xsqb"]], tag=f"sq2_{tb}f{fo}")

        def stats2(tb, fo):
            o1 = (tb % 2) * 512
            b1 = b[f"cpb{tb % 2}"]
            sch.op("pe", lambda e, fo=fo, o1=o1: e.matmul(
                self.p_cp[0:1, o1:o1 + 512], self.s_ones128f[:],
                self.s_xres2[:, fo, :],
                start=(fo == 0), stop=(fo == NFC - 1)),
                reads=[b["xres2"], b["consts"]], writes=[b1], tag=f"sum2_{tb}f{fo}")
            sch.op("pe", lambda e, fo=fo, o1=o1: e.matmul(
                self.p_cp[64:65, o1:o1 + 512], self.s_ones128f[:],
                self.s_xsqb[:, fo % 2, :],
                start=(fo == 0), stop=(fo == NFC - 1)),
                reads=[b["xsqb"], b["consts"]], writes=[b1], tag=f"sumsq2_{tb}f{fo}")

        def rowchainA(tb):
            o1 = (tb % 2) * 512
            b1 = b[f"cpb{tb % 2}"]
            sch.op("act", lambda e, o1=o1: e.activation(
                out=self.s_lrows[:], in_=self.p_cp[0:1, o1:o1 + 512],
                func=AF.Identity, scale=1.0 / D),
                reads=[b1], writes=[b["lrows"]], tag=f"mean2_{tb}")
            sch.op("act", lambda e, o1=o1: e.activation(
                out=self.s_ltmp[:], in_=self.p_cp[64:65, o1:o1 + 512],
                func=AF.Identity, scale=1.0 / D),
                reads=[b1], writes=[b["ltmp"]], tag=f"msq2_{tb}")
            sch.op("act", lambda e: e.activation(
                out=self.s_laux[:], in_=self.s_lrows[:], func=AF.Square),
                reads=[b["lrows"]], writes=[b["laux"]], tag=f"m22_{tb}")

        def rowchain2(tb):
            sch.op("dve", lambda e: e.scalar_tensor_tensor(
                out=self.s_lrows[:], in0=self.s_ltmp[:], scalar=EPS,
                in1=self.s_laux[:], op0=ALU.add, op1=ALU.subtract),
                reads=[b["ltmp"], b["laux"]], writes=[b["lrows"]], tag=f"var2_{tb}")
            sch.op("act", lambda e: e.activation(
                out=self.s_ltmp[:], in_=self.s_lrows[:], func=AF.Sqrt),
                reads=[b["lrows"]], writes=[b["ltmp"]], tag=f"sqrt2_{tb}")
            sch.op("dve", lambda e: e.reciprocal(self.s_laux[:], self.s_ltmp[:]),
                   reads=[b["ltmp"]], writes=[b["laux"]], tag=f"rstd2_{tb}")

        def bmean2(tb):
            sch.op("pe", lambda e: e.matmul(
                self.p_sl[2][:, :], self.s_ones1f[:],
                self.s_lrows[:], start=True, stop=True),
                reads=[b["lrows"], b["consts"]], writes=[b["sl2"]],
                tag=f"bmean2_{tb}")

        def brstd2(tb):
            sch.op("pe", lambda e: e.matmul(
                self.p_sl[1][:, :], self.s_ones1f[:],
                self.s_laux[:], start=True, stop=True),
                reads=[b["laux"], b["consts"]], writes=[b["sl1"]],
                tag=f"brstd2_{tb}")

        def sub2(tb, fo):
            sch.op("dve", lambda e, fo=fo: e.scalar_tensor_tensor(
                out=self.s_r[:] if fo % 2 == 0 else self.s_oh[:],
                in0=self.p_sl[2][:, :],
                scalar=-1.0, in1=self.s_xres2[:, fo, :],
                op0=ALU.mult, op1=ALU.add),
                reads=[b["xres2"], b["sl2"]],
                writes=[b["r"] if fo % 2 == 0 else b["oh"]],
                tag=f"sub2_{tb}f{fo}")

        def x2(tb, fo):
            bi, qb = tb // 4, tb % 4
            sch.op("dve", lambda e, fo=fo: e.tensor_tensor(
                out=self.s_x2o[:, fo, :], in0=self.p_sl[1][:, :],
                in1=self.s_r[:] if fo % 2 == 0 else self.s_oh[:], op=ALU.mult),
                reads=[b["r"] if fo % 2 == 0 else b["oh"], b["sl1"]],
                writes=[b["x2o"]], tag=f"x2_{tb}f{fo}")
            dst = bass.AP(self.d_out, bi * D * S + fo * 128 * S + qb * 512,
                          [[S, 128], [1, 512]])
            sch.op("act", lambda e, dst=dst, fo=fo:
                   e.dma_start(out=dst, in_=self.s_x2o[:, fo, :]),
                   reads=[b["x2o"]], is_dma=True, tag=f"st_x2_{tb}f{fo}")

        NTB = T // 512
        for tb in range(NTB):
            loads(tb)
            mgb(tb)
            if tb > 0:
                rowchainA(tb - 1)
            emm(tb, 0)
            if tb > 0:
                bmean2(tb - 1)
                rowchain2(tb - 1)
            emm(tb, 1)
            if tb > 0:
                brstd2(tb - 1)
                sch.op("act", lambda e: e.nop(), reads=[b["x2o"]],
                       writes=[b["x2o"]], flushq="q_act", tag=f"x2flush_{tb}")
            for fo in range(NFC):
                if tb > 0:
                    sub2(tb - 1, fo)
                    if fo > 0:
                        x2(tb - 1, fo - 1)
                xres2(tb, fo)
                if fo + 2 < NFC:
                    emm(tb, fo + 2)
                if fo > 0:
                    stats2(tb, fo - 1)
            if tb > 0:
                x2(tb - 1, NFC - 1)
            stats2(tb, NFC - 1)
        tb = NTB - 1
        rowchainA(tb)
        bmean2(tb)
        rowchain2(tb)
        brstd2(tb)
        sch.op("act", lambda e: e.nop(), reads=[b["x2o"]],
               writes=[b["x2o"]], flushq="q_act", tag="x2flush_last")
        for fo in range(NFC):
            sub2(tb, fo)
            if fo > 0:
                x2(tb, fo - 1)
        x2(tb, NFC - 1)

    def realize(self):
        sch = self.sch
        counts = sch.finalize()
        nc = self.nc
        streams = {}
        for o in sch.order:
            streams.setdefault(o.stream, []).append(o)

        with contextlib.ExitStack() as es:
            sems = {}
            for name in ["s_pe", "s_dve", "s_act", "s_gp", "s_sp", "q_sp", "q_gp",
                         "q_act", "s_cc"]:
                sems[name] = es.enter_context(nc.semaphore(name))
            block = es.enter_context(nc.Block())

            def replay(eng, ops, pad_raw=False, chain_dma=None):
                watermark = {}
                pos = {}
                slot = 0
                for o in ops:
                    if o.flushq is not None and o.flushval:
                        if watermark.get(o.flushq, 0) < o.flushval:
                            eng.wait_ge(sems[o.flushq], o.flushval)
                            watermark[o.flushq] = o.flushval
                    if chain_dma and o.is_dma and o.sem == chain_dma \
                            and o.incval is not None and o.incval > 16:
                        v = o.incval - 16
                        if watermark.get(o.sem, 0) < v:
                            eng.wait_ge(sems[o.sem], v)
                            watermark[o.sem] = v
                    waited_pe = False
                    for d in o.deps:
                        if d.stream == o.stream and not d.is_dma and not o.is_dma:
                            continue
                        if d.incval is None:
                            continue
                        if watermark.get(d.sem, 0) >= d.incval:
                            continue
                        eng.wait_ge(sems[d.sem], d.incval)
                        watermark[d.sem] = d.incval
                        if d.sem == "s_pe":
                            waited_pe = True
                    if pad_raw and waited_pe and o.fn is not None:
                        for _ in range(6):
                            eng.engine_nop()
                        slot += 6
                    if pad_raw and o.fn is not None:
                        need = 0
                        for d in o.deps:
                            if d.stream == o.stream and not d.is_dma and id(d) in pos:
                                need = max(need, pos[id(d)] + 3 - slot)
                        for _ in range(need):
                            eng.engine_nop()
                            slot += 1
                    if o.fn is None:
                        continue
                    inst = o.fn(eng)
                    pos[id(o)] = slot
                    slot += 1
                    if o.incval is not None:
                        assert inst is not None, o.tag
                        inst.then_inc(sems[o.sem], o.delta)

            @block.tensor
            def _(eng):
                replay(eng, streams.get("pe", []))

            @block.vector
            def _(eng):
                replay(eng, streams.get("dve", []), pad_raw=True)

            @block.scalar
            def _(eng):
                replay(eng, streams.get("act", []))

            @block.gpsimd
            def _(eng):
                replay(eng, streams.get("gp", []), chain_dma="q_gp")

            @block.sync
            def _(eng):
                replay(eng, streams.get("sp", []))
        return nc


def build(nb=NB, npair=NPAIR, use_cc=True, debug=(), skip=()):
    bld = Builder(nb=nb, npair=npair, use_cc=use_cc, debug=debug, skip=skip)
    bld.emit_all()
    nc = bld.realize()
    return nc, bld


def kernel(**inputs):
    in_maps = stage(inputs)
    nc, _ = build()
    res = run_bass_kernel_spmd(nc, in_maps, core_ids=list(range(NCORES)))
    out = np.empty((32, 2048, 1024), dtype=np.float32)
    for c in range(NCORES):
        out[c * NB:(c + 1) * NB] = res.results[c]["x2t"].transpose(0, 2, 1)
    return out
